# version 02
# baseline (speedup 1.0000x reference)
import math

import numpy as np
import ml_dtypes

import concourse.bass as bass
import concourse.mybir as mybir
from concourse.bass_utils import run_bass_kernel_spmd

F32 = mybir.dt.float32
BF16 = mybir.dt.bfloat16
AF = mybir.ActivationFunctionType
ALU = mybir.AluOpType
AX = mybir.AxisListType
BF = ml_dtypes.bfloat16

B, S, E, T = 4, 2048, 2048, 1024
KC = E // 128
H, D, NB = 16, 128, 4
C = 512
CCH = C // 128
FH = 8192
FHC = FH // 128
LN_EPS = 1.0e-5
RMS_EPS = float(np.finfo(np.float32).eps)
A_EPS = 1.0e-6
SM_SCALE = 1.0 / 12.0
FWHT_SCALE = 1.0 / math.sqrt(FH)
NCORES = 8
UPTO = 4
DBG = None

ENGS = ("tensor", "vector", "scalar", "sync", "gpsimd")

AR_XC = 0
AR_SQ1 = 61440
AR_SGQ = 16384
AR_MEM = 32768
AR_SEL = 40960
AR_SGS = 49152
AR_CFB = 57344
AR_FOLD = 16384
AR_HID = 0
SQ3 = 16384
AB_KR = 0
AB_CKV = 32768
AB_QR = 40960
AB_AQ = 57344
AB_SCR = 57344
AB_CQR = 61440
AB_ATT = 61440
AB_UB = 40960
AB_CQN = 0
XN_T1B = 0
XN_WUV = 0
XN_AWO = 8192
XN_ATT = 0
XN_OF = 2048


class Buf:
    __slots__ = ("name", "writer", "readers")

    def __init__(self, name):
        self.name = name
        self.writer = None
        self.readers = []


def _had(n):
    h = np.array([[1.0]], dtype=np.float64)
    while h.shape[0] < n:
        h = np.block([[h, h], [h, -h]])
    return h


def build_program(upto=UPTO):
    nc = bass.Bass("TRN2", target_bir_lowering=False)
    DT = nc.dram_tensor

    x_in = DT("x", [KC, 128, T], F32, kind="ExternalInput").ap()
    wq_t = DT("wqT", [KC, 128, KC * 128], F32, kind="ExternalInput").ap()
    wo_t = DT("woT", [KC, 128, KC * 128], F32, kind="ExternalInput").ap()
    mem_t = DT("mem", [128, NB * H * 128], F32, kind="ExternalInput").ap()
    mnblk = DT("mnblk", [128, KC * 64], F32, kind="ExternalInput").ap()
    mnorm_n = DT("mnorm_n", [128, H * D], F32, kind="ExternalInput").ap()
    bw_in = DT("bw", [128, NB], F32, kind="ExternalInput").ap()
    w1_t = DT("w1T", [FHC, 128, KC * 128], F32, kind="ExternalInput").ap()
    w3_t = DT("w3T", [FHC, 128, KC * 128], F32, kind="ExternalInput").ap()
    w2p1 = DT("w2p1", [64, 128, E], F32, kind="ExternalInput").ap()
    wdq_t = DT("wdqT", [CCH, 128, KC * 128], F32, kind="ExternalInput").ap()
    wdkv_t = DT("wdkvT", [CCH, 128, KC * 128], F32, kind="ExternalInput").ap()
    wuq_t = DT("wuqT", [KC, 128, CCH * 128], F32, kind="ExternalInput").ap()
    wuk_n = DT("wuk", [CCH, 128, KC * 128], F32, kind="ExternalInput").ap()
    wkr_t = DT("wkrT", [KC, 128, KC * 128], F32, kind="ExternalInput").ap()
    wqr_t = DT("wqrT", [KC, 128, CCH * 128], F32, kind="ExternalInput").ap()
    wuv_n = DT("wuv", [KC, 128, C], F32, kind="ExternalInput").ap()
    wot_b = DT("woTb", [KC, 128, KC * 128], F32, kind="ExternalInput").ap()
    g1_in = DT("g1", [128, CCH], F32, kind="ExternalInput").ap()
    g2_in = DT("g2", [128, CCH], F32, kind="ExternalInput").ap()
    g3_in = DT("g3", [128, KC], F32, kind="ExternalInput").ap()
    consts_in = DT("consts", [128, 512], BF16, kind="ExternalInput").ap()
    selh_in = DT("selh", [128, 64], BF16, kind="ExternalInput").ap()
    seln_in = DT("seln", [128, 64], BF16, kind="ExternalInput").ap()
    mask64_in = DT("mask64", [128, 4], F32, kind="ExternalInput").ap()
    sel_in = DT("sel", [128, 64 * 128], BF16, kind="ExternalInput").ap()
    rope_in = DT("rope", [128, 2 * T], BF16, kind="ExternalInput").ap()

    y_out = DT("y", [KC, 128, T], F32, kind="ExternalOutput").ap()

    x_hbm = [DT(f"xs{i}", [KC, 128, T], F32, kind="Internal").ap() for i in range(2)]
    w2mid = DT("w2mid", [8192, E], BF16, kind="Internal").ap()
    w2fin = DT("w2fin", [KC, 128, FHC * 128], BF16, kind="Internal").ap()
    AGN = CCH * 128 * T + KC * 128 * T
    ag_in = DT("ag_in", [1, AGN], BF16, kind="Internal").ap()
    ag_out = DT("ag_out", [2, AGN], BF16, kind="Internal").ap()
    fence_in = DT("fence_in", [1, 512], BF16, kind="Internal").ap()
    fence_out = DT("fence_out", [1, 512], BF16, kind="Internal").ap()

    es = []

    def sbt(name, shape, dtype):
        g = nc.sbuf_tensor(name, shape, dtype)
        t = g.__enter__()
        es.append(g)
        return t

    sem_h = {}
    for e in ENGS:
        g = nc.semaphore(f"sem_{e}")
        sem_h[e] = g.__enter__()
        es.append(g)

    xn = sbt("xn", [128, KC * T], BF16)
    arena = sbt("arena", [128, 65536], BF16)
    wr = sbt("wr", [128, 4 * 2048], BF16)
    wr2 = sbt("wr2", [128, 2 * 512], BF16)
    fA = sbt("fA", [128, T], F32)
    fB = sbt("fB", [128, T], F32)
    xr = sbt("xr", [128, 2 * T], F32)
    sc1 = sbt("sc1", [128, T], F32)
    consts = sbt("consts_sb", [128, 512], BF16)
    selh = sbt("selh_sb", [128, 64], BF16)
    seln = sbt("seln_sb", [128, 64], BF16)
    mask64 = sbt("mask64_sb", [128, 4], F32)
    g_sb = sbt("g_sb", [128, 2 * CCH + KC], F32)
    rope_sb = sbt("rope_sb", [128, 2 * T], BF16)
    mnblk_sb = sbt("mnblk_sb", [128, KC * 64], BF16)
    bw_sb = sbt("bw_sb", [128, 16], F32)
    wact = sbt("wact_sb", [64, 4], F32)
    mrow = sbt("mrow", [128, 8], F32)

    IDENT = consts[:, 0:128]
    ONES = consts[:, 128:256]
    H128 = consts[:, 256:384]
    I2H64 = consts[:, 384:512]
    G1c = lambda cc: g_sb[:, cc:cc + 1]
    G2c = lambda cc: g_sb[:, CCH + cc:CCH + cc + 1]
    G3c = lambda k: g_sb[:, 2 * CCH + k:2 * CCH + k + 1]
    COSA = rope_sb[:, 0:T]
    SINB = rope_sb[:, T:2 * T]

    PP = []
    for i in range(4):
        g = nc.psum_tensor(f"pp{i}", [128, 1024], F32)
        PP.append(g.__enter__())
        es.append(g)

    def PPl(i):
        return PP[i][:, 0:512]

    def PPh(i):
        return PP[i][:, 512:1024]

    blkg = nc.Block()
    blk = blkg.__enter__()

    cnt = {e: 0 for e in ENGS}
    lastwait = {e: {f: 0 for f in ENGS} for e in ENGS}
    trusted = {"tensor", "vector", "scalar"}

    def op(eng, emit, reads=(), writes=(), dma=False):
        waits = {}

        def need(tok):
            if tok is None:
                return
            s, v = tok
            if s == eng and eng in trusted:
                return
            if s == eng and v <= cnt[eng]:
                pass
            if lastwait[eng].get(s, 0) >= v:
                return
            if cnt[s] < v:
                raise RuntimeError(f"{eng} waits future {s}>={v} (at {cnt[s]})")
            waits[s] = max(waits.get(s, 0), v)

        for b in reads:
            need(b.writer)
        for b in writes:
            need(b.writer)
            for t in b.readers:
                need(t)

        inc = 16 if dma else 1
        tok = (eng, cnt[eng] + inc)

        def body(h):
            for s, v in sorted(waits.items()):
                h.wait_ge(sem_h[s], v)
                lastwait[eng][s] = max(lastwait[eng][s], v)
            last = emit(h)
            last.then_inc(sem_h[eng], inc)

        getattr(blk, eng)(body)
        cnt[eng] += inc
        for b in reads:
            b.readers.append(tok)
        for b in writes:
            b.writer = tok
            b.readers = []
        return tok

    def claim(bufs):
        op("vector", lambda h: h.tensor_copy(mrow[:, 7:8], mrow[:, 6:7]),
           reads=[], writes=list(bufs))

    b_small = Buf("small")
    b_coefb = Buf("coefb")
    b_wact = Buf("wact")
    b_fA = Buf("fA")
    b_fB = Buf("fB")
    b_sc1 = Buf("sc1")
    b_mrow = Buf("mrow")
    b_bw = Buf("bw")
    b_coef = Buf("coef")
    b_pp = [Buf(f"pp{i}") for i in range(4)]
    b_wr = [Buf(f"wr{i}") for i in range(4)]
    b_wr2 = [Buf(f"wr2{i}") for i in range(2)]
    b_xr = [Buf("xr0"), Buf("xr1")]
    b_xn = Buf("xn")
    b_mem = Buf("mem")
    b_sel = Buf("sel")
    b_mnblk = Buf("mnblk")
    b_xc = Buf("xc")
    b_sq = [Buf("sq0"), Buf("sq1")]
    b_sgq = Buf("sgq")
    b_sgs = [Buf(f"sgs{i}") for i in range(6)]
    b_bst = [Buf("bst0"), Buf("bst1")]
    b_hid = [Buf(f"hid{i}") for i in range(FHC)]
    b_w2slot = [Buf("w2s0"), Buf("w2s1")]
    b_xin = Buf("xin")
    b_x0 = Buf("xhbm0")
    b_x1 = Buf("xhbm1")
    b_y = Buf("yout")
    b_w2mid = Buf("w2mid")
    b_w2fin = Buf("w2fin")
    b_fold = [Buf(f"fold{i}") for i in range(4)]
    b_fold2 = [Buf(f"fold2_{i}") for i in range(8)]
    AB = {n: Buf(n) for n in ("kr", "ckv", "qr", "aq", "cqr", "cqn", "t1b0", "t1b1",
                              "att", "ub", "wuv", "awo", "attT", "of", "agin",
                              "agout", "ckvt", "stg", "ckvr", "fin", "fence")}

    for dst, src in ((consts[:], consts_in), (selh[:], selh_in), (seln[:], seln_in),
                     (mask64[:], mask64_in), (g_sb[:, 0:CCH], g1_in),
                     (g_sb[:, CCH:2 * CCH], g2_in),
                     (g_sb[:, 2 * CCH:2 * CCH + KC], g3_in), (rope_sb[:], rope_in)):
        op("sync", lambda h, dst=dst, src=src: h.dma_start(out=dst, in_=src),
           writes=[b_small], dma=True)
    op("sync", lambda h: h.dma_start(out=bw_sb[:, 0:NB], in_=bw_in), writes=[b_bw], dma=True)
    op("sync", lambda h: h.dma_start(out=xr[:, 0:T], in_=mnorm_n[:, 0:T]), writes=[b_xr[0]], dma=True)
    op("sync", lambda h: h.dma_start(out=xr[:, T:2 * T], in_=mnorm_n[:, T:2 * T]), writes=[b_xr[1]], dma=True)
    op("gpsimd", lambda h: h.dma_start(out=mnblk_sb[:], in_=mnblk), writes=[b_mnblk], dma=True)
    op("gpsimd", lambda h: h.dma_start(out=arena[:, AR_MEM:AR_MEM + 8192], in_=mem_t), writes=[b_mem], dma=True)
    op("gpsimd", lambda h: h.dma_start(out=arena[:, AR_SEL:AR_SEL + 8192], in_=sel_in), writes=[b_sel], dma=True)

    def p_bw1(v):
        v.reduce_sum(mrow[:, 0:1], xr[:, 0:T], AX.X)
        v.reduce_sum(mrow[:, 1:2], xr[:, T:2 * T], AX.X)
        v.tensor_copy(mrow[:, 6:7], bw_sb[:, 0:1])
        v.tensor_copy(mrow[:, 6:7], bw_sb[:, 0:1])
        v.tensor_tensor(mrow[:, 0:1], mrow[:, 0:1], mrow[:, 1:2], ALU.add)
        v.tensor_copy(mrow[:, 6:7], bw_sb[:, 0:1])
        v.tensor_copy(mrow[:, 6:7], bw_sb[:, 0:1])
        v.tensor_scalar(mrow[:, 2:3], mrow[:, 0:1], A_EPS, None, ALU.is_ge)
        v.reduce_max(mrow[:, 3:4], bw_sb[:, 0:NB], AX.X)
        v.tensor_copy(mrow[:, 6:7], mrow[:, 3:4])
        v.tensor_copy(mrow[:, 6:7], mrow[:, 3:4])
        return v.tensor_scalar(bw_sb[:, 4:4 + NB], bw_sb[:, 0:NB], mrow[:, 3:4], None, ALU.subtract)
    op("vector", p_bw1, reads=[b_xr[0], b_xr[1], b_bw], writes=[b_mrow, b_bw])
    op("scalar", lambda h: h.activation(bw_sb[:, 8:8 + NB], bw_sb[:, 4:4 + NB], AF.Exp),
       reads=[b_bw], writes=[b_bw])

    def p_bw3(v):
        v.reduce_sum(mrow[:, 4:5], bw_sb[:, 8:8 + NB], AX.X)
        v.tensor_copy(mrow[:, 6:7], bw_sb[:, 0:1])
        v.tensor_copy(mrow[:, 6:7], bw_sb[:, 0:1])
        v.reciprocal(mrow[:, 5:6], mrow[:, 4:5])
        v.tensor_copy(mrow[:, 6:7], bw_sb[:, 0:1])
        v.tensor_copy(mrow[:, 6:7], bw_sb[:, 0:1])
        v.tensor_scalar(bw_sb[:, 12:12 + NB], bw_sb[:, 8:8 + NB], mrow[:, 5:6], None, ALU.mult)
        v.tensor_copy(mrow[:, 6:7], bw_sb[:, 0:1])
        v.tensor_copy(mrow[:, 6:7], bw_sb[:, 0:1])
        v.tensor_copy(arena[0:16, AR_CFB:AR_CFB + 4], bw_sb[0:16, 12:16])
        return v.tensor_copy(arena[0:4, AR_CFB + 8:AR_CFB + 9], mrow[0:4, 2:3])
    op("vector", p_bw3, reads=[b_bw, b_mrow], writes=[b_bw, b_mrow, b_coefb])

    def p_bw4(t):
        t.matmul(PP[2][0:64, 0:4], selh[0:16, 0:64], arena[0:16, AR_CFB:AR_CFB + 4],
                 start=True, stop=True)
        return t.matmul(PP[2][0:64, 8:9], seln[0:4, 0:64], arena[0:4, AR_CFB + 8:AR_CFB + 9],
                        start=True, stop=True)
    op("tensor", p_bw4, reads=[b_small, b_coefb], writes=[b_pp[2]])

    def p_bw5(v):
        v.tensor_tensor(wact[:, 0:4], PP[2][0:64, 0:4], mask64[0:64, 0:4], ALU.mult)
        v.tensor_copy(mrow[:, 6:7], bw_sb[:, 0:1])
        v.tensor_copy(mrow[:, 6:7], bw_sb[:, 0:1])
        v.reduce_sum(wact[:, 2:3], wact[:, 0:4], AX.X)
        v.tensor_copy(mrow[:, 6:7], bw_sb[:, 0:1])
        v.tensor_copy(mrow[:, 6:7], bw_sb[:, 0:1])
        v.tensor_tensor(wact[:, 1:2], wact[:, 2:3], PP[2][0:64, 8:9], ALU.mult)
        v.tensor_copy(arena[0:64, AR_CFB + 1024:AR_CFB + 1028], PP[2][0:64, 0:4])
        v.tensor_copy(arena[0:64, AR_CFB + 1028:AR_CFB + 1029], PP[2][0:64, 8:9])
        v.tensor_copy(arena[0:16, AR_CFB + 1032:AR_CFB + 1048], bw_sb[0:16, 0:16])
        return v.tensor_copy(arena[0:8, AR_CFB + 1048:AR_CFB + 1056], mrow[0:8, 0:8])
    op("vector", p_bw5, reads=[b_pp[2], b_small], writes=[b_wact, b_coefb])

    def w2_fold():
        FG = 8

        def fold_load1(g2):
            sl = g2 % 4
            op("gpsimd",
               lambda h, g2=g2, sl=sl: h.dma_start(
                   out=wr[:, sl * 2048:sl * 2048 + 2048], in_=w2p1[g2]),
               writes=[b_wr[sl]], dma=True)

        for g2 in range(4):
            fold_load1(g2)
        for g2 in range(64):
            sl = g2 % 4
            i = g2 % 2
            stg = b_fold[g2 % 4]
            soff = AR_FOLD + (g2 % 4) * 2048

            def p_m1(t, sl=sl, i=i):
                t.matmul(PPl(i), I2H64, wr[:, sl * 2048:sl * 2048 + 512], start=True, stop=True)
                return t.matmul(PPh(i), I2H64, wr[:, sl * 2048 + 512:sl * 2048 + 1024],
                                start=True, stop=True)

            def p_m2(t, sl=sl, i=i):
                t.matmul(PPl(i), I2H64, wr[:, sl * 2048 + 1024:sl * 2048 + 1536], start=True, stop=True)
                return t.matmul(PPh(i), I2H64, wr[:, sl * 2048 + 1536:sl * 2048 + 2048],
                                start=True, stop=True)
            op("tensor", p_m1, reads=[b_small, b_wr[sl]], writes=[b_pp[i]])
            op("vector", lambda h, i=i, soff=soff: h.tensor_copy(arena[:, soff:soff + 1024], PP[i][:]),
               reads=[b_pp[i]], writes=[stg])
            op("tensor", p_m2, reads=[b_small, b_wr[sl]], writes=[b_pp[i]])
            op("vector", lambda h, i=i, soff=soff: h.tensor_copy(arena[:, soff + 1024:soff + 2048], PP[i][:]),
               reads=[b_pp[i]], writes=[stg])
            op("sync",
               lambda h, g2=g2, soff=soff: h.dma_start(
                   out=w2mid[128 * g2:128 * g2 + 128, :], in_=arena[:, soff:soff + 2048]),
               reads=[stg], writes=[b_w2mid], dma=True)
            if g2 + 4 < 64:
                fold_load1(g2 + 4)

        def fold_load2(cpos):
            sl = cpos % 4
            sap = bass.AP(w2mid.tensor, cpos * 2048, [[64 * 2048, 128], [1, 2048]])
            op("gpsimd",
               lambda h, sap=sap, sl=sl: h.dma_start(out=wr[:, sl * 2048:sl * 2048 + 2048], in_=sap),
               reads=[b_w2mid], writes=[b_wr[sl]], dma=True)

        for cpos in range(4):
            fold_load2(cpos)
        for cpos in range(64):
            sl = cpos % 4
            i = 2 + (cpos % 2)
            stg = b_fold2[cpos % FG]
            soff = AR_FOLD + (cpos % FG) * 2048

            def q_m1(t, sl=sl, i=i):
                t.matmul(PPl(i), H128, wr[:, sl * 2048:sl * 2048 + 512], start=True, stop=True)
                return t.matmul(PPh(i), H128, wr[:, sl * 2048 + 512:sl * 2048 + 1024],
                                start=True, stop=True)

            def q_m2(t, sl=sl, i=i):
                t.matmul(PPl(i), H128, wr[:, sl * 2048 + 1024:sl * 2048 + 1536], start=True, stop=True)
                return t.matmul(PPh(i), H128, wr[:, sl * 2048 + 1536:sl * 2048 + 2048],
                                start=True, stop=True)
            op("tensor", q_m1, reads=[b_small, b_wr[sl]], writes=[b_pp[i]])
            op("scalar", lambda h, i=i, soff=soff: h.activation(
                arena[:, soff:soff + 1024], PP[i][:], AF.Copy, scale=FWHT_SCALE),
               reads=[b_pp[i]], writes=[stg])
            op("tensor", q_m2, reads=[b_small, b_wr[sl]], writes=[b_pp[i]])
            op("scalar", lambda h, i=i, soff=soff: h.activation(
                arena[:, soff + 1024:soff + 2048], PP[i][:], AF.Copy, scale=FWHT_SCALE),
               reads=[b_pp[i]], writes=[stg])
            if cpos + 4 < 64:
                fold_load2(cpos + 4)
            if cpos % FG == FG - 1:
                grp = cpos // FG
                rd = [b_fold2[j] for j in range(FG)]
                for eo in range(KC):
                    src2 = bass.AP(arena, AR_FOLD + eo * 128,
                                   [[65536, 128], [2048, FG], [1, 128]])
                    op("sync",
                       lambda h, eo=eo, grp=grp, src2=src2: h.dma_start(
                           out=w2fin[eo][:, grp * FG * 128:grp * FG * 128 + FG * 128],
                           in_=src2),
                       reads=rd, writes=[b_w2fin], dma=True)

    if upto >= 2:
        w2_fold()

    def ln_rows_finish(eps, n_feat):
        def p_r(v):
            v.tensor_scalar(fA[:], PP[2][:], 1.0 / n_feat, None, ALU.mult)
            v.tensor_scalar(fB[:], PP[3][:], 1.0 / n_feat, None, ALU.mult)
            v.tensor_tensor(sc1[:], fA[:], fA[:], ALU.mult)
            v.tensor_tensor(fB[:], fB[:], sc1[:], ALU.subtract)
            v.tensor_scalar(fB[:], fB[:], eps, None, ALU.add)
            return v.reciprocal(fB[:], fB[:])
        op("vector", p_r, reads=[b_pp[2], b_pp[3]], writes=[b_fA, b_fB, b_sc1])
        op("scalar", lambda h: h.activation(sc1[:], fB[:], AF.Sqrt),
           reads=[b_fB], writes=[b_sc1])

        def p_b(v):
            v.tensor_tensor(fB[:], fA[:], sc1[:], ALU.mult)
            v.tensor_scalar(fB[:], fB[:], -1.0, None, ALU.mult)
            return v.tensor_copy(fA[:], sc1[:])
        op("vector", p_b, reads=[b_fA, b_sc1], writes=[b_fA, b_fB])

    def ln_apply(xc_off):
        for k in range(KC):
            slot = sc1[:] if k % 2 == 0 else xr[:, 0:T]
            bslot = b_sc1 if k % 2 == 0 else b_xr[0]
            op("scalar",
               lambda h, k=k, slot=slot: h.activation(
                   slot, arena[:, xc_off + k * T:xc_off + (k + 1) * T], AF.Copy),
               reads=[b_xc], writes=[bslot])

            def p_a(v, k=k, slot=slot):
                v.tensor_tensor(slot, slot, fA[:], ALU.mult)
                return v.tensor_tensor(xn[:, k * T:(k + 1) * T], slot, fB[:], ALU.add)
            op("vector", p_a, reads=[bslot, b_fA, b_fB], writes=[bslot, b_xn])

    def ln_stats_chunk(k, n_ch, xc_off, sq_off):
        sq = b_sq[k % 2]
        op("scalar",
           lambda h, k=k: h.activation(
               arena[:, sq_off + (k % 2) * T:sq_off + (k % 2) * T + T],
               arena[:, xc_off + k * T:xc_off + (k + 1) * T], AF.Square),
           reads=[b_xc], writes=[sq])

        def p_m(t, k=k):
            t.matmul(PPl(2), ONES, arena[:, xc_off + k * T:xc_off + k * T + 512],
                     start=(k == 0), stop=(k == n_ch - 1))
            t.matmul(PPh(2), ONES, arena[:, xc_off + k * T + 512:xc_off + k * T + 1024],
                     start=(k == 0), stop=(k == n_ch - 1))
            t.matmul(PPl(3), ONES, arena[:, sq_off + (k % 2) * T:sq_off + (k % 2) * T + 512],
                     start=(k == 0), stop=(k == n_ch - 1))
            return t.matmul(PPh(3), ONES,
                            arena[:, sq_off + (k % 2) * T + 512:sq_off + (k % 2) * T + 1024],
                            start=(k == 0), stop=(k == n_ch - 1))
        op("tensor", p_m, reads=[b_xc, sq, b_small], writes=[b_pp[2], b_pp[3]])

    def ln_classic(src, src_buf, xc_off, sq_off):
        for k in range(KC):
            xrb = b_xr[k % 2]
            op("sync",
               lambda h, k=k: h.dma_start(out=xr[:, (k % 2) * T:(k % 2) * T + T], in_=src[k]),
               reads=[src_buf], writes=[xrb], dma=True)
            op("vector",
               lambda h, k=k: h.tensor_copy(
                   arena[:, xc_off + k * T:xc_off + (k + 1) * T],
                   xr[:, (k % 2) * T:(k % 2) * T + T]),
               reads=[xrb], writes=[b_xc])
            ln_stats_chunk(k, KC, xc_off, sq_off)
        ln_rows_finish(LN_EPS, E)
        ln_apply(xc_off)

    def projection(wsrc, n_eo, nkw, rhs_fn, rhs_bufs, evac_fn,
                   ring=None, ring_bufs=None, nslots=4, slot_elems=2048,
                   qeng="gpsimd", wsrc_buf=None, tile_elems=None):
        if ring is None:
            ring, ring_bufs = wr, b_wr
        if tile_elems is None:
            tile_elems = nkw * 128

        def emit_load(eo):
            sl = eo % nslots
            op(qeng,
               lambda h, eo=eo, sl=sl: h.dma_start(
                   out=ring[:, sl * slot_elems:sl * slot_elems + tile_elems], in_=wsrc[eo]),
               reads=([wsrc_buf] if wsrc_buf is not None else []),
               writes=[ring_bufs[sl]], dma=True)

        for eo in range(min(nslots, n_eo)):
            emit_load(eo)
        for eo in range(n_eo):
            sl = eo % nslots
            i = eo % 2

            def p_m(t, eo=eo, sl=sl, i=i):
                for k in range(nkw):
                    lhs = ring[:, sl * slot_elems + k * 128:sl * slot_elems + (k + 1) * 128]
                    t.matmul(PPl(i), lhs, rhs_fn(k, 0), start=(k == 0), stop=(k == nkw - 1))
                    mi = t.matmul(PPh(i), lhs, rhs_fn(k, 1), start=(k == 0), stop=(k == nkw - 1))
                return mi
            op("tensor", p_m, reads=[ring_bufs[sl]] + list(rhs_bufs), writes=[b_pp[i]])
            if eo + nslots < n_eo:
                emit_load(eo + nslots)
            evac_fn(eo, i)

    def xn_rhs(k, hh):
        return xn[:, k * T + hh * 512:k * T + hh * 512 + 512]

    def proj_residual(wsrc, xold, xold_buf, xnew, xnew_buf, fuse_ln,
                      rhs_fn=None, rhs_bufs=None, qeng="gpsimd", wsrc_buf=None,
                      ring=None, ring_bufs=None, nslots=4, slot_elems=2048,
                      nkw=KC, tile_elems=None):
        if rhs_fn is None:
            rhs_fn, rhs_bufs = xn_rhs, [b_xn]

        def ev(eo, i):
            xrb = b_xr[eo % 2]
            op("sync",
               lambda h, eo=eo: h.dma_start(out=xr[:, (eo % 2) * T:(eo % 2) * T + T],
                                            in_=xold[eo]),
               reads=[xold_buf], writes=[xrb], dma=True)
            op("vector",
               lambda h, eo=eo, i=i: h.tensor_tensor(
                   xr[:, (eo % 2) * T:(eo % 2) * T + T],
                   xr[:, (eo % 2) * T:(eo % 2) * T + T], PP[i][:], ALU.add),
               reads=[xrb, b_pp[i]], writes=[xrb])
            op("sync",
               lambda h, eo=eo: h.dma_start(out=xnew[eo],
                                            in_=xr[:, (eo % 2) * T:(eo % 2) * T + T]),
               reads=[xrb], writes=[xnew_buf], dma=True)
            if fuse_ln:
                op("vector",
                   lambda h, eo=eo: h.tensor_copy(
                       arena[:, AR_XC + eo * T:AR_XC + (eo + 1) * T],
                       xr[:, (eo % 2) * T:(eo % 2) * T + T]),
                   reads=[xrb], writes=[b_xc])
                ln_stats_chunk(eo, KC, AR_XC, AR_SQ1)
        projection(wsrc, KC, nkw, rhs_fn, rhs_bufs, ev, ring=ring, ring_bufs=ring_bufs,
                   nslots=nslots, slot_elems=slot_elems, qeng=qeng, wsrc_buf=wsrc_buf,
                   tile_elems=tile_elems)
        if fuse_ln:
            ln_rows_finish(LN_EPS, E)
            ln_apply(AR_XC)

    def attn_a():
        ln_classic(x_in, b_xin, AR_XC, AR_SQ1)

        def ev_sgq(eo, i):
            op("vector",
               lambda h, i=i: h.tensor_scalar(sc1[:], PP[i][:], 0.0, None, ALU.min),
               reads=[b_pp[i]], writes=[b_sc1])
            op("scalar", lambda h: h.activation(fA[:], sc1[:], AF.Exp),
               reads=[b_sc1], writes=[b_fA])
            ew = [b_sgq] + (b_fold + b_fold2 if eo == 0 else [])
            op("vector",
               lambda h, eo=eo, i=i: h.scalar_tensor_tensor(
                   arena[:, AR_SGQ + eo * T:AR_SGQ + (eo + 1) * T],
                   PP[i][:], 0.0, fA[:], ALU.max, ALU.add),
               reads=[b_pp[i], b_fA], writes=ew)
        projection(wq_t, KC, KC, xn_rhs, [b_xn], ev_sgq)

        def p_norm(t):
            for k in range(KC):
                t.matmul(PP[2][0:64, 0:512], mnblk_sb[:, k * 64:(k + 1) * 64],
                         arena[:, AR_SGQ + k * T:AR_SGQ + k * T + 512],
                         start=(k == 0), stop=(k == KC - 1))
                mi = t.matmul(PP[2][0:64, 512:1024], mnblk_sb[:, k * 64:(k + 1) * 64],
                              arena[:, AR_SGQ + k * T + 512:AR_SGQ + k * T + 1024],
                              start=(k == 0), stop=(k == KC - 1))
            return mi
        op("tensor", p_norm, reads=[b_sgq, b_mnblk], writes=[b_pp[2]])

        def p_coef(v):
            v.tensor_scalar(fB[0:64, :], PP[2][0:64, :], A_EPS, None, ALU.max)
            v.reciprocal(fB[0:64, :], fB[0:64, :])
            v.tensor_scalar(fB[0:64, :], fB[0:64, :], wact[0:64, 1:2], None, ALU.mult)
            return v.tensor_copy(arena[0:64, AR_CFB:AR_CFB + T], fB[0:64, :])
        op("vector", p_coef, reads=[b_pp[2], b_wact], writes=[b_fB, b_coefb])

        for h_ in range(H):
            pa = h_ % 2
            for n_ in range(NB):
                j = h_ * NB + n_
                pb = 2 + (j % 2)
                r = 4 * h_ + n_

                def p_bm(t, r=r, pb=pb):
                    t.matmul(PPl(pb), arena[0:64, AR_SEL + r * 128:AR_SEL + (r + 1) * 128],
                             arena[0:64, AR_CFB:AR_CFB + 512], start=True, stop=True)
                    return t.matmul(PPh(pb), arena[0:64, AR_SEL + r * 128:AR_SEL + (r + 1) * 128],
                                    arena[0:64, AR_CFB + 512:AR_CFB + 1024], start=True, stop=True)
                op("tensor", p_bm, reads=[b_sel, b_coefb], writes=[b_pp[pb]])
                op("scalar",
                   lambda h, j=j, pb=pb: h.activation(
                       arena[:, AR_SGS + (6 + j % 2) * T:AR_SGS + (6 + j % 2) * T + T],
                       PP[pb][:], AF.Copy),
                   reads=[b_pp[pb]], writes=[b_bst[j % 2]])
                op("vector",
                   lambda h, j=j, h_=h_: h.tensor_tensor(
                       arena[:, AR_SGS + (j % 6) * T:AR_SGS + (j % 6) * T + T],
                       arena[:, AR_SGQ + h_ * T:AR_SGQ + (h_ + 1) * T],
                       arena[:, AR_SGS + (6 + j % 2) * T:AR_SGS + (6 + j % 2) * T + T],
                       ALU.mult),
                   reads=[b_sgq, b_bst[j % 2]], writes=[b_sgs[j % 6]])

                def p_am(t, j=j, n_=n_, h_=h_, pa=pa):
                    ms = arena[:, AR_MEM + (n_ * H + h_) * 128:AR_MEM + (n_ * H + h_ + 1) * 128]
                    t.matmul(PPl(pa), ms,
                             arena[:, AR_SGS + (j % 6) * T:AR_SGS + (j % 6) * T + 512],
                             start=(n_ == 0), stop=(n_ == NB - 1))
                    return t.matmul(PPh(pa), ms,
                                    arena[:, AR_SGS + (j % 6) * T + 512:AR_SGS + (j % 6) * T + 1024],
                                    start=(n_ == 0), stop=(n_ == NB - 1))
                op("tensor", p_am, reads=[b_mem, b_sgs[j % 6]], writes=[b_pp[pa]])
            op("vector",
               lambda h, h_=h_, pa=pa: h.tensor_copy(xn[:, h_ * T:(h_ + 1) * T], PP[pa][:]),
               reads=[b_pp[pa]], writes=[b_xn])

        proj_residual(wo_t, x_in, b_xin, x_hbm[0], b_x0, fuse_ln=True)

    def ffn(xold, xold_buf, xnew, xnew_buf, final, arena_claim):
        claim(arena_claim + b_hid)
        for hc in range(FHC):
            sl1 = (2 * hc) % 4
            sl3 = (2 * hc + 1) % 4
            op("gpsimd",
               lambda h, hc=hc, sl1=sl1: h.dma_start(
                   out=wr[:, sl1 * 2048:sl1 * 2048 + 2048], in_=w1_t[hc]),
               writes=[b_wr[sl1]], dma=True)
            op("gpsimd",
               lambda h, hc=hc, sl3=sl3: h.dma_start(
                   out=wr[:, sl3 * 2048:sl3 * 2048 + 2048], in_=w3_t[hc]),
               writes=[b_wr[sl3]], dma=True)
            pg = hc % 2
            pl = 2 + hc % 2

            def p_g(t, sl1=sl1, pg=pg):
                for k in range(KC):
                    lhs = wr[:, sl1 * 2048 + k * 128:sl1 * 2048 + (k + 1) * 128]
                    t.matmul(PPl(pg), lhs, xn_rhs(k, 0), start=(k == 0), stop=(k == KC - 1))
                    mi = t.matmul(PPh(pg), lhs, xn_rhs(k, 1), start=(k == 0), stop=(k == KC - 1))
                return mi
            op("tensor", p_g, reads=[b_wr[sl1], b_xn], writes=[b_pp[pg]])

            def p_l(t, sl3=sl3, pl=pl):
                for k in range(KC):
                    lhs = wr[:, sl3 * 2048 + k * 128:sl3 * 2048 + (k + 1) * 128]
                    t.matmul(PPl(pl), lhs, xn_rhs(k, 0), start=(k == 0), stop=(k == KC - 1))
                    mi = t.matmul(PPh(pl), lhs, xn_rhs(k, 1), start=(k == 0), stop=(k == KC - 1))
                return mi
            op("tensor", p_l, reads=[b_wr[sl3], b_xn], writes=[b_pp[pl]])
            op("scalar",
               lambda h, pg=pg: h.activation(sc1[:], PP[pg][:], AF.Silu),
               reads=[b_pp[pg]], writes=[b_sc1])
            op("vector",
               lambda h, hc=hc, pl=pl: h.tensor_tensor(
                   arena[:, AR_HID + hc * T:AR_HID + (hc + 1) * T], sc1[:], PP[pl][:], ALU.mult),
               reads=[b_sc1, b_pp[pl]], writes=[b_hid[hc]])

        claim([b_xn, b_w2slot[0], b_w2slot[1]])

        def hid_rhs(k, hh):
            return arena[:, AR_HID + k * T + hh * 512:AR_HID + k * T + hh * 512 + 512]
        proj_residual(w2fin, xold, xold_buf, xnew, xnew_buf, fuse_ln=False,
                      rhs_fn=hid_rhs, rhs_bufs=b_hid, qeng="sync", wsrc_buf=b_w2fin,
                      ring=xn, ring_bufs=b_w2slot, nslots=2, slot_elems=8192,
                      nkw=FHC, tile_elems=FHC * 128)

    def rms_rows_finish(n_feat):
        def p_r(v):
            v.tensor_scalar(fA[:], PP[2][:], 1.0 / n_feat, None, ALU.mult)
            v.tensor_scalar(fA[:], fA[:], RMS_EPS, None, ALU.add)
            return v.reciprocal(fA[:], fA[:])
        op("vector", p_r, reads=[b_pp[2]], writes=[b_fA])
        op("scalar", lambda h: h.activation(fA[:], fA[:], AF.Sqrt),
           reads=[b_fA], writes=[b_fA])

    def mk_cproj_evac(raw_off, raw_buf, n_ch):
        def ev(eo, i):
            op("vector",
               lambda h, eo=eo, i=i: h.tensor_copy(
                   arena[:, raw_off + eo * T:raw_off + (eo + 1) * T], PP[i][:]),
               reads=[b_pp[i]], writes=[raw_buf])
            op("scalar",
               lambda h, eo=eo, i=i: h.activation(
                   arena[:, AB_SCR + (eo % 2) * T:AB_SCR + (eo % 2) * T + T],
                   PP[i][:], AF.Square),
               reads=[b_pp[i]], writes=[b_sq[eo % 2]])

            def p_m(t, eo=eo):
                t.matmul(PPl(2), ONES,
                         arena[:, AB_SCR + (eo % 2) * T:AB_SCR + (eo % 2) * T + 512],
                         start=(eo == 0), stop=(eo == n_ch - 1))
                return t.matmul(PPh(2), ONES,
                                arena[:, AB_SCR + (eo % 2) * T + 512:AB_SCR + (eo % 2) * T + 1024],
                                start=(eo == 0), stop=(eo == n_ch - 1))
            op("tensor", p_m, reads=[b_sq[eo % 2], b_small], writes=[b_pp[2]])
        return ev

    def rms_apply(n_ch, raw_off, raw_buf, out_ap_fn, out_buf, g_col):
        for cc in range(n_ch):
            op("scalar",
               lambda h, cc=cc: h.activation(
                   sc1[:], arena[:, raw_off + cc * T:raw_off + (cc + 1) * T], AF.Copy),
               reads=[raw_buf], writes=[b_sc1])
            op("vector",
               lambda h: h.tensor_tensor(sc1[:], sc1[:], fA[:], ALU.mult),
               reads=[b_sc1, b_fA], writes=[b_sc1])
            op("scalar",
               lambda h, cc=cc: h.activation(out_ap_fn(cc), sc1[:], AF.Copy, scale=g_col(cc)),
               reads=[b_sc1, b_small], writes=[out_buf])

    def mk_rope_evac(dst_fn, dst_buf, extra_first=()):
        def ev(eo, i):
            op("scalar",
               lambda h, i=i: h.activation(arena[:, AB_SCR:AB_SCR + T], PP[i][:], AF.Copy),
               reads=[b_pp[i]], writes=[b_sq[0]])
            ew = [b_sq[0], b_sq[1], dst_buf] + (list(extra_first) if eo == 0 else [])

            def p_v(v, eo=eo):
                mask = []
                for q in range(0, 32, 2):
                    mask += [q + 1, q]
                v.stream_shuffle(arena[:, AB_SCR + T:AB_SCR + 2 * T],
                                 arena[:, AB_SCR:AB_SCR + T], mask)
                v.tensor_tensor(arena[:, AB_SCR + T:AB_SCR + 2 * T],
                                arena[:, AB_SCR + T:AB_SCR + 2 * T], SINB, ALU.mult)
                v.tensor_tensor(arena[:, AB_SCR:AB_SCR + T],
                                arena[:, AB_SCR:AB_SCR + T], COSA, ALU.mult)
                return v.tensor_tensor(dst_fn(eo), arena[:, AB_SCR:AB_SCR + T],
                                       arena[:, AB_SCR + T:AB_SCR + 2 * T], ALU.add)
            op("vector", p_v, reads=[b_sq[0], b_small], writes=ew)
        return ev

    def attn_b():
        claim([b_xc, b_sq[0], b_sq[1]] + b_hid + [b_w2slot[0], b_w2slot[1]])
        ln_classic(x_hbm[1], b_x1, AR_XC, SQ3)

        claim([AB["cqr"], b_sq[0], b_sq[1]])
        projection(wdq_t, CCH, KC, xn_rhs, [b_xn], mk_cproj_evac(AB_CQR, AB["cqr"], CCH))
        rms_rows_finish(C)
        claim([AB["cqn"]] + b_hid[:4])
        rms_apply(CCH, AB_CQR, AB["cqr"],
                  lambda cc: arena[:, AB_CQN + cc * T:AB_CQN + (cc + 1) * T], AB["cqn"], G1c)

        def cqn_rhs(k, hh):
            return arena[:, AB_CQN + k * T + hh * 512:AB_CQN + k * T + hh * 512 + 512]
        claim([AB["qr"]] + b_hid[40:56])
        projection(wqr_t, KC, CCH, cqn_rhs, [AB["cqn"]],
                   mk_rope_evac(lambda eo: arena[:, AB_QR + eo * T:AB_QR + (eo + 1) * T],
                                AB["qr"]),
                   ring=wr2, ring_bufs=b_wr2, nslots=2, slot_elems=512, tile_elems=CCH * 128)

        AB_CKVR = 4096
        claim(b_hid[32:40])
        claim([AB["ckvr"], b_xc] + b_hid[4:8])
        projection(wdkv_t, CCH, KC, xn_rhs, [b_xn], mk_cproj_evac(AB_CKVR, AB["ckvr"], CCH))
        rms_rows_finish(C)
        claim([AB["ckv"]])
        rms_apply(CCH, AB_CKVR, AB["ckvr"],
                  lambda cc: arena[:, AB_CKV + cc * 2048:AB_CKV + cc * 2048 + T], AB["ckv"], G2c)

        claim([AB["kr"], AB["cqn"], AB["ckvr"]] + b_hid[:32])
        projection(wkr_t, KC, KC, xn_rhs, [b_xn],
                   mk_rope_evac(lambda eo: arena[:, AB_KR + eo * 2048:AB_KR + eo * 2048 + T],
                                AB["kr"]))

        claim([b_xn, AB["t1b0"], AB["t1b1"]])

        def cqr_rhs(k, hh):
            return arena[:, AB_CQR + k * T + hh * 512:AB_CQR + k * T + hh * 512 + 512]
        claim([AB["aq"], b_sq[0], b_sq[1]])
        for eo_a in range(CCH):
            sl_a = eo_a % 4
            op("gpsimd",
               lambda h, eo_a=eo_a, sl_a=sl_a: h.dma_start(
                   out=wr[:, sl_a * 2048:sl_a * 2048 + 2048], in_=wuk_n[eo_a]),
               writes=[b_wr[sl_a]], dma=True)
            pa = 2 + eo_a % 2
            i_a = eo_a % 2
            for k2 in range(KC):
                slw = k2 % 2
                op("gpsimd",
                   lambda h, k2=k2, slw=slw: h.dma_start(
                       out=wr2[:, slw * 512:slw * 512 + 512], in_=wuq_t[k2]),
                   writes=[b_wr2[slw]], dma=True)

                def p_t1(t, k2=k2, slw=slw, pa=pa):
                    for k in range(CCH):
                        lhs = wr2[:, slw * 512 + k * 128:slw * 512 + (k + 1) * 128]
                        t.matmul(PPl(pa), lhs, cqr_rhs(k, 0), start=(k == 0), stop=(k == CCH - 1))
                        mi = t.matmul(PPh(pa), lhs, cqr_rhs(k, 1), start=(k == 0), stop=(k == CCH - 1))
                    return mi
                op("tensor", p_t1, reads=[b_wr2[slw], AB["cqr"]], writes=[b_pp[pa]])
                tb = AB[f"t1b{k2 % 2}"]
                op("vector",
                   lambda h, k2=k2, pa=pa: h.tensor_copy(
                       xn[:, XN_T1B + (k2 % 2) * T:XN_T1B + (k2 % 2) * T + T], PP[pa][:]),
                   reads=[b_pp[pa]], writes=[tb])

                def p_aq(t, k2=k2, sl_a=sl_a, i_a=i_a):
                    lhs = wr[:, sl_a * 2048 + k2 * 128:sl_a * 2048 + (k2 + 1) * 128]
                    t.matmul(PPl(i_a), lhs,
                             xn[:, XN_T1B + (k2 % 2) * T:XN_T1B + (k2 % 2) * T + 512],
                             start=(k2 == 0), stop=(k2 == KC - 1))
                    return t.matmul(PPh(i_a), lhs,
                                    xn[:, XN_T1B + (k2 % 2) * T + 512:XN_T1B + (k2 % 2) * T + 1024],
                                    start=(k2 == 0), stop=(k2 == KC - 1))
                op("tensor", p_aq, reads=[b_wr[sl_a], tb], writes=[b_pp[i_a]])
            op("vector",
               lambda h, eo_a=eo_a, i_a=i_a: h.tensor_copy(
                   arena[:, AB_AQ + eo_a * T:AB_AQ + (eo_a + 1) * T], PP[i_a][:]),
               reads=[b_pp[i_a]], writes=[AB["aq"]])

        ck_src = bass.AP(arena, AB_CKV, [[65536, 128], [2048, CCH], [1, T]])
        ck_dst = bass.AP(ag_in.tensor, 0, [[T, 128], [128 * T, CCH], [1, T]])
        op("sync", lambda h: h.dma_start(out=ck_dst, in_=ck_src),
           reads=[AB["ckv"]], writes=[AB["agin"]], dma=True)
        kr_src = bass.AP(arena, AB_KR, [[65536, 128], [2048, KC], [1, T]])
        kr_dst = bass.AP(ag_in.tensor, CCH * 128 * T, [[T, 128], [128 * T, KC], [1, T]])
        op("sync", lambda h: h.dma_start(out=kr_dst, in_=kr_src),
           reads=[AB["kr"]], writes=[AB["agin"]], dma=True)
        op("sync", lambda h: h.dma_start(out=fence_in[:], in_=consts_in[0:1, :]),
           writes=[AB["fin"]], dma=True)
        op("gpsimd",
           lambda h: h.collective_compute(
               "AllGather", ALU.bypass,
               replica_groups=[[0, 1], [2, 3], [4, 5], [6, 7]],
               ins=[ag_in[:]], outs=[ag_out[:]]),
           reads=[AB["agin"]], writes=[AB["agout"]])
        op("gpsimd",
           lambda h: h.collective_compute(
               "AllReduce", ALU.add,
               replica_groups=[[0, 1], [2, 3], [4, 5], [6, 7]],
               ins=[fence_in[:]], outs=[fence_out[:]]),
           reads=[AB["fin"], AB["agout"]], writes=[AB["fence"]])
        for _ in range(2):
            op("gpsimd",
               lambda h: h.dma_start(out=mrow[:, 0:4],
                                     in_=bass.AP(ag_out.tensor, 0, [[4, 128], [1, 4]])),
               reads=[AB["fence"], AB["agout"]], writes=[b_mrow], dma=True)

        claim([b_xn, AB["t1b0"], AB["t1b1"], AB["wuv"]])
        for k in range(KC):
            op("gpsimd",
               lambda h, k=k: h.dma_start(out=xn[:, XN_WUV + k * 512:XN_WUV + (k + 1) * 512],
                                          in_=wuv_n[k]),
               reads=[], writes=[AB["wuv"]], dma=True)
        claim([AB["awo"], AB["stg"]])
        for eo in range(KC):
            sl = eo % 4
            op("gpsimd",
               lambda h, eo=eo, sl=sl: h.dma_start(
                   out=wr[:, sl * 2048:sl * 2048 + 2048], in_=wot_b[eo]),
               writes=[b_wr[sl]], dma=True)
            i = eo % 2

            def p_aw(t, eo=eo, sl=sl, i=i):
                for k in range(KC):
                    mi = t.matmul(PPl(i), wr[:, sl * 2048 + k * 128:sl * 2048 + (k + 1) * 128],
                                  xn[:, XN_WUV + k * 512:XN_WUV + (k + 1) * 512],
                                  start=(k == 0), stop=(k == KC - 1))
                return mi
            op("tensor", p_aw, reads=[b_wr[sl], AB["wuv"]], writes=[b_pp[i]])
            op("vector",
               lambda h, i=i, eo=eo: h.tensor_copy(
                   arena[:, AB_ATT + (eo % 2) * 512:AB_ATT + (eo % 2) * 512 + 512], PPl(i)),
               reads=[b_pp[i]], writes=[AB["stg"]])

            def p_tr(t, eo=eo, i=i):
                for cc in range(CCH):
                    mi = t.matmul(PPh(i)[:, 0:512][:, cc * 128:(cc + 1) * 128] if False else
                                  PP[i][:, 512 + cc * 128:512 + (cc + 1) * 128],
                                  arena[:, AB_ATT + (eo % 2) * 512 + cc * 128:AB_ATT + (eo % 2) * 512 + (cc + 1) * 128],
                                  IDENT, start=True, stop=True)
                return mi
            op("tensor", p_tr, reads=[AB["stg"], b_small], writes=[b_pp[i]])
            dst = bass.AP(xn, XN_AWO + eo * 128, [[KC * T, 128], [2048, CCH], [1, 128]])
            op("vector",
               lambda h, i=i, dst=dst: h.tensor_copy(dst, PP[i][:, 512:1024]),
               reads=[b_pp[i]], writes=[AB["awo"]])

        for r in range(2):
            for ec in range(KC):
                src = bass.AP(ag_out.tensor, r * AGN + CCH * 128 * T + ec * 128 * T,
                              [[T, 128], [1, T]])
                op("sync",
                   lambda h, r=r, ec=ec, src=src: h.dma_start(
                       out=arena[:, AB_KR + ec * 2048 + r * T:AB_KR + ec * 2048 + (r + 1) * T],
                       in_=src),
                   reads=[AB["agout"], AB["fence"], b_mrow], writes=[AB["kr"]], dma=True)
            for cc in range(CCH):
                src = bass.AP(ag_out.tensor, r * AGN + cc * 128 * T, [[T, 128], [1, T]])
                op("sync",
                   lambda h, r=r, cc=cc, src=src: h.dma_start(
                       out=arena[:, AB_CKV + cc * 2048 + r * T:AB_CKV + cc * 2048 + (r + 1) * T],
                       in_=src),
                   reads=[AB["agout"], AB["fence"], b_mrow], writes=[AB["ckv"]], dma=True)

        claim(b_wr + [AB["ckvt"]])
        for tk in range(KC):
            def p_ct(t, tk=tk):
                for cc in range(CCH):
                    mi = t.matmul(PP[2][:, cc * 128:(cc + 1) * 128],
                                  arena[:, AB_CKV + cc * 2048 + tk * 128:AB_CKV + cc * 2048 + (tk + 1) * 128],
                                  IDENT, start=True, stop=True)
                return mi
            op("tensor", p_ct, reads=[AB["ckv"], b_small], writes=[b_pp[2]])
            op("vector",
               lambda h, tk=tk: h.tensor_copy(wr[:, tk * 512:(tk + 1) * 512], PP[2][:, 0:512]),
               reads=[b_pp[2]], writes=[AB["ckvt"]])

        claim([AB["att"], AB["cqr"]])
        claim([AB["attT"], AB["of"], AB["wuv"], AB["stg"]])
        for q2 in range(8):
            for half in range(2):
                def p_sc(t, q2=q2, half=half):
                    for kq in range(2):
                        tgt = PP[half][:, kq * 512:(kq + 1) * 512]
                        first = True
                        for cc in range(CCH):
                            t.matmul(tgt, arena[0:128, AB_AQ + cc * T + q2 * 128:AB_AQ + cc * T + (q2 + 1) * 128],
                                     arena[:, AB_CKV + cc * 2048 + half * T + kq * 512:AB_CKV + cc * 2048 + half * T + (kq + 1) * 512],
                                     start=first, stop=False)
                            first = False
                        for ec in range(KC):
                            mi = t.matmul(tgt, arena[:, AB_QR + ec * T + q2 * 128:AB_QR + ec * T + (q2 + 1) * 128],
                                          arena[:, AB_KR + ec * 2048 + half * T + kq * 512:AB_KR + ec * 2048 + half * T + (kq + 1) * 512],
                                          start=False, stop=(ec == KC - 1))
                    return mi
                op("tensor", p_sc, reads=[AB["aq"], AB["ckv"], AB["qr"], AB["kr"]],
                   writes=[b_pp[half]])

            def p_mx(v):
                v.reduce_max(mrow[:, 0:1], PP[0][:], AX.X)
                v.reduce_max(mrow[:, 1:2], PP[1][:], AX.X)
                v.tensor_copy(mrow[:, 6:7], bw_sb[:, 0:1])
                v.tensor_copy(mrow[:, 6:7], bw_sb[:, 0:1])
                v.tensor_tensor(mrow[:, 0:1], mrow[:, 0:1], mrow[:, 1:2], ALU.max)
                v.tensor_copy(mrow[:, 6:7], bw_sb[:, 0:1])
                v.tensor_copy(mrow[:, 6:7], bw_sb[:, 0:1])
                return v.tensor_scalar(mrow[:, 1:2], mrow[:, 0:1], -SM_SCALE, None, ALU.mult)
            op("vector", p_mx, reads=[b_pp[0], b_pp[1]], writes=[b_mrow])
            op("scalar",
               lambda h: h.activation(arena[:, AB_ATT:AB_ATT + T], PP[0][:], AF.Exp,
                                      bias=mrow[:, 1:2], scale=SM_SCALE),
               reads=[b_pp[0], b_mrow], writes=[AB["att"]])
            op("scalar",
               lambda h: h.activation(arena[:, AB_ATT + T:AB_ATT + 2 * T], PP[1][:], AF.Exp,
                                      bias=mrow[:, 1:2], scale=SM_SCALE),
               reads=[b_pp[1], b_mrow], writes=[AB["att"]])

            def p_sum(v):
                v.reduce_sum(mrow[:, 2:3], arena[:, AB_ATT:AB_ATT + 2 * T], AX.X)
                v.tensor_copy(mrow[:, 6:7], bw_sb[:, 0:1])
                v.tensor_copy(mrow[:, 6:7], bw_sb[:, 0:1])
                v.reciprocal(mrow[:, 3:4], mrow[:, 2:3])
                v.tensor_copy(mrow[:, 6:7], bw_sb[:, 0:1])
                v.tensor_copy(mrow[:, 6:7], bw_sb[:, 0:1])
                return v.tensor_scalar(arena[:, AB_ATT:AB_ATT + 2 * T],
                                       arena[:, AB_ATT:AB_ATT + 2 * T],
                                       mrow[:, 3:4], None, ALU.mult)
            op("vector", p_sum, reads=[AB["att"]], writes=[AB["att"], b_mrow])

            for grp in range(4):
                pt = 2 + grp % 2

                def p_at(t, q2=q2, grp=grp, pt=pt):
                    for j in range(4):
                        tk = grp * 4 + j
                        mi = t.matmul(PP[pt][:, (grp % 2) * 0 + j * 128:j * 128 + 128],
                                      arena[:, AB_ATT + tk * 128:AB_ATT + (tk + 1) * 128],
                                      IDENT, start=True, stop=True)
                    return mi
                op("tensor", p_at, reads=[AB["att"], b_small], writes=[b_pp[pt]])
                op("vector",
                   lambda h, grp=grp, pt=pt: h.tensor_copy(
                       xn[:, XN_ATT + grp * 512:XN_ATT + (grp + 1) * 512], PP[pt][:, 0:512]),
                   reads=[b_pp[pt]], writes=[AB["attT"]])

            def p_o(t, q2=q2):
                for cc in range(CCH):
                    for tk in range(KC):
                        mi = t.matmul(PP[2][:, 512 + cc * 128:512 + (cc + 1) * 128],
                                      wr[:, tk * 512 + cc * 128:tk * 512 + (cc + 1) * 128],
                                      xn[:, XN_ATT + tk * 128:XN_ATT + (tk + 1) * 128],
                                      start=(tk == 0), stop=(tk == KC - 1))
                return mi
            op("tensor", p_o, reads=[AB["ckvt"], AB["attT"]], writes=[b_pp[2]])
            dst = bass.AP(xn, XN_OF + q2 * 128, [[KC * T, 128], [T, CCH], [1, 128]])
            op("vector", lambda h, dst=dst: h.tensor_copy(dst, PP[2][:, 512:1024]),
               reads=[b_pp[2]], writes=[AB["of"]])

        claim([AB["ub"], AB["qr"], b_sq[0], b_sq[1]])

        def p_u(eo, i):
            op("vector",
               lambda h, eo=eo, i=i: h.tensor_copy(
                   arena[:, AB_UB + eo * T:AB_UB + (eo + 1) * T], PP[i][:]),
               reads=[b_pp[i]], writes=[AB["ub"]])
            op("scalar",
               lambda h, eo=eo, i=i: h.activation(
                   arena[:, AB_SCR + (eo % 2) * T:AB_SCR + (eo % 2) * T + T],
                   PP[i][:], AF.Square),
               reads=[b_pp[i]], writes=[b_sq[eo % 2]])

            def p_m(t, eo=eo):
                t.matmul(PPl(2), ONES,
                         arena[:, AB_SCR + (eo % 2) * T:AB_SCR + (eo % 2) * T + 512],
                         start=(eo == 0), stop=(eo == KC - 1))
                return t.matmul(PPh(2), ONES,
                                arena[:, AB_SCR + (eo % 2) * T + 512:AB_SCR + (eo % 2) * T + 1024],
                                start=(eo == 0), stop=(eo == KC - 1))
            op("tensor", p_m, reads=[b_sq[eo % 2], b_small], writes=[b_pp[2]])

        for eo in range(KC):
            i = eo % 2

            def p_up(t, eo=eo, i=i):
                for cc in range(CCH):
                    lhs = xn[:, XN_AWO + cc * 2048 + eo * 128:XN_AWO + cc * 2048 + (eo + 1) * 128]
                    t.matmul(PPl(i), lhs, xn[:, XN_OF + cc * T:XN_OF + cc * T + 512],
                             start=(cc == 0), stop=(cc == CCH - 1))
                    mi = t.matmul(PPh(i), lhs, xn[:, XN_OF + cc * T + 512:XN_OF + cc * T + 1024],
                                  start=(cc == 0), stop=(cc == CCH - 1))
                return mi
            op("tensor", p_up, reads=[AB["awo"], AB["of"]], writes=[b_pp[i]])
            p_u(eo, i)
        rms_rows_finish(E)

        claim([b_xc] + b_hid[:16])
        for k in range(KC):
            xrb = b_xr[k % 2]
            op("sync",
               lambda h, k=k: h.dma_start(out=xr[:, (k % 2) * T:(k % 2) * T + T],
                                          in_=x_hbm[1][k]),
               reads=[b_x1], writes=[xrb], dma=True)
            op("scalar",
               lambda h, k=k: h.activation(sc1[:], arena[:, AB_UB + k * T:AB_UB + (k + 1) * T],
                                           AF.Copy, scale=G3c(k)),
               reads=[AB["ub"], b_small], writes=[b_sc1])

            def p_r(v, k=k):
                v.tensor_tensor(sc1[:], sc1[:], fA[:], ALU.mult)
                return v.tensor_tensor(xr[:, (k % 2) * T:(k % 2) * T + T],
                                       xr[:, (k % 2) * T:(k % 2) * T + T], sc1[:], ALU.add)
            op("vector", p_r, reads=[b_sc1, b_fA, xrb], writes=[b_sc1, xrb])
            op("sync",
               lambda h, k=k: h.dma_start(out=x_hbm[0][k],
                                          in_=xr[:, (k % 2) * T:(k % 2) * T + T]),
               reads=[xrb], writes=[b_x0], dma=True)
            op("vector",
               lambda h, k=k: h.tensor_copy(arena[:, AR_XC + k * T:AR_XC + (k + 1) * T],
                                            xr[:, (k % 2) * T:(k % 2) * T + T]),
               reads=[xrb], writes=[b_xc])
            ln_stats_chunk(k, KC, AR_XC, AB_SCR)
        ln_rows_finish(LN_EPS, E)
        claim([b_xn, AB["attT"], AB["of"], AB["wuv"], AB["awo"], AB["t1b0"], AB["t1b1"],
               b_w2slot[0], b_w2slot[1]])
        ln_apply(AR_XC)

    if upto >= 1:
        attn_a()
    if upto >= 2:
        ffn(x_hbm[0], b_x0, x_hbm[1], b_x1, final=False,
            arena_claim=[b_xc, b_sgq, b_mem, b_sel, b_coefb] + b_sgs + b_bst + b_sq)
    if upto >= 3:
        attn_b()
    if upto >= 4:
        ffn(x_hbm[0], b_x0, y_out, b_y, final=True,
            arena_claim=[b_xc] + [AB[k] for k in ("kr", "ckv", "qr", "aq", "cqr",
                                                  "cqn", "att", "ub", "ckvr", "ckvt")]
                        + b_sq + b_wr)

    def finalize(src, src_buf):
        for k in range(KC):
            xrb = b_xr[k % 2]
            op("sync", lambda h, k=k: h.dma_start(
                out=xr[:, (k % 2) * T:(k % 2) * T + T], in_=src[k]),
               reads=[src_buf], writes=[xrb], dma=True)
            op("sync", lambda h, k=k: h.dma_start(
                out=y_out[k], in_=xr[:, (k % 2) * T:(k % 2) * T + T]),
               reads=[xrb], writes=[b_y], dma=True)

    if upto == 1:
        finalize(x_hbm[0], b_x0)
    elif upto == 2:
        finalize(x_hbm[1], b_x1)
    elif upto == 3:
        finalize(x_hbm[0], b_x0)

    if DBG is not None:
        allb = [b_xn, b_sgq, b_xc, b_coefb, b_sel, b_mem, b_wact] + b_sgs + b_bst + b_hid[:1] + [AB["kr"], AB["ckv"], AB["qr"], AB["aq"], AB["cqr"], AB["of"], AB["attT"], AB["awo"], AB["ub"], AB["ckvt"], AB["ckvr"]]
        if isinstance(DBG, tuple):
            dl = [(DBG[0], DBG[1] + kk * T) for kk in range(KC)]
        else:
            dl = DBG
        for kk, (tn, off) in enumerate(dl):
            srcten = arena if tn == "arena" else xn
            lim = 65536 if tn == "arena" else KC * T
            o0 = min(off, lim - T)
            op("gpsimd", lambda h, kk=kk, o0=o0, srcten=srcten: h.dma_start(
                out=y_out[kk], in_=srcten[:, o0:o0 + T]),
               reads=allb, writes=[b_y], dma=True)

    op("vector", lambda h: h.tensor_copy(mrow[:, 6:7], mrow[:, 5:6]),
       reads=[b_y], writes=[b_mrow])

    blkg.__exit__(None, None, None)
    for g in reversed(es):
        g.__exit__(None, None, None)
    return nc


def _tile_weight_T(w):
    win, wout = w.shape[1], w.shape[0]
    nk, neo = win // 128, wout // 128
    wt = np.ascontiguousarray(w.T)
    t = wt.reshape(nk, 128, neo, 128)
    return np.ascontiguousarray(t.transpose(2, 1, 0, 3).reshape(neo, 128, nk * 128))


def _tile_weight_N(w):
    win, wout = w.shape
    nk, neo = win // 128, wout // 128
    t = w.reshape(nk, 128, neo, 128)
    return np.ascontiguousarray(t.transpose(2, 1, 0, 3).reshape(neo, 128, nk * 128))


def host_prepare(inputs):
    f32 = np.float32
    d = {}
    d["wqT"] = _tile_weight_T(np.asarray(inputs["L0_attn_a_Wq"], f32))
    d["woT"] = _tile_weight_T(np.asarray(inputs["L0_attn_a_Wo"], f32))
    mem = np.asarray(inputs["L0_attn_a_memories"], f32)
    d["mem"] = np.ascontiguousarray(
        mem.reshape(NB * H, 128, 128).transpose(1, 0, 2).reshape(128, NB * H * 128))
    mn = np.asarray(inputs["L0_attn_a_memory_norms"], f32)
    mb = np.zeros((KC, 128, 64), f32)
    for h in range(H):
        for n in range(NB):
            mb[h, :, 4 * h + n] = mn[n, h, :]
    d["mnblk"] = np.ascontiguousarray(mb.transpose(1, 0, 2).reshape(128, KC * 64))
    mn_n = np.zeros((128, H * D), f32)
    mn_n[0:NB] = mn.reshape(NB, H * D)
    d["mnorm_n"] = mn_n
    bw = np.zeros((128, NB), f32)
    bw[0:H] = np.asarray(inputs["L0_attn_a_bank_weights"], f32)
    d["bw"] = bw
    d["w1T"] = _tile_weight_T(np.asarray(inputs["L0_ffn_W1"], f32))
    d["w3T"] = _tile_weight_T(np.asarray(inputs["L0_ffn_W3"], f32))
    w2t = np.ascontiguousarray(np.asarray(inputs["L0_ffn_W2"], f32).T)
    d["w2p1"] = np.ascontiguousarray(
        w2t.reshape(64, 64, 2, E).transpose(1, 2, 0, 3).reshape(64, 128, E))
    d["wdqT"] = _tile_weight_T(np.asarray(inputs["L0_attn_b_w_d_q"], f32))
    d["wdkvT"] = _tile_weight_T(np.asarray(inputs["L0_attn_b_w_d_kv"], f32))
    d["wuqT"] = _tile_weight_T(np.asarray(inputs["L0_attn_b_w_u_q"], f32))
    d["wuk"] = _tile_weight_N(np.asarray(inputs["L0_attn_b_w_u_k"], f32))
    d["wkrT"] = _tile_weight_T(np.asarray(inputs["L0_attn_b_w_kr"], f32))
    d["wqrT"] = _tile_weight_T(np.asarray(inputs["L0_attn_b_w_qr"], f32))
    wuv = np.asarray(inputs["L0_attn_b_w_u_v"], f32)
    d["wuv"] = np.ascontiguousarray(wuv.reshape(KC, 128, C))
    d["woTb"] = _tile_weight_T(np.asarray(inputs["L0_attn_b_w_o"], f32))
    d["g1"] = np.ascontiguousarray(np.asarray(inputs["L0_attn_b_g1"], f32).reshape(CCH, 128).T)
    d["g2"] = np.ascontiguousarray(np.asarray(inputs["L0_attn_b_g2"], f32).reshape(CCH, 128).T)
    d["g3"] = np.ascontiguousarray(np.asarray(inputs["L0_attn_b_g3"], f32).reshape(KC, 128).T)

    consts = np.zeros((128, 512), BF)
    consts[:, 0:128] = np.eye(128, dtype=BF)
    consts[:, 128:256] = np.ones((128, 128), BF)
    consts[:, 256:384] = _had(128).astype(BF)
    consts[:, 384:512] = np.kron(np.eye(2), _had(64)).astype(BF)
    d["consts"] = consts
    selh = np.zeros((128, 64), BF)
    for m in range(64):
        selh[m // 4, m] = 1
    d["selh"] = selh
    seln = np.zeros((128, 64), BF)
    for m in range(64):
        seln[m % 4, m] = 1
    d["seln"] = seln
    mask = np.zeros((128, 4), np.float32)
    for m in range(64):
        mask[m, m % 4] = 1
    d["mask64"] = mask
    sel = np.zeros((128, 64 * 128), BF)
    for r in range(64):
        sel[r, r * 128:(r + 1) * 128] = 1
    d["sel"] = sel
    return d


def rope_tables(core_id):
    s0 = (core_id % 2) * T
    pos = s0 + np.arange(T, dtype=np.float64)
    freqs = 1.0 / (10000.0 ** (np.arange(0, D, 2, dtype=np.float64) / D))
    ang = pos[None, :] * freqs[:, None]
    rt = np.empty((128, 2 * T), dtype=BF)
    c = np.cos(ang)
    s = np.sin(ang)
    rt[0::2, 0:T] = c
    rt[1::2, 0:T] = c
    rt[0::2, T:2 * T] = -s
    rt[1::2, T:2 * T] = s
    return rt


def kernel(**inputs):
    shared = host_prepare(inputs)
    x = np.asarray(inputs["L0_attn_a_hidden_states"], np.float32)
    in_maps = []
    for c in range(NCORES):
        b, half = c // 2, c % 2
        xs = np.ascontiguousarray(x[b, half * T:(half + 1) * T, :].T)
        m = dict(shared)
        m["x"] = np.ascontiguousarray(xs.reshape(KC, 128, T))
        m["rope"] = rope_tables(c)
        in_maps.append(m)
    nc = build_program(upto=UPTO)
    res = run_bass_kernel_spmd(nc, in_maps, core_ids=list(range(NCORES)))
    out = np.empty_like(x)
    for c in range(NCORES):
        b, half = c // 2, c % 2
        y = res.results[c]["y"].reshape(E, T)
        out[b, half * T:(half + 1) * T, :] = y.T
    return out


# version 03
# speedup vs baseline: 1.0444x; 1.0444x over previous
import math

import numpy as np
import ml_dtypes

import concourse.bass as bass
import concourse.mybir as mybir
from concourse.bass_utils import run_bass_kernel_spmd

F32 = mybir.dt.float32
BF16 = mybir.dt.bfloat16
AF = mybir.ActivationFunctionType
ALU = mybir.AluOpType
AX = mybir.AxisListType
BF = ml_dtypes.bfloat16

B, S, E, T = 4, 2048, 2048, 1024
KC = E // 128
H, D, NB = 16, 128, 4
C = 512
CCH = C // 128
FH = 8192
FHC = FH // 128
LN_EPS = 1.0e-5
RMS_EPS = float(np.finfo(np.float32).eps)
A_EPS = 1.0e-6
SM_SCALE = 1.0 / 12.0
FWHT_SCALE = 1.0 / math.sqrt(FH)
NCORES = 8
UPTO = 4
DBG = None

ENGS = ("tensor", "vector", "scalar", "sync", "gpsimd")

AR_XC = 0
AR_SQ1 = 61440
AR_SGQ = 16384
AR_MEM = 32768
AR_SEL = 40960
AR_SGS = 49152
AR_CFB = 57344
AR_FOLD = 16384
AR_HID = 0
SQ3 = 16384
AB_KR = 0
AB_CKV = 32768
AB_QR = 40960
AB_AQ = 57344
AB_SCR = 57344
AB_CQR = 61440
AB_ATT = 61440
AB_UB = 40960
AB_CQN = 0
XN_T1B = 0
XN_WUQ = 2048
XN_WUV = 0
XN_AWO = 8192
XN_ATT = 0
XN_OF = 2048


class Buf:
    __slots__ = ("name", "writer", "readers")

    def __init__(self, name):
        self.name = name
        self.writer = None
        self.readers = []


def _had(n):
    h = np.array([[1.0]], dtype=np.float64)
    while h.shape[0] < n:
        h = np.block([[h, h], [h, -h]])
    return h


def build_program(upto=UPTO):
    nc = bass.Bass("TRN2", target_bir_lowering=False)
    DT = nc.dram_tensor

    x_in = DT("x", [KC, 128, T], F32, kind="ExternalInput").ap()
    wq_t = DT("wqT", [KC, 128, KC * 128], F32, kind="ExternalInput").ap()
    wo_t = DT("woT", [KC, 128, KC * 128], F32, kind="ExternalInput").ap()
    mem_t = DT("mem", [128, NB * H * 128], F32, kind="ExternalInput").ap()
    mnblk = DT("mnblk", [128, KC * 64], F32, kind="ExternalInput").ap()
    mnorm_n = DT("mnorm_n", [128, H * D], F32, kind="ExternalInput").ap()
    bw_in = DT("bw", [128, NB], F32, kind="ExternalInput").ap()
    w1_t = DT("w1T", [FHC, 128, KC * 128], F32, kind="ExternalInput").ap()
    w3_t = DT("w3T", [FHC, 128, KC * 128], F32, kind="ExternalInput").ap()
    w2p1 = DT("w2p1", [64, 128, E], F32, kind="ExternalInput").ap()
    wdq_t = DT("wdqT", [CCH, 128, KC * 128], F32, kind="ExternalInput").ap()
    wdkv_t = DT("wdkvT", [CCH, 128, KC * 128], F32, kind="ExternalInput").ap()
    wuq_t = DT("wuqT", [KC, 128, CCH * 128], F32, kind="ExternalInput").ap()
    wuk_n = DT("wuk", [CCH, 128, KC * 128], F32, kind="ExternalInput").ap()
    wkr_t = DT("wkrT", [KC, 128, KC * 128], F32, kind="ExternalInput").ap()
    wqr_t = DT("wqrT", [KC, 128, CCH * 128], F32, kind="ExternalInput").ap()
    wuv_n = DT("wuv", [KC, 128, C], F32, kind="ExternalInput").ap()
    wot_b = DT("woTb", [KC, 128, KC * 128], F32, kind="ExternalInput").ap()
    g1_in = DT("g1", [128, CCH], F32, kind="ExternalInput").ap()
    g2_in = DT("g2", [128, CCH], F32, kind="ExternalInput").ap()
    g3_in = DT("g3", [128, KC], F32, kind="ExternalInput").ap()
    consts_in = DT("consts", [128, 512], BF16, kind="ExternalInput").ap()
    selh_in = DT("selh", [128, 64], BF16, kind="ExternalInput").ap()
    seln_in = DT("seln", [128, 64], BF16, kind="ExternalInput").ap()
    mask64_in = DT("mask64", [128, 4], F32, kind="ExternalInput").ap()
    sel_in = DT("sel", [128, 64 * 128], BF16, kind="ExternalInput").ap()
    rope_in = DT("rope", [128, 2 * T], BF16, kind="ExternalInput").ap()

    y_out = DT("y", [KC, 128, T], F32, kind="ExternalOutput").ap()

    x_hbm = [DT(f"xs{i}", [KC, 128, T], F32, kind="Internal").ap() for i in range(2)]
    w2mid = DT("w2mid", [8192, E], BF16, kind="Internal").ap()
    w2fin = DT("w2fin", [KC, 128, FHC * 128], BF16, kind="Internal").ap()
    AGN = CCH * 128 * T + KC * 128 * T
    ag_in = DT("ag_in", [1, AGN], BF16, kind="Internal").ap()
    ag_out = DT("ag_out", [2, AGN], BF16, kind="Internal").ap()
    fence_in = DT("fence_in", [1, 512], BF16, kind="Internal").ap()
    fence_out = DT("fence_out", [1, 512], BF16, kind="Internal").ap()

    es = []

    def sbt(name, shape, dtype):
        g = nc.sbuf_tensor(name, shape, dtype)
        t = g.__enter__()
        es.append(g)
        return t

    sem_h = {}
    for e in ENGS:
        g = nc.semaphore(f"sem_{e}")
        sem_h[e] = g.__enter__()
        es.append(g)

    xn = sbt("xn", [128, KC * T], BF16)
    arena = sbt("arena", [128, 65536], BF16)
    wr = sbt("wr", [128, 4 * 2048], BF16)
    wr2 = sbt("wr2", [128, 2 * 512], BF16)
    fA = sbt("fA", [128, T], F32)
    fB = sbt("fB", [128, T], F32)
    xr = sbt("xr", [128, 2 * T], F32)
    sc1 = sbt("sc1", [128, T], F32)
    consts = sbt("consts_sb", [128, 512], BF16)
    selh = sbt("selh_sb", [128, 64], BF16)
    seln = sbt("seln_sb", [128, 64], BF16)
    mask64 = sbt("mask64_sb", [128, 4], F32)
    g_sb = sbt("g_sb", [128, 2 * CCH + KC], F32)
    rope_sb = sbt("rope_sb", [128, 2 * T], BF16)
    mnblk_sb = sbt("mnblk_sb", [128, KC * 64], BF16)
    bw_sb = sbt("bw_sb", [128, 16], F32)
    wact = sbt("wact_sb", [64, 4], F32)
    mrow = sbt("mrow", [128, 8], F32)

    IDENT = consts[:, 0:128]
    ONES = consts[:, 128:256]
    H128 = consts[:, 256:384]
    I2H64 = consts[:, 384:512]
    G1c = lambda cc: g_sb[:, cc:cc + 1]
    G2c = lambda cc: g_sb[:, CCH + cc:CCH + cc + 1]
    G3c = lambda k: g_sb[:, 2 * CCH + k:2 * CCH + k + 1]
    COSA = rope_sb[:, 0:T]
    SINB = rope_sb[:, T:2 * T]

    PP = []
    for i in range(4):
        g = nc.psum_tensor(f"pp{i}", [128, 1024], F32)
        PP.append(g.__enter__())
        es.append(g)

    def PPl(i):
        return PP[i][:, 0:512]

    def PPh(i):
        return PP[i][:, 512:1024]

    blkg = nc.Block()
    blk = blkg.__enter__()

    cnt = {e: 0 for e in ENGS}
    lastwait = {e: {f: 0 for f in ENGS} for e in ENGS}
    trusted = {"tensor", "vector", "scalar"}

    def op(eng, emit, reads=(), writes=(), dma=False):
        waits = {}

        def need(tok):
            if tok is None:
                return
            s, v = tok
            if s == eng and eng in trusted:
                return
            if s == eng and v <= cnt[eng]:
                pass
            if lastwait[eng].get(s, 0) >= v:
                return
            if cnt[s] < v:
                raise RuntimeError(f"{eng} waits future {s}>={v} (at {cnt[s]})")
            waits[s] = max(waits.get(s, 0), v)

        for b in reads:
            need(b.writer)
        for b in writes:
            need(b.writer)
            for t in b.readers:
                need(t)

        inc = 16 if dma else 1
        tok = (eng, cnt[eng] + inc)

        def body(h):
            for s, v in sorted(waits.items()):
                h.wait_ge(sem_h[s], v)
                lastwait[eng][s] = max(lastwait[eng][s], v)
            last = emit(h)
            last.then_inc(sem_h[eng], inc)

        getattr(blk, eng)(body)
        cnt[eng] += inc
        for b in reads:
            b.readers.append(tok)
        for b in writes:
            b.writer = tok
            b.readers = []
        return tok

    def claim(bufs):
        op("vector", lambda h: h.tensor_copy(mrow[:, 7:8], mrow[:, 6:7]),
           reads=[], writes=list(bufs))

    b_small = Buf("small")
    b_coefb = Buf("coefb")
    b_wact = Buf("wact")
    b_fA = Buf("fA")
    b_fB = Buf("fB")
    b_sc1 = Buf("sc1")
    b_mrow = Buf("mrow")
    b_bw = Buf("bw")
    b_coef = Buf("coef")
    b_pp = [Buf(f"pp{i}") for i in range(4)]
    b_wr = [Buf(f"wr{i}") for i in range(4)]
    b_wr2 = [Buf(f"wr2{i}") for i in range(2)]
    b_xr = [Buf("xr0"), Buf("xr1")]
    b_xn = Buf("xn")
    b_mem = Buf("mem")
    b_sel = Buf("sel")
    b_mnblk = Buf("mnblk")
    b_xc = Buf("xc")
    b_sq = [Buf("sq0"), Buf("sq1")]
    b_sgq = Buf("sgq")
    b_sgs = [Buf(f"sgs{i}") for i in range(6)]
    b_bst = [Buf("bst0"), Buf("bst1")]
    b_hid = [Buf(f"hid{i}") for i in range(FHC)]
    b_w2slot = [Buf("w2s0"), Buf("w2s1")]
    b_xin = Buf("xin")
    b_x0 = Buf("xhbm0")
    b_x1 = Buf("xhbm1")
    b_y = Buf("yout")
    b_w2mid = Buf("w2mid")
    b_w2fin = Buf("w2fin")
    b_fold = [Buf(f"fold{i}") for i in range(4)]
    b_fold2 = [Buf(f"fold2_{i}") for i in range(8)]
    AB = {n: Buf(n) for n in ("kr", "ckv", "qr", "aq", "cqr", "cqn", "t1b0", "t1b1",
                              "att", "ub", "wuv", "awo", "attT", "of", "agin",
                              "agout", "ckvt", "stg", "ckvr", "fin", "fence", "wuqr")}

    for dst, src in ((consts[:], consts_in), (selh[:], selh_in), (seln[:], seln_in),
                     (mask64[:], mask64_in), (g_sb[:, 0:CCH], g1_in),
                     (g_sb[:, CCH:2 * CCH], g2_in),
                     (g_sb[:, 2 * CCH:2 * CCH + KC], g3_in), (rope_sb[:], rope_in)):
        op("sync", lambda h, dst=dst, src=src: h.dma_start(out=dst, in_=src),
           writes=[b_small], dma=True)
    op("sync", lambda h: h.dma_start(out=bw_sb[:, 0:NB], in_=bw_in), writes=[b_bw], dma=True)
    op("sync", lambda h: h.dma_start(out=xr[:, 0:T], in_=mnorm_n[:, 0:T]), writes=[b_xr[0]], dma=True)
    op("sync", lambda h: h.dma_start(out=xr[:, T:2 * T], in_=mnorm_n[:, T:2 * T]), writes=[b_xr[1]], dma=True)
    op("gpsimd", lambda h: h.dma_start(out=mnblk_sb[:], in_=mnblk), writes=[b_mnblk], dma=True)
    op("gpsimd", lambda h: h.dma_start(out=arena[:, AR_MEM:AR_MEM + 8192], in_=mem_t), writes=[b_mem], dma=True)
    op("gpsimd", lambda h: h.dma_start(out=arena[:, AR_SEL:AR_SEL + 8192], in_=sel_in), writes=[b_sel], dma=True)

    def p_bw1(v):
        v.reduce_sum(mrow[:, 0:1], xr[:, 0:T], AX.X)
        v.reduce_sum(mrow[:, 1:2], xr[:, T:2 * T], AX.X)
        v.tensor_copy(mrow[:, 6:7], bw_sb[:, 0:1])
        v.tensor_copy(mrow[:, 6:7], bw_sb[:, 0:1])
        v.tensor_tensor(mrow[:, 0:1], mrow[:, 0:1], mrow[:, 1:2], ALU.add)
        v.tensor_copy(mrow[:, 6:7], bw_sb[:, 0:1])
        v.tensor_copy(mrow[:, 6:7], bw_sb[:, 0:1])
        v.tensor_scalar(mrow[:, 2:3], mrow[:, 0:1], A_EPS, None, ALU.is_ge)
        v.reduce_max(mrow[:, 3:4], bw_sb[:, 0:NB], AX.X)
        v.tensor_copy(mrow[:, 6:7], mrow[:, 3:4])
        v.tensor_copy(mrow[:, 6:7], mrow[:, 3:4])
        return v.tensor_scalar(bw_sb[:, 4:4 + NB], bw_sb[:, 0:NB], mrow[:, 3:4], None, ALU.subtract)
    op("vector", p_bw1, reads=[b_xr[0], b_xr[1], b_bw], writes=[b_mrow, b_bw])
    op("scalar", lambda h: h.activation(bw_sb[:, 8:8 + NB], bw_sb[:, 4:4 + NB], AF.Exp),
       reads=[b_bw], writes=[b_bw])

    def p_bw3(v):
        v.reduce_sum(mrow[:, 4:5], bw_sb[:, 8:8 + NB], AX.X)
        v.tensor_copy(mrow[:, 6:7], bw_sb[:, 0:1])
        v.tensor_copy(mrow[:, 6:7], bw_sb[:, 0:1])
        v.reciprocal(mrow[:, 5:6], mrow[:, 4:5])
        v.tensor_copy(mrow[:, 6:7], bw_sb[:, 0:1])
        v.tensor_copy(mrow[:, 6:7], bw_sb[:, 0:1])
        v.tensor_scalar(bw_sb[:, 12:12 + NB], bw_sb[:, 8:8 + NB], mrow[:, 5:6], None, ALU.mult)
        v.tensor_copy(mrow[:, 6:7], bw_sb[:, 0:1])
        v.tensor_copy(mrow[:, 6:7], bw_sb[:, 0:1])
        v.tensor_copy(arena[0:16, AR_CFB:AR_CFB + 4], bw_sb[0:16, 12:16])
        return v.tensor_copy(arena[0:4, AR_CFB + 8:AR_CFB + 9], mrow[0:4, 2:3])
    op("vector", p_bw3, reads=[b_bw, b_mrow], writes=[b_bw, b_mrow, b_coefb])

    def p_bw4(t):
        t.matmul(PP[2][0:64, 0:4], selh[0:16, 0:64], arena[0:16, AR_CFB:AR_CFB + 4],
                 start=True, stop=True)
        return t.matmul(PP[2][0:64, 8:9], seln[0:4, 0:64], arena[0:4, AR_CFB + 8:AR_CFB + 9],
                        start=True, stop=True)
    op("tensor", p_bw4, reads=[b_small, b_coefb], writes=[b_pp[2]])

    def p_bw5(v):
        v.tensor_tensor(wact[:, 0:4], PP[2][0:64, 0:4], mask64[0:64, 0:4], ALU.mult)
        v.tensor_copy(mrow[:, 6:7], bw_sb[:, 0:1])
        v.tensor_copy(mrow[:, 6:7], bw_sb[:, 0:1])
        v.reduce_sum(wact[:, 2:3], wact[:, 0:4], AX.X)
        v.tensor_copy(mrow[:, 6:7], bw_sb[:, 0:1])
        v.tensor_copy(mrow[:, 6:7], bw_sb[:, 0:1])
        v.tensor_tensor(wact[:, 1:2], wact[:, 2:3], PP[2][0:64, 8:9], ALU.mult)
        v.tensor_copy(arena[0:64, AR_CFB + 1024:AR_CFB + 1028], PP[2][0:64, 0:4])
        v.tensor_copy(arena[0:64, AR_CFB + 1028:AR_CFB + 1029], PP[2][0:64, 8:9])
        v.tensor_copy(arena[0:16, AR_CFB + 1032:AR_CFB + 1048], bw_sb[0:16, 0:16])
        return v.tensor_copy(arena[0:8, AR_CFB + 1048:AR_CFB + 1056], mrow[0:8, 0:8])
    op("vector", p_bw5, reads=[b_pp[2], b_small], writes=[b_wact, b_coefb])

    def w2_fold():
        FG = 8

        def fold_load1(g2):
            sl = g2 % 4
            op("gpsimd",
               lambda h, g2=g2, sl=sl: h.dma_start(
                   out=wr[:, sl * 2048:sl * 2048 + 2048], in_=w2p1[g2]),
               writes=[b_wr[sl]], dma=True)

        for g2 in range(4):
            fold_load1(g2)
        for g2 in range(64):
            sl = g2 % 4
            i = g2 % 2
            stg = b_fold[g2 % 4]
            soff = AR_FOLD + (g2 % 4) * 2048

            def p_m1(t, sl=sl, i=i):
                t.matmul(PPl(i), I2H64, wr[:, sl * 2048:sl * 2048 + 512], start=True, stop=True)
                return t.matmul(PPh(i), I2H64, wr[:, sl * 2048 + 512:sl * 2048 + 1024],
                                start=True, stop=True)

            def p_m2(t, sl=sl, i=i):
                t.matmul(PPl(i), I2H64, wr[:, sl * 2048 + 1024:sl * 2048 + 1536], start=True, stop=True)
                return t.matmul(PPh(i), I2H64, wr[:, sl * 2048 + 1536:sl * 2048 + 2048],
                                start=True, stop=True)
            op("tensor", p_m1, reads=[b_small, b_wr[sl]], writes=[b_pp[i]])
            op("vector", lambda h, i=i, soff=soff: h.tensor_copy(arena[:, soff:soff + 1024], PP[i][:]),
               reads=[b_pp[i]], writes=[stg])
            op("tensor", p_m2, reads=[b_small, b_wr[sl]], writes=[b_pp[i]])
            op("vector", lambda h, i=i, soff=soff: h.tensor_copy(arena[:, soff + 1024:soff + 2048], PP[i][:]),
               reads=[b_pp[i]], writes=[stg])
            op("sync",
               lambda h, g2=g2, soff=soff: h.dma_start(
                   out=w2mid[128 * g2:128 * g2 + 128, :], in_=arena[:, soff:soff + 2048]),
               reads=[stg], writes=[b_w2mid], dma=True)
            if g2 + 4 < 64:
                fold_load1(g2 + 4)

        def fold_load2(cpos):
            sl = cpos % 4
            sap = bass.AP(w2mid.tensor, cpos * 2048, [[64 * 2048, 128], [1, 2048]])
            op("gpsimd",
               lambda h, sap=sap, sl=sl: h.dma_start(out=wr[:, sl * 2048:sl * 2048 + 2048], in_=sap),
               reads=[b_w2mid], writes=[b_wr[sl]], dma=True)

        for cpos in range(4):
            fold_load2(cpos)
        for cpos in range(64):
            sl = cpos % 4
            i = 2 + (cpos % 2)
            j = cpos % FG
            stg = b_fold2[j]

            def q_m1(t, sl=sl, i=i):
                t.matmul(PPl(i), H128, wr[:, sl * 2048:sl * 2048 + 512], start=True, stop=True)
                return t.matmul(PPh(i), H128, wr[:, sl * 2048 + 512:sl * 2048 + 1024],
                                start=True, stop=True)

            def q_m2(t, sl=sl, i=i):
                t.matmul(PPl(i), H128, wr[:, sl * 2048 + 1024:sl * 2048 + 1536], start=True, stop=True)
                return t.matmul(PPh(i), H128, wr[:, sl * 2048 + 1536:sl * 2048 + 2048],
                                start=True, stop=True)
            op("tensor", q_m1, reads=[b_small, b_wr[sl]], writes=[b_pp[i]])
            op("vector", lambda h, i=i, j=j: h.tensor_scalar(
                bass.AP(arena, AR_FOLD + j * 128, [[65536, 128], [1024, 8], [1, 128]]),
                PP[i][:], FWHT_SCALE, None, ALU.mult),
               reads=[b_pp[i]], writes=[stg])
            op("tensor", q_m2, reads=[b_small, b_wr[sl]], writes=[b_pp[i]])
            op("vector", lambda h, i=i, j=j: h.tensor_scalar(
                bass.AP(arena, AR_FOLD + 8192 + j * 128, [[65536, 128], [1024, 8], [1, 128]]),
                PP[i][:], FWHT_SCALE, None, ALU.mult),
               reads=[b_pp[i]], writes=[stg])
            if cpos + 4 < 64:
                fold_load2(cpos + 4)
            if cpos % FG == FG - 1:
                grp = cpos // FG
                rd = [b_fold2[jj] for jj in range(FG)]
                for eo in range(KC):
                    op("sync",
                       lambda h, eo=eo, grp=grp: h.dma_start(
                           out=w2fin[eo][:, grp * FG * 128:grp * FG * 128 + FG * 128],
                           in_=arena[:, AR_FOLD + eo * 1024:AR_FOLD + (eo + 1) * 1024]),
                       reads=rd, writes=[b_w2fin], dma=True)

    if upto >= 2:
        w2_fold()

    def ln_rows_finish(eps, n_feat):
        def p_r(v):
            v.tensor_scalar(fA[:], PP[2][:], 1.0 / n_feat, None, ALU.mult)
            v.tensor_scalar(fB[:], PP[3][:], 1.0 / n_feat, None, ALU.mult)
            v.tensor_tensor(sc1[:], fA[:], fA[:], ALU.mult)
            v.tensor_tensor(fB[:], fB[:], sc1[:], ALU.subtract)
            v.tensor_scalar(fB[:], fB[:], eps, None, ALU.add)
            return v.reciprocal(fB[:], fB[:])
        op("vector", p_r, reads=[b_pp[2], b_pp[3]], writes=[b_fA, b_fB, b_sc1])
        op("scalar", lambda h: h.activation(sc1[:], fB[:], AF.Sqrt),
           reads=[b_fB], writes=[b_sc1])

        def p_b(v):
            v.tensor_tensor(fB[:], fA[:], sc1[:], ALU.mult)
            v.tensor_scalar(fB[:], fB[:], -1.0, None, ALU.mult)
            return v.tensor_copy(fA[:], sc1[:])
        op("vector", p_b, reads=[b_fA, b_sc1], writes=[b_fA, b_fB])

    def ln_apply(xc_off):
        for k in range(KC):
            slot = sc1[:] if k % 2 == 0 else xr[:, 0:T]
            bslot = b_sc1 if k % 2 == 0 else b_xr[0]
            op("scalar",
               lambda h, k=k, slot=slot: h.activation(
                   slot, arena[:, xc_off + k * T:xc_off + (k + 1) * T], AF.Copy),
               reads=[b_xc], writes=[bslot])

            def p_a(v, k=k, slot=slot):
                v.tensor_tensor(slot, slot, fA[:], ALU.mult)
                return v.tensor_tensor(xn[:, k * T:(k + 1) * T], slot, fB[:], ALU.add)
            op("vector", p_a, reads=[bslot, b_fA, b_fB], writes=[bslot, b_xn])

    def ln_stats_chunk(k, n_ch, xc_off, sq_off):
        sq = b_sq[k % 2]
        op("scalar",
           lambda h, k=k: h.activation(
               arena[:, sq_off + (k % 2) * T:sq_off + (k % 2) * T + T],
               arena[:, xc_off + k * T:xc_off + (k + 1) * T], AF.Square),
           reads=[b_xc], writes=[sq])

        def p_m(t, k=k):
            t.matmul(PPl(2), ONES, arena[:, xc_off + k * T:xc_off + k * T + 512],
                     start=(k == 0), stop=(k == n_ch - 1))
            t.matmul(PPh(2), ONES, arena[:, xc_off + k * T + 512:xc_off + k * T + 1024],
                     start=(k == 0), stop=(k == n_ch - 1))
            t.matmul(PPl(3), ONES, arena[:, sq_off + (k % 2) * T:sq_off + (k % 2) * T + 512],
                     start=(k == 0), stop=(k == n_ch - 1))
            return t.matmul(PPh(3), ONES,
                            arena[:, sq_off + (k % 2) * T + 512:sq_off + (k % 2) * T + 1024],
                            start=(k == 0), stop=(k == n_ch - 1))
        op("tensor", p_m, reads=[b_xc, sq, b_small], writes=[b_pp[2], b_pp[3]])

    def ln_classic(src, src_buf, xc_off, sq_off):
        for k in range(KC):
            xrb = b_xr[k % 2]
            op("sync",
               lambda h, k=k: h.dma_start(out=xr[:, (k % 2) * T:(k % 2) * T + T], in_=src[k]),
               reads=[src_buf], writes=[xrb], dma=True)
            op("vector",
               lambda h, k=k: h.tensor_copy(
                   arena[:, xc_off + k * T:xc_off + (k + 1) * T],
                   xr[:, (k % 2) * T:(k % 2) * T + T]),
               reads=[xrb], writes=[b_xc])
            ln_stats_chunk(k, KC, xc_off, sq_off)
        ln_rows_finish(LN_EPS, E)
        ln_apply(xc_off)

    def projection(wsrc, n_eo, nkw, rhs_fn, rhs_bufs, evac_fn,
                   ring=None, ring_bufs=None, nslots=4, slot_elems=2048,
                   qeng="gpsimd", wsrc_buf=None, tile_elems=None):
        if ring is None:
            ring, ring_bufs = wr, b_wr
        if tile_elems is None:
            tile_elems = nkw * 128

        def emit_load(eo):
            sl = eo % nslots
            op(qeng,
               lambda h, eo=eo, sl=sl: h.dma_start(
                   out=ring[:, sl * slot_elems:sl * slot_elems + tile_elems], in_=wsrc[eo]),
               reads=([wsrc_buf] if wsrc_buf is not None else []),
               writes=[ring_bufs[sl]], dma=True)

        for eo in range(min(nslots, n_eo)):
            emit_load(eo)
        for eo in range(n_eo):
            sl = eo % nslots
            i = eo % 2

            def p_m(t, eo=eo, sl=sl, i=i):
                for k in range(nkw):
                    lhs = ring[:, sl * slot_elems + k * 128:sl * slot_elems + (k + 1) * 128]
                    t.matmul(PPl(i), lhs, rhs_fn(k, 0), start=(k == 0), stop=(k == nkw - 1))
                    mi = t.matmul(PPh(i), lhs, rhs_fn(k, 1), start=(k == 0), stop=(k == nkw - 1))
                return mi
            op("tensor", p_m, reads=[ring_bufs[sl]] + list(rhs_bufs), writes=[b_pp[i]])
            if eo + nslots < n_eo:
                emit_load(eo + nslots)
            evac_fn(eo, i)

    def xn_rhs(k, hh):
        return xn[:, k * T + hh * 512:k * T + hh * 512 + 512]

    def proj_residual(wsrc, xold, xold_buf, xnew, xnew_buf, fuse_ln,
                      rhs_fn=None, rhs_bufs=None, qeng="gpsimd", wsrc_buf=None,
                      ring=None, ring_bufs=None, nslots=4, slot_elems=2048,
                      nkw=KC, tile_elems=None):
        if rhs_fn is None:
            rhs_fn, rhs_bufs = xn_rhs, [b_xn]

        def ev(eo, i):
            xrb = b_xr[eo % 2]
            op("sync",
               lambda h, eo=eo: h.dma_start(out=xr[:, (eo % 2) * T:(eo % 2) * T + T],
                                            in_=xold[eo]),
               reads=[xold_buf], writes=[xrb], dma=True)
            op("vector",
               lambda h, eo=eo, i=i: h.tensor_tensor(
                   xr[:, (eo % 2) * T:(eo % 2) * T + T],
                   xr[:, (eo % 2) * T:(eo % 2) * T + T], PP[i][:], ALU.add),
               reads=[xrb, b_pp[i]], writes=[xrb])
            op("sync",
               lambda h, eo=eo: h.dma_start(out=xnew[eo],
                                            in_=xr[:, (eo % 2) * T:(eo % 2) * T + T]),
               reads=[xrb], writes=[xnew_buf], dma=True)
            if fuse_ln:
                op("vector",
                   lambda h, eo=eo: h.tensor_copy(
                       arena[:, AR_XC + eo * T:AR_XC + (eo + 1) * T],
                       xr[:, (eo % 2) * T:(eo % 2) * T + T]),
                   reads=[xrb], writes=[b_xc])
                ln_stats_chunk(eo, KC, AR_XC, AR_SQ1)
        projection(wsrc, KC, nkw, rhs_fn, rhs_bufs, ev, ring=ring, ring_bufs=ring_bufs,
                   nslots=nslots, slot_elems=slot_elems, qeng=qeng, wsrc_buf=wsrc_buf,
                   tile_elems=tile_elems)
        if fuse_ln:
            ln_rows_finish(LN_EPS, E)
            ln_apply(AR_XC)

    def attn_a():
        ln_classic(x_in, b_xin, AR_XC, AR_SQ1)

        def ev_sgq(eo, i):
            op("vector",
               lambda h, i=i: h.tensor_scalar(sc1[:], PP[i][:], 0.0, None, ALU.min),
               reads=[b_pp[i]], writes=[b_sc1])
            op("scalar", lambda h: h.activation(fA[:], sc1[:], AF.Exp),
               reads=[b_sc1], writes=[b_fA])
            ew = [b_sgq] + (b_fold + b_fold2 if eo == 0 else [])
            op("vector",
               lambda h, eo=eo, i=i: h.scalar_tensor_tensor(
                   arena[:, AR_SGQ + eo * T:AR_SGQ + (eo + 1) * T],
                   PP[i][:], 0.0, fA[:], ALU.max, ALU.add),
               reads=[b_pp[i], b_fA], writes=ew)
        projection(wq_t, KC, KC, xn_rhs, [b_xn], ev_sgq)

        def p_norm(t):
            for k in range(KC):
                t.matmul(PP[2][0:64, 0:512], mnblk_sb[:, k * 64:(k + 1) * 64],
                         arena[:, AR_SGQ + k * T:AR_SGQ + k * T + 512],
                         start=(k == 0), stop=(k == KC - 1))
                mi = t.matmul(PP[2][0:64, 512:1024], mnblk_sb[:, k * 64:(k + 1) * 64],
                              arena[:, AR_SGQ + k * T + 512:AR_SGQ + k * T + 1024],
                              start=(k == 0), stop=(k == KC - 1))
            return mi
        op("tensor", p_norm, reads=[b_sgq, b_mnblk], writes=[b_pp[2]])

        def p_coef(v):
            v.tensor_scalar(fB[0:64, :], PP[2][0:64, :], A_EPS, None, ALU.max)
            v.reciprocal(fB[0:64, :], fB[0:64, :])
            v.tensor_scalar(fB[0:64, :], fB[0:64, :], wact[0:64, 1:2], None, ALU.mult)
            return v.tensor_copy(arena[0:64, AR_CFB:AR_CFB + T], fB[0:64, :])
        op("vector", p_coef, reads=[b_pp[2], b_wact], writes=[b_fB, b_coefb])

        for h_ in range(H):
            pa = h_ % 2
            for n_ in range(NB):
                j = h_ * NB + n_
                pb = 2 + (j % 2)
                r = 4 * h_ + n_

                def p_bm(t, r=r, pb=pb):
                    t.matmul(PPl(pb), arena[0:64, AR_SEL + r * 128:AR_SEL + (r + 1) * 128],
                             arena[0:64, AR_CFB:AR_CFB + 512], start=True, stop=True)
                    return t.matmul(PPh(pb), arena[0:64, AR_SEL + r * 128:AR_SEL + (r + 1) * 128],
                                    arena[0:64, AR_CFB + 512:AR_CFB + 1024], start=True, stop=True)
                op("tensor", p_bm, reads=[b_sel, b_coefb], writes=[b_pp[pb]])
                op("scalar",
                   lambda h, j=j, pb=pb: h.activation(
                       arena[:, AR_SGS + (6 + j % 2) * T:AR_SGS + (6 + j % 2) * T + T],
                       PP[pb][:], AF.Copy),
                   reads=[b_pp[pb]], writes=[b_bst[j % 2]])
                op("vector",
                   lambda h, j=j, h_=h_: h.tensor_tensor(
                       arena[:, AR_SGS + (j % 6) * T:AR_SGS + (j % 6) * T + T],
                       arena[:, AR_SGQ + h_ * T:AR_SGQ + (h_ + 1) * T],
                       arena[:, AR_SGS + (6 + j % 2) * T:AR_SGS + (6 + j % 2) * T + T],
                       ALU.mult),
                   reads=[b_sgq, b_bst[j % 2]], writes=[b_sgs[j % 6]])

                def p_am(t, j=j, n_=n_, h_=h_, pa=pa):
                    ms = arena[:, AR_MEM + (n_ * H + h_) * 128:AR_MEM + (n_ * H + h_ + 1) * 128]
                    t.matmul(PPl(pa), ms,
                             arena[:, AR_SGS + (j % 6) * T:AR_SGS + (j % 6) * T + 512],
                             start=(n_ == 0), stop=(n_ == NB - 1))
                    return t.matmul(PPh(pa), ms,
                                    arena[:, AR_SGS + (j % 6) * T + 512:AR_SGS + (j % 6) * T + 1024],
                                    start=(n_ == 0), stop=(n_ == NB - 1))
                op("tensor", p_am, reads=[b_mem, b_sgs[j % 6]], writes=[b_pp[pa]])
            op("vector",
               lambda h, h_=h_, pa=pa: h.tensor_copy(xn[:, h_ * T:(h_ + 1) * T], PP[pa][:]),
               reads=[b_pp[pa]], writes=[b_xn])

        proj_residual(wo_t, x_in, b_xin, x_hbm[0], b_x0, fuse_ln=True)

    def ffn(xold, xold_buf, xnew, xnew_buf, final, arena_claim):
        claim(arena_claim + b_hid)
        for hc in range(FHC):
            sl1 = (2 * hc) % 4
            sl3 = (2 * hc + 1) % 4
            op("gpsimd",
               lambda h, hc=hc, sl1=sl1: h.dma_start(
                   out=wr[:, sl1 * 2048:sl1 * 2048 + 2048], in_=w1_t[hc]),
               writes=[b_wr[sl1]], dma=True)
            op("gpsimd",
               lambda h, hc=hc, sl3=sl3: h.dma_start(
                   out=wr[:, sl3 * 2048:sl3 * 2048 + 2048], in_=w3_t[hc]),
               writes=[b_wr[sl3]], dma=True)
            pg = hc % 2
            pl = 2 + hc % 2

            def p_g(t, sl1=sl1, pg=pg):
                for k in range(KC):
                    lhs = wr[:, sl1 * 2048 + k * 128:sl1 * 2048 + (k + 1) * 128]
                    t.matmul(PPl(pg), lhs, xn_rhs(k, 0), start=(k == 0), stop=(k == KC - 1))
                    mi = t.matmul(PPh(pg), lhs, xn_rhs(k, 1), start=(k == 0), stop=(k == KC - 1))
                return mi
            op("tensor", p_g, reads=[b_wr[sl1], b_xn], writes=[b_pp[pg]])

            def p_l(t, sl3=sl3, pl=pl):
                for k in range(KC):
                    lhs = wr[:, sl3 * 2048 + k * 128:sl3 * 2048 + (k + 1) * 128]
                    t.matmul(PPl(pl), lhs, xn_rhs(k, 0), start=(k == 0), stop=(k == KC - 1))
                    mi = t.matmul(PPh(pl), lhs, xn_rhs(k, 1), start=(k == 0), stop=(k == KC - 1))
                return mi
            op("tensor", p_l, reads=[b_wr[sl3], b_xn], writes=[b_pp[pl]])
            op("scalar",
               lambda h, pg=pg: h.activation(sc1[:], PP[pg][:], AF.Silu),
               reads=[b_pp[pg]], writes=[b_sc1])
            op("vector",
               lambda h, hc=hc, pl=pl: h.tensor_tensor(
                   arena[:, AR_HID + hc * T:AR_HID + (hc + 1) * T], sc1[:], PP[pl][:], ALU.mult),
               reads=[b_sc1, b_pp[pl]], writes=[b_hid[hc]])

        claim([b_xn, b_w2slot[0], b_w2slot[1]])

        def hid_rhs(k, hh):
            return arena[:, AR_HID + k * T + hh * 512:AR_HID + k * T + hh * 512 + 512]
        proj_residual(w2fin, xold, xold_buf, xnew, xnew_buf, fuse_ln=False,
                      rhs_fn=hid_rhs, rhs_bufs=b_hid, qeng="sync", wsrc_buf=b_w2fin,
                      ring=xn, ring_bufs=b_w2slot, nslots=2, slot_elems=8192,
                      nkw=FHC, tile_elems=FHC * 128)

    def rms_rows_finish(n_feat):
        def p_r(v):
            v.tensor_scalar(fA[:], PP[2][:], 1.0 / n_feat, None, ALU.mult)
            v.tensor_scalar(fA[:], fA[:], RMS_EPS, None, ALU.add)
            return v.reciprocal(fA[:], fA[:])
        op("vector", p_r, reads=[b_pp[2]], writes=[b_fA])
        op("scalar", lambda h: h.activation(fA[:], fA[:], AF.Sqrt),
           reads=[b_fA], writes=[b_fA])

    def mk_cproj_evac(raw_off, raw_buf, n_ch):
        def ev(eo, i):
            op("vector",
               lambda h, eo=eo, i=i: h.tensor_copy(
                   arena[:, raw_off + eo * T:raw_off + (eo + 1) * T], PP[i][:]),
               reads=[b_pp[i]], writes=[raw_buf])
            op("scalar",
               lambda h, eo=eo, i=i: h.activation(
                   arena[:, AB_SCR + (eo % 2) * T:AB_SCR + (eo % 2) * T + T],
                   PP[i][:], AF.Square),
               reads=[b_pp[i]], writes=[b_sq[eo % 2]])

            def p_m(t, eo=eo):
                t.matmul(PPl(2), ONES,
                         arena[:, AB_SCR + (eo % 2) * T:AB_SCR + (eo % 2) * T + 512],
                         start=(eo == 0), stop=(eo == n_ch - 1))
                return t.matmul(PPh(2), ONES,
                                arena[:, AB_SCR + (eo % 2) * T + 512:AB_SCR + (eo % 2) * T + 1024],
                                start=(eo == 0), stop=(eo == n_ch - 1))
            op("tensor", p_m, reads=[b_sq[eo % 2], b_small], writes=[b_pp[2]])
        return ev

    def rms_apply(n_ch, raw_off, raw_buf, out_ap_fn, out_buf, g_col):
        for cc in range(n_ch):
            op("scalar",
               lambda h, cc=cc: h.activation(
                   sc1[:], arena[:, raw_off + cc * T:raw_off + (cc + 1) * T], AF.Copy),
               reads=[raw_buf], writes=[b_sc1])
            op("vector",
               lambda h: h.tensor_tensor(sc1[:], sc1[:], fA[:], ALU.mult),
               reads=[b_sc1, b_fA], writes=[b_sc1])
            op("scalar",
               lambda h, cc=cc: h.activation(out_ap_fn(cc), sc1[:], AF.Copy, scale=g_col(cc)),
               reads=[b_sc1, b_small], writes=[out_buf])

    def mk_rope_evac(dst_fn, dst_buf, extra_first=()):
        def ev(eo, i):
            op("scalar",
               lambda h, i=i: h.activation(arena[:, AB_SCR:AB_SCR + T], PP[i][:], AF.Copy),
               reads=[b_pp[i]], writes=[b_sq[0]])
            ew = [b_sq[0], b_sq[1], dst_buf] + (list(extra_first) if eo == 0 else [])

            def p_v(v, eo=eo):
                mask = []
                for q in range(0, 32, 2):
                    mask += [q + 1, q]
                v.stream_shuffle(arena[:, AB_SCR + T:AB_SCR + 2 * T],
                                 arena[:, AB_SCR:AB_SCR + T], mask)
                v.tensor_tensor(arena[:, AB_SCR + T:AB_SCR + 2 * T],
                                arena[:, AB_SCR + T:AB_SCR + 2 * T], SINB, ALU.mult)
                v.tensor_tensor(arena[:, AB_SCR:AB_SCR + T],
                                arena[:, AB_SCR:AB_SCR + T], COSA, ALU.mult)
                return v.tensor_tensor(dst_fn(eo), arena[:, AB_SCR:AB_SCR + T],
                                       arena[:, AB_SCR + T:AB_SCR + 2 * T], ALU.add)
            op("vector", p_v, reads=[b_sq[0], b_small], writes=ew)
        return ev

    def attn_b():
        claim([b_xc, b_sq[0], b_sq[1]] + b_hid + [b_w2slot[0], b_w2slot[1]])
        ln_classic(x_hbm[1], b_x1, AR_XC, SQ3)

        claim([AB["cqr"], b_sq[0], b_sq[1]])
        projection(wdq_t, CCH, KC, xn_rhs, [b_xn], mk_cproj_evac(AB_CQR, AB["cqr"], CCH))
        rms_rows_finish(C)
        claim([AB["cqn"]] + b_hid[:4])
        rms_apply(CCH, AB_CQR, AB["cqr"],
                  lambda cc: arena[:, AB_CQN + cc * T:AB_CQN + (cc + 1) * T], AB["cqn"], G1c)

        def cqn_rhs(k, hh):
            return arena[:, AB_CQN + k * T + hh * 512:AB_CQN + k * T + hh * 512 + 512]
        claim([AB["qr"]] + b_hid[40:56])
        projection(wqr_t, KC, CCH, cqn_rhs, [AB["cqn"]],
                   mk_rope_evac(lambda eo: arena[:, AB_QR + eo * T:AB_QR + (eo + 1) * T],
                                AB["qr"]),
                   ring=wr2, ring_bufs=b_wr2, nslots=2, slot_elems=512, tile_elems=CCH * 128)

        AB_CKVR = 4096
        claim(b_hid[32:40])
        claim([AB["ckvr"], b_xc] + b_hid[4:8])
        projection(wdkv_t, CCH, KC, xn_rhs, [b_xn], mk_cproj_evac(AB_CKVR, AB["ckvr"], CCH))
        rms_rows_finish(C)
        claim([AB["ckv"]])
        rms_apply(CCH, AB_CKVR, AB["ckvr"],
                  lambda cc: arena[:, AB_CKV + cc * 2048:AB_CKV + cc * 2048 + T], AB["ckv"], G2c)

        claim([AB["kr"], AB["cqn"], AB["ckvr"]] + b_hid[:32])
        projection(wkr_t, KC, KC, xn_rhs, [b_xn],
                   mk_rope_evac(lambda eo: arena[:, AB_KR + eo * 2048:AB_KR + eo * 2048 + T],
                                AB["kr"]))

        claim([b_xn, AB["t1b0"], AB["t1b1"], AB["wuqr"]])
        for eo_a in range(CCH):
            op("gpsimd",
               lambda h, eo_a=eo_a: h.dma_start(
                   out=wr[:, eo_a * 2048:eo_a * 2048 + 2048], in_=wuk_n[eo_a]),
               writes=[b_wr[eo_a]], dma=True)
        for k2 in range(KC):
            op("gpsimd",
               lambda h, k2=k2: h.dma_start(
                   out=xn[:, XN_WUQ + k2 * 512:XN_WUQ + (k2 + 1) * 512], in_=wuq_t[k2]),
               writes=[AB["wuqr"]], dma=True)

        ck_src = bass.AP(arena, AB_CKV, [[65536, 128], [2048, CCH], [1, T]])
        ck_dst = bass.AP(ag_in.tensor, 0, [[T, 128], [128 * T, CCH], [1, T]])
        op("sync", lambda h: h.dma_start(out=ck_dst, in_=ck_src),
           reads=[AB["ckv"]], writes=[AB["agin"]], dma=True)
        kr_src = bass.AP(arena, AB_KR, [[65536, 128], [2048, KC], [1, T]])
        kr_dst = bass.AP(ag_in.tensor, CCH * 128 * T, [[T, 128], [128 * T, KC], [1, T]])
        op("sync", lambda h: h.dma_start(out=kr_dst, in_=kr_src),
           reads=[AB["kr"]], writes=[AB["agin"]], dma=True)
        op("sync", lambda h: h.dma_start(out=fence_in[:], in_=consts_in[0:1, :]),
           writes=[AB["fin"]], dma=True)
        op("gpsimd",
           lambda h: h.collective_compute(
               "AllGather", ALU.bypass,
               replica_groups=[[0, 1], [2, 3], [4, 5], [6, 7]],
               ins=[ag_in[:]], outs=[ag_out[:]]),
           reads=[AB["agin"]], writes=[AB["agout"]])
        op("gpsimd",
           lambda h: h.collective_compute(
               "AllReduce", ALU.add,
               replica_groups=[[0, 1], [2, 3], [4, 5], [6, 7]],
               ins=[fence_in[:]], outs=[fence_out[:]]),
           reads=[AB["fin"], AB["agout"]], writes=[AB["fence"]])
        for _ in range(2):
            op("gpsimd",
               lambda h: h.dma_start(out=mrow[:, 0:4],
                                     in_=bass.AP(ag_out.tensor, 0, [[4, 128], [1, 4]])),
               reads=[AB["fence"], AB["agout"]], writes=[b_mrow], dma=True)

        def cqr_rhs(k, hh):
            return arena[:, AB_CQR + k * T + hh * 512:AB_CQR + k * T + hh * 512 + 512]
        claim([AB["aq"], b_sq[0], b_sq[1]])
        for eo_a in range(CCH):
            pa = 2 + eo_a % 2
            i_a = eo_a % 2
            for k2 in range(KC):
                def p_t1(t, k2=k2, pa=pa):
                    for k in range(CCH):
                        lhs = xn[:, XN_WUQ + k2 * 512 + k * 128:XN_WUQ + k2 * 512 + (k + 1) * 128]
                        t.matmul(PPl(pa), lhs, cqr_rhs(k, 0), start=(k == 0), stop=(k == CCH - 1))
                        mi = t.matmul(PPh(pa), lhs, cqr_rhs(k, 1), start=(k == 0), stop=(k == CCH - 1))
                    return mi
                op("tensor", p_t1, reads=[AB["wuqr"], AB["cqr"]], writes=[b_pp[pa]])
                tb = AB[f"t1b{k2 % 2}"]
                op("vector",
                   lambda h, k2=k2, pa=pa: h.tensor_copy(
                       xn[:, XN_T1B + (k2 % 2) * T:XN_T1B + (k2 % 2) * T + T], PP[pa][:]),
                   reads=[b_pp[pa]], writes=[tb])

                def p_aq(t, k2=k2, eo_a=eo_a, i_a=i_a):
                    lhs = wr[:, eo_a * 2048 + k2 * 128:eo_a * 2048 + (k2 + 1) * 128]
                    t.matmul(PPl(i_a), lhs,
                             xn[:, XN_T1B + (k2 % 2) * T:XN_T1B + (k2 % 2) * T + 512],
                             start=(k2 == 0), stop=(k2 == KC - 1))
                    return t.matmul(PPh(i_a), lhs,
                                    xn[:, XN_T1B + (k2 % 2) * T + 512:XN_T1B + (k2 % 2) * T + 1024],
                                    start=(k2 == 0), stop=(k2 == KC - 1))
                op("tensor", p_aq, reads=[b_wr[eo_a], tb], writes=[b_pp[i_a]])
            op("vector",
               lambda h, eo_a=eo_a, i_a=i_a: h.tensor_copy(
                   arena[:, AB_AQ + eo_a * T:AB_AQ + (eo_a + 1) * T], PP[i_a][:]),
               reads=[b_pp[i_a]], writes=[AB["aq"]])

        claim([b_xn, AB["t1b0"], AB["t1b1"], AB["wuqr"], AB["wuv"]])
        for k in range(KC):
            op("gpsimd",
               lambda h, k=k: h.dma_start(out=xn[:, XN_WUV + k * 512:XN_WUV + (k + 1) * 512],
                                          in_=wuv_n[k]),
               reads=[], writes=[AB["wuv"]], dma=True)
        claim([AB["awo"], AB["stg"]])
        for eo in range(KC):
            sl = eo % 4
            op("gpsimd",
               lambda h, eo=eo, sl=sl: h.dma_start(
                   out=wr[:, sl * 2048:sl * 2048 + 2048], in_=wot_b[eo]),
               writes=[b_wr[sl]], dma=True)
            i = eo % 2

            def p_aw(t, eo=eo, sl=sl, i=i):
                for k in range(KC):
                    mi = t.matmul(PPl(i), wr[:, sl * 2048 + k * 128:sl * 2048 + (k + 1) * 128],
                                  xn[:, XN_WUV + k * 512:XN_WUV + (k + 1) * 512],
                                  start=(k == 0), stop=(k == KC - 1))
                return mi
            op("tensor", p_aw, reads=[b_wr[sl], AB["wuv"]], writes=[b_pp[i]])
            op("vector",
               lambda h, i=i, eo=eo: h.tensor_copy(
                   arena[:, AB_ATT + (eo % 2) * 512:AB_ATT + (eo % 2) * 512 + 512], PPl(i)),
               reads=[b_pp[i]], writes=[AB["stg"]])

            def p_tr(t, eo=eo, i=i):
                for cc in range(CCH):
                    mi = t.matmul(PPh(i)[:, 0:512][:, cc * 128:(cc + 1) * 128] if False else
                                  PP[i][:, 512 + cc * 128:512 + (cc + 1) * 128],
                                  arena[:, AB_ATT + (eo % 2) * 512 + cc * 128:AB_ATT + (eo % 2) * 512 + (cc + 1) * 128],
                                  IDENT, start=True, stop=True)
                return mi
            op("tensor", p_tr, reads=[AB["stg"], b_small], writes=[b_pp[i]])
            dst = bass.AP(xn, XN_AWO + eo * 128, [[KC * T, 128], [2048, CCH], [1, 128]])
            op("vector",
               lambda h, i=i, dst=dst: h.tensor_copy(dst, PP[i][:, 512:1024]),
               reads=[b_pp[i]], writes=[AB["awo"]])

        for r in range(2):
            for ec in range(KC):
                src = bass.AP(ag_out.tensor, r * AGN + CCH * 128 * T + ec * 128 * T,
                              [[T, 128], [1, T]])
                op("sync",
                   lambda h, r=r, ec=ec, src=src: h.dma_start(
                       out=arena[:, AB_KR + ec * 2048 + r * T:AB_KR + ec * 2048 + (r + 1) * T],
                       in_=src),
                   reads=[AB["agout"], AB["fence"], b_mrow], writes=[AB["kr"]], dma=True)
            for cc in range(CCH):
                src = bass.AP(ag_out.tensor, r * AGN + cc * 128 * T, [[T, 128], [1, T]])
                op("sync",
                   lambda h, r=r, cc=cc, src=src: h.dma_start(
                       out=arena[:, AB_CKV + cc * 2048 + r * T:AB_CKV + cc * 2048 + (r + 1) * T],
                       in_=src),
                   reads=[AB["agout"], AB["fence"], b_mrow], writes=[AB["ckv"]], dma=True)

        claim(b_wr + [AB["ckvt"]])
        for tk in range(KC):
            def p_ct(t, tk=tk):
                for cc in range(CCH):
                    mi = t.matmul(PP[2][:, cc * 128:(cc + 1) * 128],
                                  arena[:, AB_CKV + cc * 2048 + tk * 128:AB_CKV + cc * 2048 + (tk + 1) * 128],
                                  IDENT, start=True, stop=True)
                return mi
            op("tensor", p_ct, reads=[AB["ckv"], b_small], writes=[b_pp[2]])
            op("vector",
               lambda h, tk=tk: h.tensor_copy(wr[:, tk * 512:(tk + 1) * 512], PP[2][:, 0:512]),
               reads=[b_pp[2]], writes=[AB["ckvt"]])

        claim([AB["att"], AB["cqr"]])
        claim([AB["attT"], AB["of"], AB["wuv"], AB["stg"]])
        for q2 in range(8):
            for half in range(2):
                def p_sc(t, q2=q2, half=half):
                    for kq in range(2):
                        tgt = PP[half][:, kq * 512:(kq + 1) * 512]
                        first = True
                        for cc in range(CCH):
                            t.matmul(tgt, arena[0:128, AB_AQ + cc * T + q2 * 128:AB_AQ + cc * T + (q2 + 1) * 128],
                                     arena[:, AB_CKV + cc * 2048 + half * T + kq * 512:AB_CKV + cc * 2048 + half * T + (kq + 1) * 512],
                                     start=first, stop=False)
                            first = False
                        for ec in range(KC):
                            mi = t.matmul(tgt, arena[:, AB_QR + ec * T + q2 * 128:AB_QR + ec * T + (q2 + 1) * 128],
                                          arena[:, AB_KR + ec * 2048 + half * T + kq * 512:AB_KR + ec * 2048 + half * T + (kq + 1) * 512],
                                          start=False, stop=(ec == KC - 1))
                    return mi
                op("tensor", p_sc, reads=[AB["aq"], AB["ckv"], AB["qr"], AB["kr"]],
                   writes=[b_pp[half]])

            def p_mx(v):
                v.reduce_max(mrow[:, 0:1], PP[0][:], AX.X)
                v.reduce_max(mrow[:, 1:2], PP[1][:], AX.X)
                v.tensor_copy(mrow[:, 6:7], bw_sb[:, 0:1])
                v.tensor_copy(mrow[:, 6:7], bw_sb[:, 0:1])
                v.tensor_tensor(mrow[:, 0:1], mrow[:, 0:1], mrow[:, 1:2], ALU.max)
                v.tensor_copy(mrow[:, 6:7], bw_sb[:, 0:1])
                v.tensor_copy(mrow[:, 6:7], bw_sb[:, 0:1])
                return v.tensor_scalar(mrow[:, 1:2], mrow[:, 0:1], -SM_SCALE, None, ALU.mult)
            op("vector", p_mx, reads=[b_pp[0], b_pp[1]], writes=[b_mrow])
            op("scalar",
               lambda h: h.activation(arena[:, AB_ATT:AB_ATT + T], PP[0][:], AF.Exp,
                                      bias=mrow[:, 1:2], scale=SM_SCALE),
               reads=[b_pp[0], b_mrow], writes=[AB["att"]])
            op("scalar",
               lambda h: h.activation(arena[:, AB_ATT + T:AB_ATT + 2 * T], PP[1][:], AF.Exp,
                                      bias=mrow[:, 1:2], scale=SM_SCALE),
               reads=[b_pp[1], b_mrow], writes=[AB["att"]])

            def p_sum(v):
                v.reduce_sum(mrow[:, 2:3], arena[:, AB_ATT:AB_ATT + 2 * T], AX.X)
                v.tensor_copy(mrow[:, 6:7], bw_sb[:, 0:1])
                v.tensor_copy(mrow[:, 6:7], bw_sb[:, 0:1])
                v.reciprocal(mrow[:, 3:4], mrow[:, 2:3])
                v.tensor_copy(mrow[:, 6:7], bw_sb[:, 0:1])
                v.tensor_copy(mrow[:, 6:7], bw_sb[:, 0:1])
                return v.tensor_scalar(arena[:, AB_ATT:AB_ATT + 2 * T],
                                       arena[:, AB_ATT:AB_ATT + 2 * T],
                                       mrow[:, 3:4], None, ALU.mult)
            op("vector", p_sum, reads=[AB["att"]], writes=[AB["att"], b_mrow])

            for grp in range(4):
                pt = 2 + grp % 2

                def p_at(t, q2=q2, grp=grp, pt=pt):
                    for j in range(4):
                        tk = grp * 4 + j
                        mi = t.matmul(PP[pt][:, (grp % 2) * 0 + j * 128:j * 128 + 128],
                                      arena[:, AB_ATT + tk * 128:AB_ATT + (tk + 1) * 128],
                                      IDENT, start=True, stop=True)
                    return mi
                op("tensor", p_at, reads=[AB["att"], b_small], writes=[b_pp[pt]])
                op("vector",
                   lambda h, grp=grp, pt=pt: h.tensor_copy(
                       xn[:, XN_ATT + grp * 512:XN_ATT + (grp + 1) * 512], PP[pt][:, 0:512]),
                   reads=[b_pp[pt]], writes=[AB["attT"]])

            def p_o(t, q2=q2):
                for cc in range(CCH):
                    for tk in range(KC):
                        mi = t.matmul(PP[2][:, 512 + cc * 128:512 + (cc + 1) * 128],
                                      wr[:, tk * 512 + cc * 128:tk * 512 + (cc + 1) * 128],
                                      xn[:, XN_ATT + tk * 128:XN_ATT + (tk + 1) * 128],
                                      start=(tk == 0), stop=(tk == KC - 1))
                return mi
            op("tensor", p_o, reads=[AB["ckvt"], AB["attT"]], writes=[b_pp[2]])
            dst = bass.AP(xn, XN_OF + q2 * 128, [[KC * T, 128], [T, CCH], [1, 128]])
            op("vector", lambda h, dst=dst: h.tensor_copy(dst, PP[2][:, 512:1024]),
               reads=[b_pp[2]], writes=[AB["of"]])

        claim([AB["ub"], AB["qr"], b_sq[0], b_sq[1]])

        def p_u(eo, i):
            op("vector",
               lambda h, eo=eo, i=i: h.tensor_copy(
                   arena[:, AB_UB + eo * T:AB_UB + (eo + 1) * T], PP[i][:]),
               reads=[b_pp[i]], writes=[AB["ub"]])
            op("scalar",
               lambda h, eo=eo, i=i: h.activation(
                   arena[:, AB_SCR + (eo % 2) * T:AB_SCR + (eo % 2) * T + T],
                   PP[i][:], AF.Square),
               reads=[b_pp[i]], writes=[b_sq[eo % 2]])

            def p_m(t, eo=eo):
                t.matmul(PPl(2), ONES,
                         arena[:, AB_SCR + (eo % 2) * T:AB_SCR + (eo % 2) * T + 512],
                         start=(eo == 0), stop=(eo == KC - 1))
                return t.matmul(PPh(2), ONES,
                                arena[:, AB_SCR + (eo % 2) * T + 512:AB_SCR + (eo % 2) * T + 1024],
                                start=(eo == 0), stop=(eo == KC - 1))
            op("tensor", p_m, reads=[b_sq[eo % 2], b_small], writes=[b_pp[2]])

        for eo in range(KC):
            i = eo % 2

            def p_up(t, eo=eo, i=i):
                for cc in range(CCH):
                    lhs = xn[:, XN_AWO + cc * 2048 + eo * 128:XN_AWO + cc * 2048 + (eo + 1) * 128]
                    t.matmul(PPl(i), lhs, xn[:, XN_OF + cc * T:XN_OF + cc * T + 512],
                             start=(cc == 0), stop=(cc == CCH - 1))
                    mi = t.matmul(PPh(i), lhs, xn[:, XN_OF + cc * T + 512:XN_OF + cc * T + 1024],
                                  start=(cc == 0), stop=(cc == CCH - 1))
                return mi
            op("tensor", p_up, reads=[AB["awo"], AB["of"]], writes=[b_pp[i]])
            p_u(eo, i)
        rms_rows_finish(E)

        claim([b_xc] + b_hid[:16])
        for k in range(KC):
            xrb = b_xr[k % 2]
            op("sync",
               lambda h, k=k: h.dma_start(out=xr[:, (k % 2) * T:(k % 2) * T + T],
                                          in_=x_hbm[1][k]),
               reads=[b_x1], writes=[xrb], dma=True)
            op("scalar",
               lambda h, k=k: h.activation(sc1[:], arena[:, AB_UB + k * T:AB_UB + (k + 1) * T],
                                           AF.Copy, scale=G3c(k)),
               reads=[AB["ub"], b_small], writes=[b_sc1])

            def p_r(v, k=k):
                v.tensor_tensor(sc1[:], sc1[:], fA[:], ALU.mult)
                return v.tensor_tensor(xr[:, (k % 2) * T:(k % 2) * T + T],
                                       xr[:, (k % 2) * T:(k % 2) * T + T], sc1[:], ALU.add)
            op("vector", p_r, reads=[b_sc1, b_fA, xrb], writes=[b_sc1, xrb])
            op("sync",
               lambda h, k=k: h.dma_start(out=x_hbm[0][k],
                                          in_=xr[:, (k % 2) * T:(k % 2) * T + T]),
               reads=[xrb], writes=[b_x0], dma=True)
            op("vector",
               lambda h, k=k: h.tensor_copy(arena[:, AR_XC + k * T:AR_XC + (k + 1) * T],
                                            xr[:, (k % 2) * T:(k % 2) * T + T]),
               reads=[xrb], writes=[b_xc])
            ln_stats_chunk(k, KC, AR_XC, AB_SCR)
        ln_rows_finish(LN_EPS, E)
        claim([b_xn, AB["attT"], AB["of"], AB["wuv"], AB["awo"], AB["t1b0"], AB["t1b1"],
               b_w2slot[0], b_w2slot[1]])
        ln_apply(AR_XC)

    if upto >= 1:
        attn_a()
    if upto >= 2:
        ffn(x_hbm[0], b_x0, x_hbm[1], b_x1, final=False,
            arena_claim=[b_xc, b_sgq, b_mem, b_sel, b_coefb] + b_sgs + b_bst + b_sq)
    if upto >= 3:
        attn_b()
    if upto >= 4:
        ffn(x_hbm[0], b_x0, y_out, b_y, final=True,
            arena_claim=[b_xc] + [AB[k] for k in ("kr", "ckv", "qr", "aq", "cqr",
                                                  "cqn", "att", "ub", "ckvr", "ckvt")]
                        + b_sq + b_wr)

    def finalize(src, src_buf):
        for k in range(KC):
            xrb = b_xr[k % 2]
            op("sync", lambda h, k=k: h.dma_start(
                out=xr[:, (k % 2) * T:(k % 2) * T + T], in_=src[k]),
               reads=[src_buf], writes=[xrb], dma=True)
            op("sync", lambda h, k=k: h.dma_start(
                out=y_out[k], in_=xr[:, (k % 2) * T:(k % 2) * T + T]),
               reads=[xrb], writes=[b_y], dma=True)

    if upto == 1:
        finalize(x_hbm[0], b_x0)
    elif upto == 2:
        finalize(x_hbm[1], b_x1)
    elif upto == 3:
        finalize(x_hbm[0], b_x0)

    if DBG is not None:
        allb = [b_xn, b_sgq, b_xc, b_coefb, b_sel, b_mem, b_wact] + b_sgs + b_bst + b_hid[:1] + [AB["kr"], AB["ckv"], AB["qr"], AB["aq"], AB["cqr"], AB["of"], AB["attT"], AB["awo"], AB["ub"], AB["ckvt"], AB["ckvr"]]
        if isinstance(DBG, tuple):
            dl = [(DBG[0], DBG[1] + kk * T) for kk in range(KC)]
        else:
            dl = DBG
        for kk, (tn, off) in enumerate(dl):
            srcten = arena if tn == "arena" else xn
            lim = 65536 if tn == "arena" else KC * T
            o0 = min(off, lim - T)
            op("gpsimd", lambda h, kk=kk, o0=o0, srcten=srcten: h.dma_start(
                out=y_out[kk], in_=srcten[:, o0:o0 + T]),
               reads=allb, writes=[b_y], dma=True)

    op("vector", lambda h: h.tensor_copy(mrow[:, 6:7], mrow[:, 5:6]),
       reads=[b_y], writes=[b_mrow])

    blkg.__exit__(None, None, None)
    for g in reversed(es):
        g.__exit__(None, None, None)
    return nc


def _tile_weight_T(w):
    win, wout = w.shape[1], w.shape[0]
    nk, neo = win // 128, wout // 128
    wt = np.ascontiguousarray(w.T)
    t = wt.reshape(nk, 128, neo, 128)
    return np.ascontiguousarray(t.transpose(2, 1, 0, 3).reshape(neo, 128, nk * 128))


def _tile_weight_N(w):
    win, wout = w.shape
    nk, neo = win // 128, wout // 128
    t = w.reshape(nk, 128, neo, 128)
    return np.ascontiguousarray(t.transpose(2, 1, 0, 3).reshape(neo, 128, nk * 128))


def host_prepare(inputs):
    f32 = np.float32
    d = {}
    d["wqT"] = _tile_weight_T(np.asarray(inputs["L0_attn_a_Wq"], f32))
    d["woT"] = _tile_weight_T(np.asarray(inputs["L0_attn_a_Wo"], f32))
    mem = np.asarray(inputs["L0_attn_a_memories"], f32)
    d["mem"] = np.ascontiguousarray(
        mem.reshape(NB * H, 128, 128).transpose(1, 0, 2).reshape(128, NB * H * 128))
    mn = np.asarray(inputs["L0_attn_a_memory_norms"], f32)
    mb = np.zeros((KC, 128, 64), f32)
    for h in range(H):
        for n in range(NB):
            mb[h, :, 4 * h + n] = mn[n, h, :]
    d["mnblk"] = np.ascontiguousarray(mb.transpose(1, 0, 2).reshape(128, KC * 64))
    mn_n = np.zeros((128, H * D), f32)
    mn_n[0:NB] = mn.reshape(NB, H * D)
    d["mnorm_n"] = mn_n
    bw = np.zeros((128, NB), f32)
    bw[0:H] = np.asarray(inputs["L0_attn_a_bank_weights"], f32)
    d["bw"] = bw
    d["w1T"] = _tile_weight_T(np.asarray(inputs["L0_ffn_W1"], f32))
    d["w3T"] = _tile_weight_T(np.asarray(inputs["L0_ffn_W3"], f32))
    w2t = np.ascontiguousarray(np.asarray(inputs["L0_ffn_W2"], f32).T)
    d["w2p1"] = np.ascontiguousarray(
        w2t.reshape(64, 64, 2, E).transpose(1, 2, 0, 3).reshape(64, 128, E))
    d["wdqT"] = _tile_weight_T(np.asarray(inputs["L0_attn_b_w_d_q"], f32))
    d["wdkvT"] = _tile_weight_T(np.asarray(inputs["L0_attn_b_w_d_kv"], f32))
    d["wuqT"] = _tile_weight_T(np.asarray(inputs["L0_attn_b_w_u_q"], f32))
    d["wuk"] = _tile_weight_N(np.asarray(inputs["L0_attn_b_w_u_k"], f32))
    d["wkrT"] = _tile_weight_T(np.asarray(inputs["L0_attn_b_w_kr"], f32))
    d["wqrT"] = _tile_weight_T(np.asarray(inputs["L0_attn_b_w_qr"], f32))
    wuv = np.asarray(inputs["L0_attn_b_w_u_v"], f32)
    d["wuv"] = np.ascontiguousarray(wuv.reshape(KC, 128, C))
    d["woTb"] = _tile_weight_T(np.asarray(inputs["L0_attn_b_w_o"], f32))
    d["g1"] = np.ascontiguousarray(np.asarray(inputs["L0_attn_b_g1"], f32).reshape(CCH, 128).T)
    d["g2"] = np.ascontiguousarray(np.asarray(inputs["L0_attn_b_g2"], f32).reshape(CCH, 128).T)
    d["g3"] = np.ascontiguousarray(np.asarray(inputs["L0_attn_b_g3"], f32).reshape(KC, 128).T)

    consts = np.zeros((128, 512), BF)
    consts[:, 0:128] = np.eye(128, dtype=BF)
    consts[:, 128:256] = np.ones((128, 128), BF)
    consts[:, 256:384] = _had(128).astype(BF)
    consts[:, 384:512] = np.kron(np.eye(2), _had(64)).astype(BF)
    d["consts"] = consts
    selh = np.zeros((128, 64), BF)
    for m in range(64):
        selh[m // 4, m] = 1
    d["selh"] = selh
    seln = np.zeros((128, 64), BF)
    for m in range(64):
        seln[m % 4, m] = 1
    d["seln"] = seln
    mask = np.zeros((128, 4), np.float32)
    for m in range(64):
        mask[m, m % 4] = 1
    d["mask64"] = mask
    sel = np.zeros((128, 64 * 128), BF)
    for r in range(64):
        sel[r, r * 128:(r + 1) * 128] = 1
    d["sel"] = sel
    return d


def rope_tables(core_id):
    s0 = (core_id % 2) * T
    pos = s0 + np.arange(T, dtype=np.float64)
    freqs = 1.0 / (10000.0 ** (np.arange(0, D, 2, dtype=np.float64) / D))
    ang = pos[None, :] * freqs[:, None]
    rt = np.empty((128, 2 * T), dtype=BF)
    c = np.cos(ang)
    s = np.sin(ang)
    rt[0::2, 0:T] = c
    rt[1::2, 0:T] = c
    rt[0::2, T:2 * T] = -s
    rt[1::2, T:2 * T] = s
    return rt


def kernel(**inputs):
    shared = host_prepare(inputs)
    x = np.asarray(inputs["L0_attn_a_hidden_states"], np.float32)
    in_maps = []
    for c in range(NCORES):
        b, half = c // 2, c % 2
        xs = np.ascontiguousarray(x[b, half * T:(half + 1) * T, :].T)
        m = dict(shared)
        m["x"] = np.ascontiguousarray(xs.reshape(KC, 128, T))
        m["rope"] = rope_tables(c)
        in_maps.append(m)
    nc = build_program(upto=UPTO)
    res = run_bass_kernel_spmd(nc, in_maps, core_ids=list(range(NCORES)))
    out = np.empty_like(x)
    for c in range(NCORES):
        b, half = c // 2, c % 2
        y = res.results[c]["y"].reshape(E, T)
        out[b, half * T:(half + 1) * T, :] = y.T
    return out


# version 04 (final)
# speedup vs baseline: 1.0663x; 1.0210x over previous
import math

import numpy as np
import ml_dtypes

import concourse.bass as bass
import concourse.mybir as mybir
from concourse.bass_utils import run_bass_kernel_spmd

F32 = mybir.dt.float32
BF16 = mybir.dt.bfloat16
AF = mybir.ActivationFunctionType
ALU = mybir.AluOpType
AX = mybir.AxisListType
BF = ml_dtypes.bfloat16

B, S, E, T = 4, 2048, 2048, 1024
KC = E // 128
H, D, NB = 16, 128, 4
C = 512
CCH = C // 128
FH = 8192
FHC = FH // 128
LN_EPS = 1.0e-5
RMS_EPS = float(np.finfo(np.float32).eps)
A_EPS = 1.0e-6
SM_SCALE = 1.0 / 12.0
FWHT_SCALE = 1.0 / math.sqrt(FH)
NCORES = 8
UPTO = 4
DBG = None

ENGS = ("tensor", "vector", "scalar", "sync", "gpsimd")

AR_XC = 0
AR_SQ1 = 61440
AR_SGQ = 16384
AR_MEM = 32768
AR_SEL = 40960
AR_SGS = 49152
AR_CFB = 57344
AR_FOLD = 16384
AR_HID = 0
SQ3 = 16384
AB_KR = 0
AB_CKV = 32768
AB_QR = 40960
AB_AQ = 57344
AB_SCR = 57344
AB_CQR = 61440
AB_ATT = 61440
AB_UB = 40960
AB_CQN = 0
XN_T1B = 0
XN_WUQ = 2048
XN_WUV = 0
XN_AWO = 8192
XN_ATT = 0
XN_OF = 2048


class Buf:
    __slots__ = ("name", "writer", "readers")

    def __init__(self, name):
        self.name = name
        self.writer = None
        self.readers = []


def _had(n):
    h = np.array([[1.0]], dtype=np.float64)
    while h.shape[0] < n:
        h = np.block([[h, h], [h, -h]])
    return h


def build_program(upto=UPTO):
    nc = bass.Bass("TRN2", target_bir_lowering=False)
    DT = nc.dram_tensor

    x_in = DT("x", [KC, 128, T], F32, kind="ExternalInput").ap()
    wq_t = DT("wqT", [KC, 128, KC * 128], F32, kind="ExternalInput").ap()
    wo_t = DT("woT", [KC, 128, KC * 128], F32, kind="ExternalInput").ap()
    mem_t = DT("mem", [128, NB * H * 128], F32, kind="ExternalInput").ap()
    mnblk = DT("mnblk", [128, KC * 64], F32, kind="ExternalInput").ap()
    mnorm_n = DT("mnorm_n", [128, H * D], F32, kind="ExternalInput").ap()
    bw_in = DT("bw", [128, NB], F32, kind="ExternalInput").ap()
    w1_t = DT("w1T", [FHC, 128, KC * 128], F32, kind="ExternalInput").ap()
    w3_t = DT("w3T", [FHC, 128, KC * 128], F32, kind="ExternalInput").ap()
    w2p1 = DT("w2p1", [64, 128, E], F32, kind="ExternalInput").ap()
    wdq_t = DT("wdqT", [CCH, 128, KC * 128], F32, kind="ExternalInput").ap()
    wdkv_t = DT("wdkvT", [CCH, 128, KC * 128], F32, kind="ExternalInput").ap()
    wuq_t = DT("wuqT", [KC, 128, CCH * 128], F32, kind="ExternalInput").ap()
    wuk_n = DT("wuk", [CCH, 128, KC * 128], F32, kind="ExternalInput").ap()
    wkr_t = DT("wkrT", [KC, 128, KC * 128], F32, kind="ExternalInput").ap()
    wqr_t = DT("wqrT", [KC, 128, CCH * 128], F32, kind="ExternalInput").ap()
    wuv_n = DT("wuv", [KC, 128, C], F32, kind="ExternalInput").ap()
    wot_b = DT("woTb", [KC, 128, KC * 128], F32, kind="ExternalInput").ap()
    g1_in = DT("g1", [128, CCH], F32, kind="ExternalInput").ap()
    g2_in = DT("g2", [128, CCH], F32, kind="ExternalInput").ap()
    g3_in = DT("g3", [128, KC], F32, kind="ExternalInput").ap()
    consts_in = DT("consts", [128, 512], BF16, kind="ExternalInput").ap()
    selh_in = DT("selh", [128, 64], BF16, kind="ExternalInput").ap()
    seln_in = DT("seln", [128, 64], BF16, kind="ExternalInput").ap()
    mask64_in = DT("mask64", [128, 4], F32, kind="ExternalInput").ap()
    sel_in = DT("sel", [128, 64 * 128], BF16, kind="ExternalInput").ap()
    rope_in = DT("rope", [128, 2 * T], BF16, kind="ExternalInput").ap()

    y_out = DT("y", [KC, 128, T], F32, kind="ExternalOutput").ap()

    x_hbm = [DT(f"xs{i}", [KC, 128, T], F32, kind="Internal").ap() for i in range(2)]
    w2mid = DT("w2mid", [8192, E], BF16, kind="Internal").ap()
    w2fin = DT("w2fin", [KC, 128, FHC * 128], BF16, kind="Internal").ap()
    AGN = CCH * 128 * T + KC * 128 * T
    ag_in = DT("ag_in", [1, AGN], BF16, kind="Internal").ap()
    ag_out = DT("ag_out", [2, AGN], BF16, kind="Internal").ap()
    fence_in = DT("fence_in", [1, 512], BF16, kind="Internal").ap()
    fence_out = DT("fence_out", [1, 512], BF16, kind="Internal").ap()

    es = []

    def sbt(name, shape, dtype):
        g = nc.sbuf_tensor(name, shape, dtype)
        t = g.__enter__()
        es.append(g)
        return t

    sem_h = {}
    for e in ENGS:
        g = nc.semaphore(f"sem_{e}")
        sem_h[e] = g.__enter__()
        es.append(g)

    xn = sbt("xn", [128, KC * T], BF16)
    arena = sbt("arena", [128, 65536], BF16)
    wr = sbt("wr", [128, 4 * 2048], BF16)
    wr2 = sbt("wr2", [128, 2 * 512], BF16)
    fA = sbt("fA", [128, T], F32)
    fB = sbt("fB", [128, T], F32)
    xr = sbt("xr", [128, 2 * T], F32)
    sc1 = sbt("sc1", [128, T], F32)
    consts = sbt("consts_sb", [128, 512], BF16)
    selh = sbt("selh_sb", [128, 64], BF16)
    seln = sbt("seln_sb", [128, 64], BF16)
    mask64 = sbt("mask64_sb", [128, 4], F32)
    g_sb = sbt("g_sb", [128, 2 * CCH + KC], F32)
    rope_sb = sbt("rope_sb", [128, 2 * T], BF16)
    mnblk_sb = sbt("mnblk_sb", [128, KC * 64], BF16)
    bw_sb = sbt("bw_sb", [128, 16], F32)
    wact = sbt("wact_sb", [64, 4], F32)
    mrow = sbt("mrow", [128, 8], F32)

    IDENT = consts[:, 0:128]
    ONES = consts[:, 128:256]
    H128 = consts[:, 256:384]
    I2H64 = consts[:, 384:512]
    G1c = lambda cc: g_sb[:, cc:cc + 1]
    G2c = lambda cc: g_sb[:, CCH + cc:CCH + cc + 1]
    G3c = lambda k: g_sb[:, 2 * CCH + k:2 * CCH + k + 1]
    COSA = rope_sb[:, 0:T]
    SINB = rope_sb[:, T:2 * T]

    PP = []
    for i in range(4):
        g = nc.psum_tensor(f"pp{i}", [128, 1024], F32)
        PP.append(g.__enter__())
        es.append(g)

    def PPl(i):
        return PP[i][:, 0:512]

    def PPh(i):
        return PP[i][:, 512:1024]

    blkg = nc.Block()
    blk = blkg.__enter__()

    cnt = {e: 0 for e in ENGS}
    lastwait = {e: {f: 0 for f in ENGS} for e in ENGS}
    trusted = {"tensor", "vector", "scalar"}

    def op(eng, emit, reads=(), writes=(), dma=False):
        waits = {}

        def need(tok):
            if tok is None:
                return
            s, v = tok
            if s == eng and eng in trusted:
                return
            if s == eng and v <= cnt[eng]:
                pass
            if lastwait[eng].get(s, 0) >= v:
                return
            if cnt[s] < v:
                raise RuntimeError(f"{eng} waits future {s}>={v} (at {cnt[s]})")
            waits[s] = max(waits.get(s, 0), v)

        for b in reads:
            need(b.writer)
        for b in writes:
            need(b.writer)
            for t in b.readers:
                need(t)

        inc = 16 if dma else 1
        tok = (eng, cnt[eng] + inc)

        def body(h):
            for s, v in sorted(waits.items()):
                h.wait_ge(sem_h[s], v)
                lastwait[eng][s] = max(lastwait[eng][s], v)
            last = emit(h)
            last.then_inc(sem_h[eng], inc)

        getattr(blk, eng)(body)
        cnt[eng] += inc
        for b in reads:
            b.readers.append(tok)
        for b in writes:
            b.writer = tok
            b.readers = []
        return tok

    def claim(bufs):
        op("vector", lambda h: h.tensor_copy(mrow[:, 7:8], mrow[:, 6:7]),
           reads=[], writes=list(bufs))

    b_small = Buf("small")
    b_coefb = Buf("coefb")
    b_wact = Buf("wact")
    b_fA = Buf("fA")
    b_fB = Buf("fB")
    b_sc1 = Buf("sc1")
    b_mrow = Buf("mrow")
    b_bw = Buf("bw")
    b_coef = Buf("coef")
    b_pp = [Buf(f"pp{i}") for i in range(4)]
    b_wr = [Buf(f"wr{i}") for i in range(4)]
    b_wr2 = [Buf(f"wr2{i}") for i in range(2)]
    b_xr = [Buf("xr0"), Buf("xr1")]
    b_xn = Buf("xn")
    b_mem = Buf("mem")
    b_sel = Buf("sel")
    b_mnblk = Buf("mnblk")
    b_xc = Buf("xc")
    b_sq = [Buf("sq0"), Buf("sq1")]
    b_sgq = Buf("sgq")
    b_sgs = [Buf(f"sgs{i}") for i in range(6)]
    b_bst = [Buf("bst0"), Buf("bst1")]
    b_hid = [Buf(f"hid{i}") for i in range(FHC)]
    b_w2slot = [Buf("w2s0"), Buf("w2s1")]
    b_xin = Buf("xin")
    b_x0 = Buf("xhbm0")
    b_x1 = Buf("xhbm1")
    b_y = Buf("yout")
    b_w2mid = Buf("w2mid")
    b_w2fin = Buf("w2fin")
    b_fold = [Buf(f"fold{i}") for i in range(4)]
    b_fold2 = [Buf(f"fold2_{i}") for i in range(16)]
    AB = {n: Buf(n) for n in ("kr", "ckv", "qr", "aq", "cqr", "cqn", "t1b0", "t1b1",
                              "att", "ub", "wuv", "awo", "attT", "of", "agin",
                              "agout", "ckvt", "stg", "ckvr", "fin", "fence", "wuqr")}

    for dst, src in ((consts[:], consts_in), (selh[:], selh_in), (seln[:], seln_in),
                     (mask64[:], mask64_in), (g_sb[:, 0:CCH], g1_in),
                     (g_sb[:, CCH:2 * CCH], g2_in),
                     (g_sb[:, 2 * CCH:2 * CCH + KC], g3_in), (rope_sb[:], rope_in)):
        op("sync", lambda h, dst=dst, src=src: h.dma_start(out=dst, in_=src),
           writes=[b_small], dma=True)
    op("sync", lambda h: h.dma_start(out=bw_sb[:, 0:NB], in_=bw_in), writes=[b_bw], dma=True)
    op("sync", lambda h: h.dma_start(out=xr[:, 0:T], in_=mnorm_n[:, 0:T]), writes=[b_xr[0]], dma=True)
    op("sync", lambda h: h.dma_start(out=xr[:, T:2 * T], in_=mnorm_n[:, T:2 * T]), writes=[b_xr[1]], dma=True)
    op("gpsimd", lambda h: h.dma_start(out=mnblk_sb[:], in_=mnblk), writes=[b_mnblk], dma=True)
    op("gpsimd", lambda h: h.dma_start(out=arena[:, AR_MEM:AR_MEM + 8192], in_=mem_t), writes=[b_mem], dma=True)
    op("gpsimd", lambda h: h.dma_start(out=arena[:, AR_SEL:AR_SEL + 8192], in_=sel_in), writes=[b_sel], dma=True)

    def p_bw1(v):
        v.reduce_sum(mrow[:, 0:1], xr[:, 0:T], AX.X)
        v.reduce_sum(mrow[:, 1:2], xr[:, T:2 * T], AX.X)
        v.tensor_copy(mrow[:, 6:7], bw_sb[:, 0:1])
        v.tensor_copy(mrow[:, 6:7], bw_sb[:, 0:1])
        v.tensor_tensor(mrow[:, 0:1], mrow[:, 0:1], mrow[:, 1:2], ALU.add)
        v.tensor_copy(mrow[:, 6:7], bw_sb[:, 0:1])
        v.tensor_copy(mrow[:, 6:7], bw_sb[:, 0:1])
        v.tensor_scalar(mrow[:, 2:3], mrow[:, 0:1], A_EPS, None, ALU.is_ge)
        v.reduce_max(mrow[:, 3:4], bw_sb[:, 0:NB], AX.X)
        v.tensor_copy(mrow[:, 6:7], mrow[:, 3:4])
        v.tensor_copy(mrow[:, 6:7], mrow[:, 3:4])
        return v.tensor_scalar(bw_sb[:, 4:4 + NB], bw_sb[:, 0:NB], mrow[:, 3:4], None, ALU.subtract)
    op("vector", p_bw1, reads=[b_xr[0], b_xr[1], b_bw], writes=[b_mrow, b_bw])
    op("scalar", lambda h: h.activation(bw_sb[:, 8:8 + NB], bw_sb[:, 4:4 + NB], AF.Exp),
       reads=[b_bw], writes=[b_bw])

    def p_bw3(v):
        v.reduce_sum(mrow[:, 4:5], bw_sb[:, 8:8 + NB], AX.X)
        v.tensor_copy(mrow[:, 6:7], bw_sb[:, 0:1])
        v.tensor_copy(mrow[:, 6:7], bw_sb[:, 0:1])
        v.reciprocal(mrow[:, 5:6], mrow[:, 4:5])
        v.tensor_copy(mrow[:, 6:7], bw_sb[:, 0:1])
        v.tensor_copy(mrow[:, 6:7], bw_sb[:, 0:1])
        v.tensor_scalar(bw_sb[:, 12:12 + NB], bw_sb[:, 8:8 + NB], mrow[:, 5:6], None, ALU.mult)
        v.tensor_copy(mrow[:, 6:7], bw_sb[:, 0:1])
        v.tensor_copy(mrow[:, 6:7], bw_sb[:, 0:1])
        v.tensor_copy(arena[0:16, AR_CFB:AR_CFB + 4], bw_sb[0:16, 12:16])
        return v.tensor_copy(arena[0:4, AR_CFB + 8:AR_CFB + 9], mrow[0:4, 2:3])
    op("vector", p_bw3, reads=[b_bw, b_mrow], writes=[b_bw, b_mrow, b_coefb])

    def p_bw4(t):
        t.matmul(PP[2][0:64, 0:4], selh[0:16, 0:64], arena[0:16, AR_CFB:AR_CFB + 4],
                 start=True, stop=True)
        return t.matmul(PP[2][0:64, 8:9], seln[0:4, 0:64], arena[0:4, AR_CFB + 8:AR_CFB + 9],
                        start=True, stop=True)
    op("tensor", p_bw4, reads=[b_small, b_coefb], writes=[b_pp[2]])

    def p_bw5(v):
        v.tensor_tensor(wact[:, 0:4], PP[2][0:64, 0:4], mask64[0:64, 0:4], ALU.mult)
        v.tensor_copy(mrow[:, 6:7], bw_sb[:, 0:1])
        v.tensor_copy(mrow[:, 6:7], bw_sb[:, 0:1])
        v.reduce_sum(wact[:, 2:3], wact[:, 0:4], AX.X)
        v.tensor_copy(mrow[:, 6:7], bw_sb[:, 0:1])
        v.tensor_copy(mrow[:, 6:7], bw_sb[:, 0:1])
        v.tensor_tensor(wact[:, 1:2], wact[:, 2:3], PP[2][0:64, 8:9], ALU.mult)
        v.tensor_copy(arena[0:64, AR_CFB + 1024:AR_CFB + 1028], PP[2][0:64, 0:4])
        v.tensor_copy(arena[0:64, AR_CFB + 1028:AR_CFB + 1029], PP[2][0:64, 8:9])
        v.tensor_copy(arena[0:16, AR_CFB + 1032:AR_CFB + 1048], bw_sb[0:16, 0:16])
        return v.tensor_copy(arena[0:8, AR_CFB + 1048:AR_CFB + 1056], mrow[0:8, 0:8])
    op("vector", p_bw5, reads=[b_pp[2], b_small], writes=[b_wact, b_coefb])

    def w2_fold():
        FG = 8

        def fold_load1(g2):
            sl = g2 % 4
            op("gpsimd",
               lambda h, g2=g2, sl=sl: h.dma_start(
                   out=wr[:, sl * 2048:sl * 2048 + 2048], in_=w2p1[g2]),
               writes=[b_wr[sl]], dma=True)

        for g2 in range(4):
            fold_load1(g2)
        for g2 in range(64):
            sl = g2 % 4
            i = g2 % 2
            stg = b_fold[g2 % 4]
            soff = AR_FOLD + (g2 % 4) * 2048

            def p_m1(t, sl=sl, i=i):
                t.matmul(PPl(i), I2H64, wr[:, sl * 2048:sl * 2048 + 512], start=True, stop=True)
                return t.matmul(PPh(i), I2H64, wr[:, sl * 2048 + 512:sl * 2048 + 1024],
                                start=True, stop=True)

            def p_m2(t, sl=sl, i=i):
                t.matmul(PPl(i), I2H64, wr[:, sl * 2048 + 1024:sl * 2048 + 1536], start=True, stop=True)
                return t.matmul(PPh(i), I2H64, wr[:, sl * 2048 + 1536:sl * 2048 + 2048],
                                start=True, stop=True)
            op("tensor", p_m1, reads=[b_small, b_wr[sl]], writes=[b_pp[i]])
            op("vector", lambda h, i=i, soff=soff: h.tensor_copy(arena[:, soff:soff + 1024], PP[i][:]),
               reads=[b_pp[i]], writes=[stg])
            op("tensor", p_m2, reads=[b_small, b_wr[sl]], writes=[b_pp[i]])
            op("vector", lambda h, i=i, soff=soff: h.tensor_copy(arena[:, soff + 1024:soff + 2048], PP[i][:]),
               reads=[b_pp[i]], writes=[stg])
            op("sync",
               lambda h, g2=g2, soff=soff: h.dma_start(
                   out=w2mid[128 * g2:128 * g2 + 128, :], in_=arena[:, soff:soff + 2048]),
               reads=[stg], writes=[b_w2mid], dma=True)
            if g2 + 4 < 64:
                fold_load1(g2 + 4)

        def fold_load2(cpos):
            sl = cpos % 4
            sap = bass.AP(w2mid.tensor, cpos * 2048, [[64 * 2048, 128], [1, 2048]])
            op("gpsimd",
               lambda h, sap=sap, sl=sl: h.dma_start(out=wr[:, sl * 2048:sl * 2048 + 2048], in_=sap),
               reads=[b_w2mid], writes=[b_wr[sl]], dma=True)

        for cpos in range(4):
            fold_load2(cpos)
        for cpos in range(64):
            sl = cpos % 4
            i = 2 + (cpos % 2)
            j = cpos % FG
            par = 0
            base = AR_FOLD
            stg = b_fold2[j]

            def q_m1(t, sl=sl, i=i):
                t.matmul(PPl(i), H128, wr[:, sl * 2048:sl * 2048 + 512], start=True, stop=True)
                return t.matmul(PPh(i), H128, wr[:, sl * 2048 + 512:sl * 2048 + 1024],
                                start=True, stop=True)

            def q_m2(t, sl=sl, i=i):
                t.matmul(PPl(i), H128, wr[:, sl * 2048 + 1024:sl * 2048 + 1536], start=True, stop=True)
                return t.matmul(PPh(i), H128, wr[:, sl * 2048 + 1536:sl * 2048 + 2048],
                                start=True, stop=True)
            op("tensor", q_m1, reads=[b_small, b_wr[sl]], writes=[b_pp[i]])
            op("vector", lambda h, i=i, j=j, base=base: h.tensor_scalar(
                bass.AP(arena, base + j * 128, [[65536, 128], [1024, 8], [1, 128]]),
                PP[i][:], FWHT_SCALE, None, ALU.mult),
               reads=[b_pp[i]], writes=[stg])
            op("tensor", q_m2, reads=[b_small, b_wr[sl]], writes=[b_pp[i]])
            op("vector", lambda h, i=i, j=j, base=base: h.tensor_scalar(
                bass.AP(arena, base + 8192 + j * 128, [[65536, 128], [1024, 8], [1, 128]]),
                PP[i][:], FWHT_SCALE, None, ALU.mult),
               reads=[b_pp[i]], writes=[stg])
            if cpos + 4 < 64:
                fold_load2(cpos + 4)
            if cpos % FG == FG - 1:
                grp = cpos // FG
                rd = [b_fold2[jj + 8 * par] for jj in range(FG)]
                for eo in range(KC):
                    op("sync",
                       lambda h, eo=eo, grp=grp, base=base: h.dma_start(
                           out=w2fin[eo][:, grp * FG * 128:grp * FG * 128 + FG * 128],
                           in_=arena[:, base + eo * 1024:base + (eo + 1) * 1024]),
                       reads=rd, writes=[b_w2fin], dma=True)

    def ln_rows_finish(eps, n_feat):
        def p_r(v):
            v.tensor_scalar(fA[:], PP[2][:], 1.0 / n_feat, None, ALU.mult)
            v.tensor_scalar(fB[:], PP[3][:], 1.0 / n_feat, None, ALU.mult)
            v.tensor_tensor(sc1[:], fA[:], fA[:], ALU.mult)
            v.tensor_tensor(fB[:], fB[:], sc1[:], ALU.subtract)
            v.tensor_scalar(fB[:], fB[:], eps, None, ALU.add)
            return v.reciprocal(fB[:], fB[:])
        op("vector", p_r, reads=[b_pp[2], b_pp[3]], writes=[b_fA, b_fB, b_sc1])
        op("scalar", lambda h: h.activation(sc1[:], fB[:], AF.Sqrt),
           reads=[b_fB], writes=[b_sc1])

        def p_b(v):
            v.tensor_tensor(fB[:], fA[:], sc1[:], ALU.mult)
            v.tensor_scalar(fB[:], fB[:], -1.0, None, ALU.mult)
            return v.tensor_copy(fA[:], sc1[:])
        op("vector", p_b, reads=[b_fA, b_sc1], writes=[b_fA, b_fB])

    def ln_apply(xc_off):
        for k in range(KC):
            slot = sc1[:] if k % 2 == 0 else xr[:, 0:T]
            bslot = b_sc1 if k % 2 == 0 else b_xr[0]
            op("scalar",
               lambda h, k=k, slot=slot: h.activation(
                   slot, arena[:, xc_off + k * T:xc_off + (k + 1) * T], AF.Copy),
               reads=[b_xc], writes=[bslot])

            def p_a(v, k=k, slot=slot):
                v.tensor_tensor(slot, slot, fA[:], ALU.mult)
                return v.tensor_tensor(xn[:, k * T:(k + 1) * T], slot, fB[:], ALU.add)
            op("vector", p_a, reads=[bslot, b_fA, b_fB], writes=[bslot, b_xn])

    def ln_stats_chunk(k, n_ch, xc_off, sq_off):
        sq = b_sq[k % 2]
        op("scalar",
           lambda h, k=k: h.activation(
               arena[:, sq_off + (k % 2) * T:sq_off + (k % 2) * T + T],
               arena[:, xc_off + k * T:xc_off + (k + 1) * T], AF.Square),
           reads=[b_xc], writes=[sq])

        def p_m(t, k=k):
            t.matmul(PPl(2), ONES, arena[:, xc_off + k * T:xc_off + k * T + 512],
                     start=(k == 0), stop=(k == n_ch - 1))
            t.matmul(PPh(2), ONES, arena[:, xc_off + k * T + 512:xc_off + k * T + 1024],
                     start=(k == 0), stop=(k == n_ch - 1))
            t.matmul(PPl(3), ONES, arena[:, sq_off + (k % 2) * T:sq_off + (k % 2) * T + 512],
                     start=(k == 0), stop=(k == n_ch - 1))
            return t.matmul(PPh(3), ONES,
                            arena[:, sq_off + (k % 2) * T + 512:sq_off + (k % 2) * T + 1024],
                            start=(k == 0), stop=(k == n_ch - 1))
        op("tensor", p_m, reads=[b_xc, sq, b_small], writes=[b_pp[2], b_pp[3]])

    def ln_classic(src, src_buf, xc_off, sq_off):
        for k in range(KC):
            xrb = b_xr[k % 2]
            op("sync",
               lambda h, k=k: h.dma_start(out=xr[:, (k % 2) * T:(k % 2) * T + T], in_=src[k]),
               reads=[src_buf], writes=[xrb], dma=True)
            op("vector",
               lambda h, k=k: h.tensor_copy(
                   arena[:, xc_off + k * T:xc_off + (k + 1) * T],
                   xr[:, (k % 2) * T:(k % 2) * T + T]),
               reads=[xrb], writes=[b_xc])
            ln_stats_chunk(k, KC, xc_off, sq_off)
        ln_rows_finish(LN_EPS, E)
        ln_apply(xc_off)

    def projection(wsrc, n_eo, nkw, rhs_fn, rhs_bufs, evac_fn,
                   ring=None, ring_bufs=None, nslots=4, slot_elems=2048,
                   qeng="gpsimd", wsrc_buf=None, tile_elems=None):
        if ring is None:
            ring, ring_bufs = wr, b_wr
        if tile_elems is None:
            tile_elems = nkw * 128

        def emit_load(eo):
            sl = eo % nslots
            op(qeng,
               lambda h, eo=eo, sl=sl: h.dma_start(
                   out=ring[:, sl * slot_elems:sl * slot_elems + tile_elems], in_=wsrc[eo]),
               reads=([wsrc_buf] if wsrc_buf is not None else []),
               writes=[ring_bufs[sl]], dma=True)

        for eo in range(min(nslots, n_eo)):
            emit_load(eo)
        for eo in range(n_eo):
            sl = eo % nslots
            i = eo % 2

            def p_m(t, eo=eo, sl=sl, i=i):
                for k in range(nkw):
                    lhs = ring[:, sl * slot_elems + k * 128:sl * slot_elems + (k + 1) * 128]
                    t.matmul(PPl(i), lhs, rhs_fn(k, 0), start=(k == 0), stop=(k == nkw - 1))
                    mi = t.matmul(PPh(i), lhs, rhs_fn(k, 1), start=(k == 0), stop=(k == nkw - 1))
                return mi
            op("tensor", p_m, reads=[ring_bufs[sl]] + list(rhs_bufs), writes=[b_pp[i]])
            if eo + nslots < n_eo:
                emit_load(eo + nslots)
            evac_fn(eo, i)

    def xn_rhs(k, hh):
        return xn[:, k * T + hh * 512:k * T + hh * 512 + 512]

    def proj_residual(wsrc, xold, xold_buf, xnew, xnew_buf, fuse_ln,
                      rhs_fn=None, rhs_bufs=None, qeng="gpsimd", wsrc_buf=None,
                      ring=None, ring_bufs=None, nslots=4, slot_elems=2048,
                      nkw=KC, tile_elems=None):
        if rhs_fn is None:
            rhs_fn, rhs_bufs = xn_rhs, [b_xn]

        def ev(eo, i):
            xrb = b_xr[eo % 2]
            op("sync",
               lambda h, eo=eo: h.dma_start(out=xr[:, (eo % 2) * T:(eo % 2) * T + T],
                                            in_=xold[eo]),
               reads=[xold_buf], writes=[xrb], dma=True)
            op("vector",
               lambda h, eo=eo, i=i: h.tensor_tensor(
                   xr[:, (eo % 2) * T:(eo % 2) * T + T],
                   xr[:, (eo % 2) * T:(eo % 2) * T + T], PP[i][:], ALU.add),
               reads=[xrb, b_pp[i]], writes=[xrb])
            op("sync",
               lambda h, eo=eo: h.dma_start(out=xnew[eo],
                                            in_=xr[:, (eo % 2) * T:(eo % 2) * T + T]),
               reads=[xrb], writes=[xnew_buf], dma=True)
            if fuse_ln:
                op("vector",
                   lambda h, eo=eo: h.tensor_copy(
                       arena[:, AR_XC + eo * T:AR_XC + (eo + 1) * T],
                       xr[:, (eo % 2) * T:(eo % 2) * T + T]),
                   reads=[xrb], writes=[b_xc])
                ln_stats_chunk(eo, KC, AR_XC, AR_SQ1)
        projection(wsrc, KC, nkw, rhs_fn, rhs_bufs, ev, ring=ring, ring_bufs=ring_bufs,
                   nslots=nslots, slot_elems=slot_elems, qeng=qeng, wsrc_buf=wsrc_buf,
                   tile_elems=tile_elems)
        if fuse_ln:
            ln_rows_finish(LN_EPS, E)
            ln_apply(AR_XC)

    def attn_a():
        ln_classic(x_in, b_xin, AR_XC, AR_SQ1)

        def ev_sgq(eo, i):
            op("vector",
               lambda h, i=i: h.tensor_scalar(sc1[:], PP[i][:], 0.0, None, ALU.min),
               reads=[b_pp[i]], writes=[b_sc1])
            op("scalar", lambda h: h.activation(fA[:], sc1[:], AF.Exp),
               reads=[b_sc1], writes=[b_fA])
            ew = [b_sgq] + (b_fold + b_fold2 if eo == 0 else [])
            op("vector",
               lambda h, eo=eo, i=i: h.scalar_tensor_tensor(
                   arena[:, AR_SGQ + eo * T:AR_SGQ + (eo + 1) * T],
                   PP[i][:], 0.0, fA[:], ALU.max, ALU.add),
               reads=[b_pp[i], b_fA], writes=ew)
        projection(wq_t, KC, KC, xn_rhs, [b_xn], ev_sgq)

        def p_norm(t):
            for k in range(KC):
                t.matmul(PP[2][0:64, 0:512], mnblk_sb[:, k * 64:(k + 1) * 64],
                         arena[:, AR_SGQ + k * T:AR_SGQ + k * T + 512],
                         start=(k == 0), stop=(k == KC - 1))
                mi = t.matmul(PP[2][0:64, 512:1024], mnblk_sb[:, k * 64:(k + 1) * 64],
                              arena[:, AR_SGQ + k * T + 512:AR_SGQ + k * T + 1024],
                              start=(k == 0), stop=(k == KC - 1))
            return mi
        op("tensor", p_norm, reads=[b_sgq, b_mnblk], writes=[b_pp[2]])

        def p_coef(v):
            v.tensor_scalar(fB[0:64, :], PP[2][0:64, :], A_EPS, None, ALU.max)
            v.reciprocal(fB[0:64, :], fB[0:64, :])
            v.tensor_scalar(fB[0:64, :], fB[0:64, :], wact[0:64, 1:2], None, ALU.mult)
            return v.tensor_copy(arena[0:64, AR_CFB:AR_CFB + T], fB[0:64, :])
        op("vector", p_coef, reads=[b_pp[2], b_wact], writes=[b_fB, b_coefb])

        for h_ in range(H):
            pa = h_ % 2
            for n_ in range(NB):
                j = h_ * NB + n_
                pb = 2 + (j % 2)
                r = 4 * h_ + n_

                def p_bm(t, r=r, pb=pb):
                    t.matmul(PPl(pb), arena[0:64, AR_SEL + r * 128:AR_SEL + (r + 1) * 128],
                             arena[0:64, AR_CFB:AR_CFB + 512], start=True, stop=True)
                    return t.matmul(PPh(pb), arena[0:64, AR_SEL + r * 128:AR_SEL + (r + 1) * 128],
                                    arena[0:64, AR_CFB + 512:AR_CFB + 1024], start=True, stop=True)
                op("tensor", p_bm, reads=[b_sel, b_coefb], writes=[b_pp[pb]])
                op("vector",
                   lambda h, j=j, pb=pb: h.tensor_copy(
                       arena[:, AR_SGS + (j % 6) * T:AR_SGS + (j % 6) * T + T], PP[pb][:]),
                   reads=[b_pp[pb]], writes=[b_sgs[j % 6]])
                op("vector",
                   lambda h, j=j, h_=h_: h.tensor_tensor(
                       arena[:, AR_SGS + (j % 6) * T:AR_SGS + (j % 6) * T + T],
                       arena[:, AR_SGS + (j % 6) * T:AR_SGS + (j % 6) * T + T],
                       arena[:, AR_SGQ + h_ * T:AR_SGQ + (h_ + 1) * T],
                       ALU.mult),
                   reads=[b_sgq, b_sgs[j % 6]], writes=[b_sgs[j % 6]])

                def p_am(t, j=j, n_=n_, h_=h_, pa=pa):
                    ms = arena[:, AR_MEM + (n_ * H + h_) * 128:AR_MEM + (n_ * H + h_ + 1) * 128]
                    t.matmul(PPl(pa), ms,
                             arena[:, AR_SGS + (j % 6) * T:AR_SGS + (j % 6) * T + 512],
                             start=(n_ == 0), stop=(n_ == NB - 1))
                    return t.matmul(PPh(pa), ms,
                                    arena[:, AR_SGS + (j % 6) * T + 512:AR_SGS + (j % 6) * T + 1024],
                                    start=(n_ == 0), stop=(n_ == NB - 1))
                op("tensor", p_am, reads=[b_mem, b_sgs[j % 6]], writes=[b_pp[pa]])
            op("vector",
               lambda h, h_=h_, pa=pa: h.tensor_copy(xn[:, h_ * T:(h_ + 1) * T], PP[pa][:]),
               reads=[b_pp[pa]], writes=[b_xn])

        proj_residual(wo_t, x_in, b_xin, x_hbm[0], b_x0, fuse_ln=True)

    def ffn(xold, xold_buf, xnew, xnew_buf, final, arena_claim):
        claim(arena_claim + b_hid)
        for hc in range(FHC):
            sl1 = (2 * hc) % 4
            sl3 = (2 * hc + 1) % 4
            op("gpsimd",
               lambda h, hc=hc, sl1=sl1: h.dma_start(
                   out=wr[:, sl1 * 2048:sl1 * 2048 + 2048], in_=w1_t[hc]),
               writes=[b_wr[sl1]], dma=True)
            op("gpsimd",
               lambda h, hc=hc, sl3=sl3: h.dma_start(
                   out=wr[:, sl3 * 2048:sl3 * 2048 + 2048], in_=w3_t[hc]),
               writes=[b_wr[sl3]], dma=True)
            pg = hc % 2
            pl = 2 + hc % 2

            def p_g(t, sl1=sl1, pg=pg):
                for k in range(KC):
                    lhs = wr[:, sl1 * 2048 + k * 128:sl1 * 2048 + (k + 1) * 128]
                    t.matmul(PPl(pg), lhs, xn_rhs(k, 0), start=(k == 0), stop=(k == KC - 1))
                    mi = t.matmul(PPh(pg), lhs, xn_rhs(k, 1), start=(k == 0), stop=(k == KC - 1))
                return mi
            op("tensor", p_g, reads=[b_wr[sl1], b_xn], writes=[b_pp[pg]])

            def p_l(t, sl3=sl3, pl=pl):
                for k in range(KC):
                    lhs = wr[:, sl3 * 2048 + k * 128:sl3 * 2048 + (k + 1) * 128]
                    t.matmul(PPl(pl), lhs, xn_rhs(k, 0), start=(k == 0), stop=(k == KC - 1))
                    mi = t.matmul(PPh(pl), lhs, xn_rhs(k, 1), start=(k == 0), stop=(k == KC - 1))
                return mi
            op("tensor", p_l, reads=[b_wr[sl3], b_xn], writes=[b_pp[pl]])
            op("scalar",
               lambda h, pg=pg: h.activation(sc1[:], PP[pg][:], AF.Silu),
               reads=[b_pp[pg]], writes=[b_sc1])
            op("vector",
               lambda h, hc=hc, pl=pl: h.tensor_tensor(
                   arena[:, AR_HID + hc * T:AR_HID + (hc + 1) * T], sc1[:], PP[pl][:], ALU.mult),
               reads=[b_sc1, b_pp[pl]], writes=[b_hid[hc]])

        claim([b_xn, b_w2slot[0], b_w2slot[1]])

        def hid_rhs(k, hh):
            return arena[:, AR_HID + k * T + hh * 512:AR_HID + k * T + hh * 512 + 512]
        proj_residual(w2fin, xold, xold_buf, xnew, xnew_buf, fuse_ln=False,
                      rhs_fn=hid_rhs, rhs_bufs=b_hid, qeng="sync", wsrc_buf=b_w2fin,
                      ring=xn, ring_bufs=b_w2slot, nslots=2, slot_elems=8192,
                      nkw=FHC, tile_elems=FHC * 128)

    def rms_rows_finish(n_feat):
        def p_r(v):
            v.tensor_scalar(fA[:], PP[2][:], 1.0 / n_feat, None, ALU.mult)
            v.tensor_scalar(fA[:], fA[:], RMS_EPS, None, ALU.add)
            return v.reciprocal(fA[:], fA[:])
        op("vector", p_r, reads=[b_pp[2]], writes=[b_fA])
        op("scalar", lambda h: h.activation(fA[:], fA[:], AF.Sqrt),
           reads=[b_fA], writes=[b_fA])

    def mk_cproj_evac(raw_off, raw_buf, n_ch):
        def ev(eo, i):
            op("vector",
               lambda h, eo=eo, i=i: h.tensor_copy(
                   arena[:, raw_off + eo * T:raw_off + (eo + 1) * T], PP[i][:]),
               reads=[b_pp[i]], writes=[raw_buf])
            op("scalar",
               lambda h, eo=eo, i=i: h.activation(
                   arena[:, AB_SCR + (eo % 2) * T:AB_SCR + (eo % 2) * T + T],
                   PP[i][:], AF.Square),
               reads=[b_pp[i]], writes=[b_sq[eo % 2]])

            def p_m(t, eo=eo):
                t.matmul(PPl(2), ONES,
                         arena[:, AB_SCR + (eo % 2) * T:AB_SCR + (eo % 2) * T + 512],
                         start=(eo == 0), stop=(eo == n_ch - 1))
                return t.matmul(PPh(2), ONES,
                                arena[:, AB_SCR + (eo % 2) * T + 512:AB_SCR + (eo % 2) * T + 1024],
                                start=(eo == 0), stop=(eo == n_ch - 1))
            op("tensor", p_m, reads=[b_sq[eo % 2], b_small], writes=[b_pp[2]])
        return ev

    def rms_apply(n_ch, raw_off, raw_buf, out_ap_fn, out_buf, g_col):
        for cc in range(n_ch):
            op("scalar",
               lambda h, cc=cc: h.activation(
                   sc1[:], arena[:, raw_off + cc * T:raw_off + (cc + 1) * T], AF.Copy),
               reads=[raw_buf], writes=[b_sc1])
            op("vector",
               lambda h: h.tensor_tensor(sc1[:], sc1[:], fA[:], ALU.mult),
               reads=[b_sc1, b_fA], writes=[b_sc1])
            op("scalar",
               lambda h, cc=cc: h.activation(out_ap_fn(cc), sc1[:], AF.Copy, scale=g_col(cc)),
               reads=[b_sc1, b_small], writes=[out_buf])

    def mk_rope_evac(dst_fn, dst_buf, extra_first=()):
        def ev(eo, i):
            op("scalar",
               lambda h, i=i: h.activation(arena[:, AB_SCR:AB_SCR + T], PP[i][:], AF.Copy),
               reads=[b_pp[i]], writes=[b_sq[0]])
            ew = [b_sq[0], b_sq[1], dst_buf] + (list(extra_first) if eo == 0 else [])

            def p_v(v, eo=eo):
                mask = []
                for q in range(0, 32, 2):
                    mask += [q + 1, q]
                v.stream_shuffle(arena[:, AB_SCR + T:AB_SCR + 2 * T],
                                 arena[:, AB_SCR:AB_SCR + T], mask)
                v.tensor_tensor(arena[:, AB_SCR + T:AB_SCR + 2 * T],
                                arena[:, AB_SCR + T:AB_SCR + 2 * T], SINB, ALU.mult)
                v.tensor_tensor(arena[:, AB_SCR:AB_SCR + T],
                                arena[:, AB_SCR:AB_SCR + T], COSA, ALU.mult)
                return v.tensor_tensor(dst_fn(eo), arena[:, AB_SCR:AB_SCR + T],
                                       arena[:, AB_SCR + T:AB_SCR + 2 * T], ALU.add)
            op("vector", p_v, reads=[b_sq[0], b_small], writes=ew)
        return ev

    def attn_b():
        claim([b_xc, b_sq[0], b_sq[1]] + b_hid + [b_w2slot[0], b_w2slot[1]])
        ln_classic(x_hbm[1], b_x1, AR_XC, SQ3)

        claim([AB["cqr"], b_sq[0], b_sq[1]])
        projection(wdq_t, CCH, KC, xn_rhs, [b_xn], mk_cproj_evac(AB_CQR, AB["cqr"], CCH))
        rms_rows_finish(C)
        claim([AB["cqn"]] + b_hid[:4])
        rms_apply(CCH, AB_CQR, AB["cqr"],
                  lambda cc: arena[:, AB_CQN + cc * T:AB_CQN + (cc + 1) * T], AB["cqn"], G1c)

        def cqn_rhs(k, hh):
            return arena[:, AB_CQN + k * T + hh * 512:AB_CQN + k * T + hh * 512 + 512]
        claim([AB["qr"]] + b_hid[40:56])
        projection(wqr_t, KC, CCH, cqn_rhs, [AB["cqn"]],
                   mk_rope_evac(lambda eo: arena[:, AB_QR + eo * T:AB_QR + (eo + 1) * T],
                                AB["qr"]),
                   ring=wr2, ring_bufs=b_wr2, nslots=2, slot_elems=512, tile_elems=CCH * 128)

        AB_CKVR = 4096
        claim(b_hid[32:40])
        claim([AB["ckvr"], b_xc] + b_hid[4:8])
        projection(wdkv_t, CCH, KC, xn_rhs, [b_xn], mk_cproj_evac(AB_CKVR, AB["ckvr"], CCH))
        rms_rows_finish(C)
        claim([AB["ckv"]])
        rms_apply(CCH, AB_CKVR, AB["ckvr"],
                  lambda cc: arena[:, AB_CKV + cc * 2048:AB_CKV + cc * 2048 + T], AB["ckv"], G2c)

        claim([AB["kr"], AB["cqn"], AB["ckvr"]] + b_hid[:32])
        projection(wkr_t, KC, KC, xn_rhs, [b_xn],
                   mk_rope_evac(lambda eo: arena[:, AB_KR + eo * 2048:AB_KR + eo * 2048 + T],
                                AB["kr"]))

        claim([b_xn, AB["t1b0"], AB["t1b1"], AB["wuqr"]])
        for eo_a in range(CCH):
            op("gpsimd",
               lambda h, eo_a=eo_a: h.dma_start(
                   out=wr[:, eo_a * 2048:eo_a * 2048 + 2048], in_=wuk_n[eo_a]),
               writes=[b_wr[eo_a]], dma=True)
        for k2 in range(KC):
            op("gpsimd",
               lambda h, k2=k2: h.dma_start(
                   out=xn[:, XN_WUQ + k2 * 512:XN_WUQ + (k2 + 1) * 512], in_=wuq_t[k2]),
               writes=[AB["wuqr"]], dma=True)

        ck_src = bass.AP(arena, AB_CKV, [[65536, 128], [2048, CCH], [1, T]])
        ck_dst = bass.AP(ag_in.tensor, 0, [[T, 128], [128 * T, CCH], [1, T]])
        op("sync", lambda h: h.dma_start(out=ck_dst, in_=ck_src),
           reads=[AB["ckv"]], writes=[AB["agin"]], dma=True)
        kr_src = bass.AP(arena, AB_KR, [[65536, 128], [2048, KC], [1, T]])
        kr_dst = bass.AP(ag_in.tensor, CCH * 128 * T, [[T, 128], [128 * T, KC], [1, T]])
        op("sync", lambda h: h.dma_start(out=kr_dst, in_=kr_src),
           reads=[AB["kr"]], writes=[AB["agin"]], dma=True)
        op("sync", lambda h: h.dma_start(out=fence_in[:], in_=consts_in[0:1, :]),
           writes=[AB["fin"]], dma=True)
        op("gpsimd",
           lambda h: h.collective_compute(
               "AllGather", ALU.bypass,
               replica_groups=[[0, 1], [2, 3], [4, 5], [6, 7]],
               ins=[ag_in[:]], outs=[ag_out[:]]),
           reads=[AB["agin"]], writes=[AB["agout"]])
        op("gpsimd",
           lambda h: h.collective_compute(
               "AllReduce", ALU.add,
               replica_groups=[[0, 1], [2, 3], [4, 5], [6, 7]],
               ins=[fence_in[:]], outs=[fence_out[:]]),
           reads=[AB["fin"], AB["agout"]], writes=[AB["fence"]])
        for _ in range(2):
            op("gpsimd",
               lambda h: h.dma_start(out=mrow[:, 0:4],
                                     in_=bass.AP(ag_out.tensor, 0, [[4, 128], [1, 4]])),
               reads=[AB["fence"], AB["agout"]], writes=[b_mrow], dma=True)

        def cqr_rhs(k, hh):
            return arena[:, AB_CQR + k * T + hh * 512:AB_CQR + k * T + hh * 512 + 512]
        claim([AB["aq"], b_sq[0], b_sq[1]])
        for eo_a in range(CCH):
            pa = 2 + eo_a % 2
            i_a = eo_a % 2
            for k2 in range(KC):
                def p_t1(t, k2=k2, pa=pa):
                    for k in range(CCH):
                        lhs = xn[:, XN_WUQ + k2 * 512 + k * 128:XN_WUQ + k2 * 512 + (k + 1) * 128]
                        t.matmul(PPl(pa), lhs, cqr_rhs(k, 0), start=(k == 0), stop=(k == CCH - 1))
                        mi = t.matmul(PPh(pa), lhs, cqr_rhs(k, 1), start=(k == 0), stop=(k == CCH - 1))
                    return mi
                op("tensor", p_t1, reads=[AB["wuqr"], AB["cqr"]], writes=[b_pp[pa]])
                tb = AB[f"t1b{k2 % 2}"]
                op("vector",
                   lambda h, k2=k2, pa=pa: h.tensor_copy(
                       xn[:, XN_T1B + (k2 % 2) * T:XN_T1B + (k2 % 2) * T + T], PP[pa][:]),
                   reads=[b_pp[pa]], writes=[tb])

                def p_aq(t, k2=k2, eo_a=eo_a, i_a=i_a):
                    lhs = wr[:, eo_a * 2048 + k2 * 128:eo_a * 2048 + (k2 + 1) * 128]
                    t.matmul(PPl(i_a), lhs,
                             xn[:, XN_T1B + (k2 % 2) * T:XN_T1B + (k2 % 2) * T + 512],
                             start=(k2 == 0), stop=(k2 == KC - 1))
                    return t.matmul(PPh(i_a), lhs,
                                    xn[:, XN_T1B + (k2 % 2) * T + 512:XN_T1B + (k2 % 2) * T + 1024],
                                    start=(k2 == 0), stop=(k2 == KC - 1))
                op("tensor", p_aq, reads=[b_wr[eo_a], tb], writes=[b_pp[i_a]])
            op("vector",
               lambda h, eo_a=eo_a, i_a=i_a: h.tensor_copy(
                   arena[:, AB_AQ + eo_a * T:AB_AQ + (eo_a + 1) * T], PP[i_a][:]),
               reads=[b_pp[i_a]], writes=[AB["aq"]])

        claim([b_xn, AB["t1b0"], AB["t1b1"], AB["wuqr"], AB["wuv"]])
        for k in range(KC):
            op("gpsimd",
               lambda h, k=k: h.dma_start(out=xn[:, XN_WUV + k * 512:XN_WUV + (k + 1) * 512],
                                          in_=wuv_n[k]),
               reads=[], writes=[AB["wuv"]], dma=True)
        claim([AB["awo"], AB["stg"]])
        for eo in range(KC):
            sl = eo % 4
            op("gpsimd",
               lambda h, eo=eo, sl=sl: h.dma_start(
                   out=wr[:, sl * 2048:sl * 2048 + 2048], in_=wot_b[eo]),
               writes=[b_wr[sl]], dma=True)
            i = eo % 2

            def p_aw(t, eo=eo, sl=sl, i=i):
                for k in range(KC):
                    mi = t.matmul(PPl(i), wr[:, sl * 2048 + k * 128:sl * 2048 + (k + 1) * 128],
                                  xn[:, XN_WUV + k * 512:XN_WUV + (k + 1) * 512],
                                  start=(k == 0), stop=(k == KC - 1))
                return mi
            op("tensor", p_aw, reads=[b_wr[sl], AB["wuv"]], writes=[b_pp[i]])
            op("vector",
               lambda h, i=i, eo=eo: h.tensor_copy(
                   arena[:, AB_ATT + (eo % 2) * 512:AB_ATT + (eo % 2) * 512 + 512], PPl(i)),
               reads=[b_pp[i]], writes=[AB["stg"]])

            def p_tr(t, eo=eo, i=i):
                for cc in range(CCH):
                    mi = t.matmul(PPh(i)[:, 0:512][:, cc * 128:(cc + 1) * 128] if False else
                                  PP[i][:, 512 + cc * 128:512 + (cc + 1) * 128],
                                  arena[:, AB_ATT + (eo % 2) * 512 + cc * 128:AB_ATT + (eo % 2) * 512 + (cc + 1) * 128],
                                  IDENT, start=True, stop=True)
                return mi
            op("tensor", p_tr, reads=[AB["stg"], b_small], writes=[b_pp[i]])
            dst = bass.AP(xn, XN_AWO + eo * 128, [[KC * T, 128], [2048, CCH], [1, 128]])
            op("vector",
               lambda h, i=i, dst=dst: h.tensor_copy(dst, PP[i][:, 512:1024]),
               reads=[b_pp[i]], writes=[AB["awo"]])

        for r in range(2):
            for ec in range(KC):
                src = bass.AP(ag_out.tensor, r * AGN + CCH * 128 * T + ec * 128 * T,
                              [[T, 128], [1, T]])
                op("sync",
                   lambda h, r=r, ec=ec, src=src: h.dma_start(
                       out=arena[:, AB_KR + ec * 2048 + r * T:AB_KR + ec * 2048 + (r + 1) * T],
                       in_=src),
                   reads=[AB["agout"], AB["fence"], b_mrow], writes=[AB["kr"]], dma=True)
            for cc in range(CCH):
                src = bass.AP(ag_out.tensor, r * AGN + cc * 128 * T, [[T, 128], [1, T]])
                op("sync",
                   lambda h, r=r, cc=cc, src=src: h.dma_start(
                       out=arena[:, AB_CKV + cc * 2048 + r * T:AB_CKV + cc * 2048 + (r + 1) * T],
                       in_=src),
                   reads=[AB["agout"], AB["fence"], b_mrow], writes=[AB["ckv"]], dma=True)

        claim(b_wr + [AB["ckvt"]])
        for tk in range(KC):
            def p_ct(t, tk=tk):
                for cc in range(CCH):
                    mi = t.matmul(PP[2][:, cc * 128:(cc + 1) * 128],
                                  arena[:, AB_CKV + cc * 2048 + tk * 128:AB_CKV + cc * 2048 + (tk + 1) * 128],
                                  IDENT, start=True, stop=True)
                return mi
            op("tensor", p_ct, reads=[AB["ckv"], b_small], writes=[b_pp[2]])
            op("vector",
               lambda h, tk=tk: h.tensor_copy(wr[:, tk * 512:(tk + 1) * 512], PP[2][:, 0:512]),
               reads=[b_pp[2]], writes=[AB["ckvt"]])

        claim([AB["att"], AB["cqr"]])
        claim([AB["attT"], AB["of"], AB["wuv"], AB["stg"]])
        for q2 in range(8):
            for half in range(2):
                def p_sc(t, q2=q2, half=half):
                    for kq in range(2):
                        tgt = PP[half][:, kq * 512:(kq + 1) * 512]
                        first = True
                        for cc in range(CCH):
                            t.matmul(tgt, arena[0:128, AB_AQ + cc * T + q2 * 128:AB_AQ + cc * T + (q2 + 1) * 128],
                                     arena[:, AB_CKV + cc * 2048 + half * T + kq * 512:AB_CKV + cc * 2048 + half * T + (kq + 1) * 512],
                                     start=first, stop=False)
                            first = False
                        for ec in range(KC):
                            mi = t.matmul(tgt, arena[:, AB_QR + ec * T + q2 * 128:AB_QR + ec * T + (q2 + 1) * 128],
                                          arena[:, AB_KR + ec * 2048 + half * T + kq * 512:AB_KR + ec * 2048 + half * T + (kq + 1) * 512],
                                          start=False, stop=(ec == KC - 1))
                    return mi
                op("tensor", p_sc, reads=[AB["aq"], AB["ckv"], AB["qr"], AB["kr"]],
                   writes=[b_pp[half]])

            def p_mx(v):
                v.reduce_max(mrow[:, 0:1], PP[0][:], AX.X)
                v.reduce_max(mrow[:, 1:2], PP[1][:], AX.X)
                v.tensor_copy(mrow[:, 6:7], bw_sb[:, 0:1])
                v.tensor_copy(mrow[:, 6:7], bw_sb[:, 0:1])
                v.tensor_tensor(mrow[:, 0:1], mrow[:, 0:1], mrow[:, 1:2], ALU.max)
                v.tensor_copy(mrow[:, 6:7], bw_sb[:, 0:1])
                v.tensor_copy(mrow[:, 6:7], bw_sb[:, 0:1])
                return v.tensor_scalar(mrow[:, 1:2], mrow[:, 0:1], -SM_SCALE, None, ALU.mult)
            op("vector", p_mx, reads=[b_pp[0], b_pp[1]], writes=[b_mrow])
            op("scalar",
               lambda h: h.activation(arena[:, AB_ATT:AB_ATT + T], PP[0][:], AF.Exp,
                                      bias=mrow[:, 1:2], scale=SM_SCALE),
               reads=[b_pp[0], b_mrow], writes=[AB["att"]])
            op("scalar",
               lambda h: h.activation(arena[:, AB_ATT + T:AB_ATT + 2 * T], PP[1][:], AF.Exp,
                                      bias=mrow[:, 1:2], scale=SM_SCALE),
               reads=[b_pp[1], b_mrow], writes=[AB["att"]])

            def p_sum(v):
                v.reduce_sum(mrow[:, 2:3], arena[:, AB_ATT:AB_ATT + 2 * T], AX.X)
                v.tensor_copy(mrow[:, 6:7], bw_sb[:, 0:1])
                v.tensor_copy(mrow[:, 6:7], bw_sb[:, 0:1])
                v.reciprocal(mrow[:, 3:4], mrow[:, 2:3])
                v.tensor_copy(mrow[:, 6:7], bw_sb[:, 0:1])
                v.tensor_copy(mrow[:, 6:7], bw_sb[:, 0:1])
                return v.tensor_scalar(arena[:, AB_ATT:AB_ATT + 2 * T],
                                       arena[:, AB_ATT:AB_ATT + 2 * T],
                                       mrow[:, 3:4], None, ALU.mult)
            op("vector", p_sum, reads=[AB["att"]], writes=[AB["att"], b_mrow])

            for grp in range(4):
                pt = 2 + grp % 2

                def p_at(t, q2=q2, grp=grp, pt=pt):
                    for j in range(4):
                        tk = grp * 4 + j
                        mi = t.matmul(PP[pt][:, (grp % 2) * 0 + j * 128:j * 128 + 128],
                                      arena[:, AB_ATT + tk * 128:AB_ATT + (tk + 1) * 128],
                                      IDENT, start=True, stop=True)
                    return mi
                op("tensor", p_at, reads=[AB["att"], b_small], writes=[b_pp[pt]])
                op("vector",
                   lambda h, grp=grp, pt=pt: h.tensor_copy(
                       xn[:, XN_ATT + grp * 512:XN_ATT + (grp + 1) * 512], PP[pt][:, 0:512]),
                   reads=[b_pp[pt]], writes=[AB["attT"]])

            def p_o(t, q2=q2):
                for cc in range(CCH):
                    for tk in range(KC):
                        mi = t.matmul(PP[2][:, 512 + cc * 128:512 + (cc + 1) * 128],
                                      wr[:, tk * 512 + cc * 128:tk * 512 + (cc + 1) * 128],
                                      xn[:, XN_ATT + tk * 128:XN_ATT + (tk + 1) * 128],
                                      start=(tk == 0), stop=(tk == KC - 1))
                return mi
            op("tensor", p_o, reads=[AB["ckvt"], AB["attT"]], writes=[b_pp[2]])
            dst = bass.AP(xn, XN_OF + q2 * 128, [[KC * T, 128], [T, CCH], [1, 128]])
            op("vector", lambda h, dst=dst: h.tensor_copy(dst, PP[2][:, 512:1024]),
               reads=[b_pp[2]], writes=[AB["of"]])

        claim([AB["ub"], AB["qr"], b_sq[0], b_sq[1]])

        def p_u(eo, i):
            op("vector",
               lambda h, eo=eo, i=i: h.tensor_copy(
                   arena[:, AB_UB + eo * T:AB_UB + (eo + 1) * T], PP[i][:]),
               reads=[b_pp[i]], writes=[AB["ub"]])
            op("scalar",
               lambda h, eo=eo, i=i: h.activation(
                   arena[:, AB_SCR + (eo % 2) * T:AB_SCR + (eo % 2) * T + T],
                   PP[i][:], AF.Square),
               reads=[b_pp[i]], writes=[b_sq[eo % 2]])

            def p_m(t, eo=eo):
                t.matmul(PPl(2), ONES,
                         arena[:, AB_SCR + (eo % 2) * T:AB_SCR + (eo % 2) * T + 512],
                         start=(eo == 0), stop=(eo == KC - 1))
                return t.matmul(PPh(2), ONES,
                                arena[:, AB_SCR + (eo % 2) * T + 512:AB_SCR + (eo % 2) * T + 1024],
                                start=(eo == 0), stop=(eo == KC - 1))
            op("tensor", p_m, reads=[b_sq[eo % 2], b_small], writes=[b_pp[2]])

        for eo in range(KC):
            i = eo % 2

            def p_up(t, eo=eo, i=i):
                for cc in range(CCH):
                    lhs = xn[:, XN_AWO + cc * 2048 + eo * 128:XN_AWO + cc * 2048 + (eo + 1) * 128]
                    t.matmul(PPl(i), lhs, xn[:, XN_OF + cc * T:XN_OF + cc * T + 512],
                             start=(cc == 0), stop=(cc == CCH - 1))
                    mi = t.matmul(PPh(i), lhs, xn[:, XN_OF + cc * T + 512:XN_OF + cc * T + 1024],
                                  start=(cc == 0), stop=(cc == CCH - 1))
                return mi
            op("tensor", p_up, reads=[AB["awo"], AB["of"]], writes=[b_pp[i]])
            p_u(eo, i)
        rms_rows_finish(E)

        claim([b_xc] + b_hid[:16])
        for k in range(KC):
            xrb = b_xr[k % 2]
            op("sync",
               lambda h, k=k: h.dma_start(out=xr[:, (k % 2) * T:(k % 2) * T + T],
                                          in_=x_hbm[1][k]),
               reads=[b_x1], writes=[xrb], dma=True)
            op("scalar",
               lambda h, k=k: h.activation(sc1[:], arena[:, AB_UB + k * T:AB_UB + (k + 1) * T],
                                           AF.Copy, scale=G3c(k)),
               reads=[AB["ub"], b_small], writes=[b_sc1])

            def p_r(v, k=k):
                v.tensor_tensor(sc1[:], sc1[:], fA[:], ALU.mult)
                return v.tensor_tensor(xr[:, (k % 2) * T:(k % 2) * T + T],
                                       xr[:, (k % 2) * T:(k % 2) * T + T], sc1[:], ALU.add)
            op("vector", p_r, reads=[b_sc1, b_fA, xrb], writes=[b_sc1, xrb])
            op("sync",
               lambda h, k=k: h.dma_start(out=x_hbm[0][k],
                                          in_=xr[:, (k % 2) * T:(k % 2) * T + T]),
               reads=[xrb], writes=[b_x0], dma=True)
            op("vector",
               lambda h, k=k: h.tensor_copy(arena[:, AR_XC + k * T:AR_XC + (k + 1) * T],
                                            xr[:, (k % 2) * T:(k % 2) * T + T]),
               reads=[xrb], writes=[b_xc])
            ln_stats_chunk(k, KC, AR_XC, AB_SCR)
        ln_rows_finish(LN_EPS, E)
        claim([b_xn, AB["attT"], AB["of"], AB["wuv"], AB["awo"], AB["t1b0"], AB["t1b1"],
               b_w2slot[0], b_w2slot[1]])
        ln_apply(AR_XC)

    if upto >= 2:
        w2_fold()
    if upto >= 1:
        attn_a()
    if upto >= 2:
        ffn(x_hbm[0], b_x0, x_hbm[1], b_x1, final=False,
            arena_claim=[b_xc, b_sgq, b_mem, b_sel, b_coefb] + b_sgs + b_bst + b_sq)
    if upto >= 3:
        attn_b()
    if upto >= 4:
        ffn(x_hbm[0], b_x0, y_out, b_y, final=True,
            arena_claim=[b_xc] + [AB[k] for k in ("kr", "ckv", "qr", "aq", "cqr",
                                                  "cqn", "att", "ub", "ckvr", "ckvt")]
                        + b_sq + b_wr)

    def finalize(src, src_buf):
        for k in range(KC):
            xrb = b_xr[k % 2]
            op("sync", lambda h, k=k: h.dma_start(
                out=xr[:, (k % 2) * T:(k % 2) * T + T], in_=src[k]),
               reads=[src_buf], writes=[xrb], dma=True)
            op("sync", lambda h, k=k: h.dma_start(
                out=y_out[k], in_=xr[:, (k % 2) * T:(k % 2) * T + T]),
               reads=[xrb], writes=[b_y], dma=True)

    if upto == 1:
        finalize(x_hbm[0], b_x0)
    elif upto == 2:
        finalize(x_hbm[1], b_x1)
    elif upto == 3:
        finalize(x_hbm[0], b_x0)

    if DBG is not None:
        allb = [b_xn, b_sgq, b_xc, b_coefb, b_sel, b_mem, b_wact] + b_sgs + b_bst + b_hid[:1] + [AB["kr"], AB["ckv"], AB["qr"], AB["aq"], AB["cqr"], AB["of"], AB["attT"], AB["awo"], AB["ub"], AB["ckvt"], AB["ckvr"]]
        if isinstance(DBG, tuple):
            dl = [(DBG[0], DBG[1] + kk * T) for kk in range(KC)]
        else:
            dl = DBG
        for kk, (tn, off) in enumerate(dl):
            srcten = arena if tn == "arena" else xn
            lim = 65536 if tn == "arena" else KC * T
            o0 = min(off, lim - T)
            op("gpsimd", lambda h, kk=kk, o0=o0, srcten=srcten: h.dma_start(
                out=y_out[kk], in_=srcten[:, o0:o0 + T]),
               reads=allb, writes=[b_y], dma=True)

    op("vector", lambda h: h.tensor_copy(mrow[:, 6:7], mrow[:, 5:6]),
       reads=[b_y], writes=[b_mrow])

    blkg.__exit__(None, None, None)
    for g in reversed(es):
        g.__exit__(None, None, None)
    return nc


def _tile_weight_T(w):
    win, wout = w.shape[1], w.shape[0]
    nk, neo = win // 128, wout // 128
    wt = np.ascontiguousarray(w.T)
    t = wt.reshape(nk, 128, neo, 128)
    return np.ascontiguousarray(t.transpose(2, 1, 0, 3).reshape(neo, 128, nk * 128))


def _tile_weight_N(w):
    win, wout = w.shape
    nk, neo = win // 128, wout // 128
    t = w.reshape(nk, 128, neo, 128)
    return np.ascontiguousarray(t.transpose(2, 1, 0, 3).reshape(neo, 128, nk * 128))


def host_prepare(inputs):
    f32 = np.float32
    d = {}
    d["wqT"] = _tile_weight_T(np.asarray(inputs["L0_attn_a_Wq"], f32))
    d["woT"] = _tile_weight_T(np.asarray(inputs["L0_attn_a_Wo"], f32))
    mem = np.asarray(inputs["L0_attn_a_memories"], f32)
    d["mem"] = np.ascontiguousarray(
        mem.reshape(NB * H, 128, 128).transpose(1, 0, 2).reshape(128, NB * H * 128))
    mn = np.asarray(inputs["L0_attn_a_memory_norms"], f32)
    mb = np.zeros((KC, 128, 64), f32)
    for h in range(H):
        for n in range(NB):
            mb[h, :, 4 * h + n] = mn[n, h, :]
    d["mnblk"] = np.ascontiguousarray(mb.transpose(1, 0, 2).reshape(128, KC * 64))
    mn_n = np.zeros((128, H * D), f32)
    mn_n[0:NB] = mn.reshape(NB, H * D)
    d["mnorm_n"] = mn_n
    bw = np.zeros((128, NB), f32)
    bw[0:H] = np.asarray(inputs["L0_attn_a_bank_weights"], f32)
    d["bw"] = bw
    d["w1T"] = _tile_weight_T(np.asarray(inputs["L0_ffn_W1"], f32))
    d["w3T"] = _tile_weight_T(np.asarray(inputs["L0_ffn_W3"], f32))
    w2t = np.ascontiguousarray(np.asarray(inputs["L0_ffn_W2"], f32).T)
    d["w2p1"] = np.ascontiguousarray(
        w2t.reshape(64, 64, 2, E).transpose(1, 2, 0, 3).reshape(64, 128, E))
    d["wdqT"] = _tile_weight_T(np.asarray(inputs["L0_attn_b_w_d_q"], f32))
    d["wdkvT"] = _tile_weight_T(np.asarray(inputs["L0_attn_b_w_d_kv"], f32))
    d["wuqT"] = _tile_weight_T(np.asarray(inputs["L0_attn_b_w_u_q"], f32))
    d["wuk"] = _tile_weight_N(np.asarray(inputs["L0_attn_b_w_u_k"], f32))
    d["wkrT"] = _tile_weight_T(np.asarray(inputs["L0_attn_b_w_kr"], f32))
    d["wqrT"] = _tile_weight_T(np.asarray(inputs["L0_attn_b_w_qr"], f32))
    wuv = np.asarray(inputs["L0_attn_b_w_u_v"], f32)
    d["wuv"] = np.ascontiguousarray(wuv.reshape(KC, 128, C))
    d["woTb"] = _tile_weight_T(np.asarray(inputs["L0_attn_b_w_o"], f32))
    d["g1"] = np.ascontiguousarray(np.asarray(inputs["L0_attn_b_g1"], f32).reshape(CCH, 128).T)
    d["g2"] = np.ascontiguousarray(np.asarray(inputs["L0_attn_b_g2"], f32).reshape(CCH, 128).T)
    d["g3"] = np.ascontiguousarray(np.asarray(inputs["L0_attn_b_g3"], f32).reshape(KC, 128).T)

    consts = np.zeros((128, 512), BF)
    consts[:, 0:128] = np.eye(128, dtype=BF)
    consts[:, 128:256] = np.ones((128, 128), BF)
    consts[:, 256:384] = _had(128).astype(BF)
    consts[:, 384:512] = np.kron(np.eye(2), _had(64)).astype(BF)
    d["consts"] = consts
    selh = np.zeros((128, 64), BF)
    for m in range(64):
        selh[m // 4, m] = 1
    d["selh"] = selh
    seln = np.zeros((128, 64), BF)
    for m in range(64):
        seln[m % 4, m] = 1
    d["seln"] = seln
    mask = np.zeros((128, 4), np.float32)
    for m in range(64):
        mask[m, m % 4] = 1
    d["mask64"] = mask
    sel = np.zeros((128, 64 * 128), BF)
    for r in range(64):
        sel[r, r * 128:(r + 1) * 128] = 1
    d["sel"] = sel
    return d


def rope_tables(core_id):
    s0 = (core_id % 2) * T
    pos = s0 + np.arange(T, dtype=np.float64)
    freqs = 1.0 / (10000.0 ** (np.arange(0, D, 2, dtype=np.float64) / D))
    ang = pos[None, :] * freqs[:, None]
    rt = np.empty((128, 2 * T), dtype=BF)
    c = np.cos(ang)
    s = np.sin(ang)
    rt[0::2, 0:T] = c
    rt[1::2, 0:T] = c
    rt[0::2, T:2 * T] = -s
    rt[1::2, T:2 * T] = s
    return rt


def kernel(**inputs):
    shared = host_prepare(inputs)
    x = np.asarray(inputs["L0_attn_a_hidden_states"], np.float32)
    in_maps = []
    for c in range(NCORES):
        b, half = c // 2, c % 2
        xs = np.ascontiguousarray(x[b, half * T:(half + 1) * T, :].T)
        m = dict(shared)
        m["x"] = np.ascontiguousarray(xs.reshape(KC, 128, T))
        m["rope"] = rope_tables(c)
        in_maps.append(m)
    nc = build_program(upto=UPTO)
    res = run_bass_kernel_spmd(nc, in_maps, core_ids=list(range(NCORES)))
    out = np.empty_like(x)
    for c in range(NCORES):
        b, half = c // 2, c % 2
        y = res.results[c]["y"].reshape(E, T)
        out[b, half * T:(half + 1) * T, :] = y.T
    return out
